# version 03
# baseline (speedup 1.0000x reference)
import numpy as np
import ml_dtypes
from contextlib import ExitStack
import concourse.bass as bass
import concourse.mybir as mybir
from concourse.bass_utils import run_bass_kernel_spmd

F32 = mybir.dt.float32
BF16 = mybir.dt.bfloat16
I32 = mybir.dt.int32
AF = mybir.ActivationFunctionType
ALU = mybir.AluOpType
AXX = mybir.AxisListType.X

NCORE = 8
B, T, C = 32, 2048, 1024
S = B // NCORE
TOK = S * T
CT = C // 128
ENGINES = ("sync", "gpsimd", "vector", "scalar", "tensor")

BF_NP = ml_dtypes.bfloat16


class Sem:
    __slots__ = ("h", "n")
    def __init__(self, h):
        self.h = h
        self.n = 0
    def inc(self, k=1):
        self.n += k
        return self.n


def dmaw4(e, sem, pre, out4, in4, nseq):
    for si in range(nseq):
        if pre + si * 16:
            e.wait_ge(sem.h, pre + si * 16)
        e.dma_start(out=out4[:, :, si, :], in_=in4[:, :, si, :]).then_inc(sem.h, 16)


def dmaw(e, sem, pre, out, in_):
    if pre:
        e.wait_ge(sem.h, pre)
    return e.dma_start(out=out, in_=in_).then_inc(sem.h, 16)


class Prog:
    def __init__(self):
        self.nc = bass.Bass("TRN2", target_bir_lowering=False)
        self.ctx = ExitStack()
        self.cur = self.ctx
        self.q = {e: [] for e in ENGINES}
        self._si = 0
        self._ti = 0
        self.banks = None
        self.bar = None
        self.barn = 0

    def stage(self):
        from contextlib import contextmanager

        @contextmanager
        def _scope():
            sub = ExitStack()
            old = self.cur
            self.cur = sub
            try:
                yield
            finally:
                self.cur = old
                sub.close()
        return _scope()

    def barrier(self):
        if self.bar is None:
            self.bar = self.sem("gbar")
        tgt = self.bar.inc(5)

        @self.on("vector")
        def _(v, t=tgt):
            v.memset(self.dumv[:], 0.0).then_inc(self.bar.h, 1)
            v.wait_ge(self.bar.h, t)

        @self.on("scalar")
        def _(e, t=tgt):
            e.memzero(self.dums[:]).then_inc(self.bar.h, 1)
            e.wait_ge(self.bar.h, t)

        @self.on("gpsimd")
        def _(e, t=tgt):
            e.memset(self.dumg[:], 0.0).then_inc(self.bar.h, 1)
            e.wait_ge(self.bar.h, t)

        @self.on("tensor")
        def _(e, t=tgt):
            e.nop().then_inc(self.bar.h, 1)
            e.wait_ge(self.bar.h, t)

        @self.on("sync")
        def _(e, t=tgt):
            e.nop().then_inc(self.bar.h, 1)
            e.wait_ge(self.bar.h, t)

    def sem(self, name=None):
        h = self.ctx.enter_context(self.nc.semaphore(f"{name or 'sm'}_{self._si}"))
        self._si += 1
        return Sem(h)

    def sbuf(self, name, shape, dt):
        self._ti += 1
        return self.cur.enter_context(self.nc.sbuf_tensor(f"{name}_{self._ti}", shape, dt))

    def dram(self, name, shape, dt, kind="Internal"):
        return self.nc.dram_tensor(name, shape, dt, kind=kind).ap()

    def mkdum(self):
        self.dumv = self.sbuf("dumv", [1, 2], F32)
        self.dums = self.sbuf("dums", [1, 2], F32)
        self.dumg = self.sbuf("dumg", [1, 2], F32)

    def init_psum(self, n_f32=8, n_bf16=0):
        self.banks = [self.ctx.enter_context(self.nc.psum_tensor(f"pb{i}", [128, 512], F32))
                      for i in range(n_f32)]
        self.tbanks = [self.ctx.enter_context(self.nc.psum_tensor(f"pt{i}", [128, 1024], BF16))
                       for i in range(n_bf16)]

    def on(self, eng):
        def deco(fn):
            self.q[eng].append(fn)
            return None
        return deco

    def finish(self):
        with self.nc.Block() as blk:
            for eng in ENGINES:
                def body(e, _eng=eng):
                    for fn in self.q[_eng]:
                        fn(e)
                getattr(blk, eng)(body)
        self.ctx.close()
        return self.nc


def emit_consts(p):
    ones_bf = p.sbuf("ones_bf", [128, 128], BF16)
    ones_tok = p.sbuf("ones_tok", [128, 1], BF16)
    ident_bf = p.sbuf("ident_bf", [128, 128], BF16)
    ident_f32 = p.sbuf("ident_f32", [128, 128], F32)
    fmp = p.sbuf("fmp", [128, 128], F32)
    eps5 = p.sbuf("eps5", [128, 1], F32)
    sc = p.sem("consts")

    @p.on("gpsimd")
    def _(g):
        g.iota(fmp[:], pattern=[[1, 128]], channel_multiplier=-1,
               allow_small_or_imprecise_dtypes=True).then_inc(sc.h, 1)
    t_iota = sc.inc()

    @p.on("vector")
    def _(v):
        v.wait_ge(sc.h, t_iota)
        v.memset(ones_bf[:], 1.0)
        v.memset(ones_tok[:], 1.0)
        v.memset(eps5[:], 1e-5)
        v.tensor_scalar(out=ident_bf[:], in0=fmp[:], scalar1=0.0, scalar2=None,
                        op0=ALU.is_equal)
        v.tensor_scalar(out=ident_f32[:], in0=fmp[:], scalar1=0.0, scalar2=None,
                        op0=ALU.is_equal).then_inc(sc.h, 1)
    sc.inc(1)
    return dict(ones_bf=ones_bf, ones_tok=ones_tok, ident_bf=ident_bf, ident_f32=ident_f32,
                eps5=eps5, fmp=fmp, sem=sc, ready=sc.n)


def ln_chunk(p, cst, W, xf, xb, sq, tmp, rm, rr, sems, bank_a, bank_b,
             out_bf=None, out_f32=None, gcol=None, bcol=None,
             bank_guard=0, sq_guard=0, out_guard=(None, 0), x_dep=(None, 0),
             xb_scr=None):
    sv, ss, st = sems["v"], sems["s"], sems["t"]
    entry_sv = sv.n
    dep_sem, dep_val = x_dep

    if xb is None:
        @p.on("vector")
        def _(v, g=sq_guard, entry=entry_sv):
            v.wait_ge(sv.h, entry)
            if dep_sem is not None:
                v.wait_ge(dep_sem, dep_val)
            v.wait_ge(st.h, g)
            ins = None
            for ct in range(CT):
                ins = v.tensor_copy(out=xb_scr(ct), in_=xf(ct))
            ins.then_inc(sv.h, 1)
        sv.inc()
        xb = xb_scr
        entry_sv = sv.n

    @p.on("scalar")
    def _(s, g=sq_guard):
        s.wait_ge(st.h, g)
        if dep_sem is not None:
            s.wait_ge(dep_sem, dep_val)
        ins = None
        for ct in range(CT):
            ins = s.activation(out=sq(ct), in_=xf(ct), func=AF.Square)
        ins.then_inc(ss.h, 1)
    sq_done = ss.inc()

    @p.on("tensor")
    def _(t, g=bank_guard, sq_done=sq_done, xbv=entry_sv):
        t.wait_ge(ss.h, sq_done)
        t.wait_ge(sv.h, xbv)
        if g:
            t.wait_ge(sv.h, g)
        for ct in range(CT):
            t.matmul(bank_a[:, 0:W], cst["ones_bf"][:], xb(ct), start=(ct == 0), stop=(ct == CT - 1))
        ins = None
        for ct in range(CT):
            ins = t.matmul(bank_b[:, 0:W], cst["ones_bf"][:], sq(ct), start=(ct == 0), stop=(ct == CT - 1))
        ins.then_inc(st.h, 1)
    stats_done = st.inc()

    @p.on("vector")
    def _(v, entry=entry_sv, stats_done=stats_done):
        v.wait_ge(st.h, stats_done)
        v.wait_ge(sv.h, entry)
        v.tensor_scalar(out=rm, in0=bank_a[:, 0:W], scalar1=1.0 / C, scalar2=None,
                        op0=ALU.mult).then_inc(sv.h, 1)
        v.wait_ge(sv.h, entry + 1)
        v.tensor_tensor(out=rr, in0=rm, in1=rm, op=ALU.mult).then_inc(sv.h, 1)
        v.wait_ge(sv.h, entry + 2)
        v.scalar_tensor_tensor(out=rr, in0=bank_b[:, 0:W], scalar=1.0 / C, in1=rr,
                               op0=ALU.mult, op1=ALU.subtract).then_inc(sv.h, 1)
        v.wait_ge(sv.h, entry + 3)
        v.tensor_scalar(out=rr, in0=rr, scalar1=0.0, scalar2=None,
                        op0=ALU.max).then_inc(sv.h, 1)
    var_done = sv.inc(4)

    @p.on("scalar")
    def _(s, var_done=var_done):
        s.wait_ge(sv.h, var_done)
        s.activation(out=rr, in_=rr, func=AF.Sqrt, bias=cst["eps5"][:, 0:1]).then_inc(ss.h, 1)
    sd_done = ss.inc()

    @p.on("vector")
    def _(v, sd_done=sd_done, var_done=var_done):
        v.wait_ge(ss.h, sd_done)
        v.reciprocal(out=rr, in_=rr).then_inc(sv.h, 1)
        v.wait_ge(sv.h, var_done + 1)
        v.tensor_tensor(out=rm, in0=rm, in1=rr, op=ALU.mult).then_inc(sv.h, 1)
    rows_done = sv.inc(2)

    ogs = out_guard if isinstance(out_guard, list) else [out_guard]
    ogs = [(a, b) for (a, b) in ogs if a is not None and b]
    og_sem = ogs[0][0] if ogs else None
    og_val = ogs[0][1] if ogs else 0

    @p.on("vector")
    def _(v, rows_done=rows_done):
        v.wait_ge(sv.h, rows_done)
        for _sem, _val in ogs:
            v.wait_ge(_sem, _val)
        ins = None
        for ct in range(CT):
            ins = v.tensor_tensor(out=tmp(ct), in0=xf(ct), in1=rr, op=ALU.mult)
        ins.then_inc(sv.h, 1)
        v.wait_ge(sv.h, rows_done + 1)
        ins = None
        for ct in range(CT):
            dst = out_f32(ct) if (out_f32 is not None and gcol is None) else tmp(ct)
            ins = v.tensor_tensor(out=dst, in0=tmp(ct), in1=rm, op=ALU.subtract)
        ins.then_inc(sv.h, 1)
    z_done = sv.inc(2)
    v_end = z_done
    s_end = ss.n

    if gcol is None:
        if out_bf is not None:
            src = out_f32 if out_f32 is not None else tmp
            @p.on("vector")
            def _(v, z_done=z_done):
                v.wait_ge(sv.h, z_done)
                ins = None
                for ct in range(CT):
                    ins = v.tensor_copy(out=out_bf(ct), in_=src(ct))
                ins.then_inc(sv.h, 1)
            v_end = sv.inc()
    else:
        @p.on("scalar")
        def _(s, z_done=z_done):
            s.wait_ge(sv.h, z_done)
            for _sem, _val in ogs:
                s.wait_ge(_sem, _val)
            ins = None
            for ct in range(CT):
                dst = out_bf(ct) if out_bf is not None else out_f32(ct)
                ins = s.activation(out=dst, in_=tmp(ct), func=AF.Identity,
                                   scale=gcol(ct), bias=bcol(ct))
            ins.then_inc(ss.h, 1)
        s_end = ss.inc()
        if out_bf is not None and out_f32 is not None:
            @p.on("vector")
            def _(v, s_end=s_end):
                v.wait_ge(ss.h, s_end)
                ins = None
                for ct in range(CT):
                    ins = v.tensor_copy(out=out_f32(ct), in_=out_bf(ct))
                ins.then_inc(sv.h, 1)
            v_end = sv.inc()

    return dict(v=v_end, s=s_end, stats=stats_done, var=var_done)


def emit_s0(p, cst, xT, xnT, barrier=False, strong=()):
    CW = 512
    NCH = TOK // CW
    xr = xT.rearrange("(ct p) s t -> p ct (s t)", p=128)
    onr = xnT.rearrange("(ct p) s t -> p ct (s t)", p=128)

    xf = p.sbuf("xf", [128, CT, 2, CW], F32)
    xb = p.sbuf("xb", [128, CT, 2, CW], BF16)
    sq = p.sbuf("sq", [128, CT, CW], BF16)
    of = p.sbuf("of", [128, CT, 2, CW], F32)
    rm = p.sbuf("rm", [128, CW], F32)
    rr = p.sbuf("rr", [128, CW], F32)

    sd = p.sem("dma_sync")
    sg = p.sem("dma_gp")
    so = p.sem("outd")
    sv = p.sem("v")
    ss = p.sem("s")
    st = p.sem("t")
    sems = dict(v=sv, s=ss, t=st)

    sbar = p.sem("bar")
    guard_bank = [0, 0]
    guard_sq = 0
    guard_out = [0, 0]
    prev_v_end = [0, 0]
    prev_stats = [0, 0]

    for ci in range(NCH):
        pb = ci % 2
        t0 = ci * CW

        if "loads" in strong:
            lneed_v, lneed_t = prev_v_end[1 - pb], prev_stats[1 - pb]
        else:
            lneed_v, lneed_t = prev_v_end[pb], prev_stats[pb]
        @p.on("sync")
        def _(e, t0=t0, pb=pb, need=lneed_v, pre=sd.n):
            if need:
                e.wait_ge(sv.h, need)
            dmaw(e, sd, pre, out=xf[:, :, pb, :], in_=xr[:, :, t0:t0 + CW])
        loads_s = sd.inc(16)

        @p.on("gpsimd")
        def _(e, t0=t0, pb=pb, need=lneed_t, pre=sg.n):
            if need:
                e.wait_ge(st.h, need)
            dmaw(e, sg, pre, out=xb[:, :, pb, :], in_=xr[:, :, t0:t0 + CW])
        loads_g = sg.inc(16)

        @p.on("vector")
        def _(v, ns=loads_s, ng=loads_g):
            v.wait_ge(sd.h, ns)
            v.wait_ge(sg.h, ng)
            v.memset(p.dumv[:], 0.0).then_inc(sv.h, 1)
        @p.on("scalar")
        def _(s, ns=loads_s, ng=loads_g):
            s.wait_ge(sd.h, ns)
            s.wait_ge(sg.h, ng)
            s.memzero(p.dums[:]).then_inc(ss.h, 1)
        sv.inc(); ss.inc()

        r = ln_chunk(
            p, cst, CW,
            xf=lambda ct, pb=pb: xf[:, ct, pb, :],
            xb=lambda ct, pb=pb: xb[:, ct, pb, :],
            sq=lambda ct: sq[:, ct, :],
            tmp=lambda ct, pb=pb: of[:, ct, pb, :],
            rm=rm[:], rr=rr[:], sems=sems,
            bank_a=p.banks[2 * pb], bank_b=p.banks[2 * pb + 1],
            out_f32=lambda ct, pb=pb: of[:, ct, pb, :],
            bank_guard=guard_bank[pb], sq_guard=guard_sq,
            out_guard=(so.h, guard_out[pb]),
        )
        guard_bank[pb] = r["v"] if "bank" in strong else r["var"]
        guard_sq = r["stats"]
        prev_v_end[pb] = r["v"]
        prev_stats[pb] = r["stats"]

        @p.on("sync")
        def _(e, t0=t0, pb=pb, need=r["v"], pre=so.n):
            e.wait_ge(sv.h, need)
            dmaw(e, so, pre, out=onr[:, :, t0:t0 + CW], in_=of[:, :, pb, :])
        guard_out[pb] = so.inc(16)

        if barrier:
            tgt = sbar.inc(4)
            for eng in ("vector", "scalar", "tensor", "gpsimd"):
                @p.on(eng)
                def _(e, tgt=tgt, need=so.n):
                    if e is not None:
                        pass
                    e.wait_ge(so.h, need)
                    e.then_inc(sbar.h, 1) if False else e.nop().then_inc(sbar.h, 1)
                    e.wait_ge(sbar.h, tgt)

    @p.on("sync")
    def _(e, need=so.n):
        e.wait_ge(so.h, need)





def emit_s2(p, cst, xnT, qw, qsc, qb, la, lb, lng, lnb, gT, hA, nblocks=3, nch_lim=None):
    gr = gT.rearrange("(ct p) s t -> p ct (s t)", p=128)
    xr = xnT.rearrange("(ct p) s t -> p ct (s t)", p=128)
    hr = hA.rearrange("(ct p) s t -> p ct (s t)", p=128)

    CW = 512
    NCH = TOK // CW if nch_lim is None else nch_lim

    wT = [p.sbuf(f"wT{j}", [128, CT, C], BF16) for j in range(2)]
    laT = [p.sbuf(f"laT{j}", [128, CT, 32], BF16) for j in range(2)]
    lbT = p.sbuf("lbT", [32, 2, C], BF16)
    qs = p.sbuf("qs", [128, 2, C], I32)
    qf = p.sbuf("qf", [128, C], F32)
    wbf = p.sbuf("wbf", [128, 2, C], BF16)
    labf = p.sbuf("labf", [32, C], BF16)
    lbbf = p.sbuf("lbbf", [128, CT, 32], BF16)
    scs = p.sbuf("scs", [128, 6, CT, 64], F32)
    qbc = p.sbuf("qbc", [128, 6, CT], F32)
    lngs = p.sbuf("lngs", [128, 3, CT], F32)
    lnbs = p.sbuf("lnbs", [128, 3, CT], F32)

    hf = p.sbuf("hf", [128, CT, 2, CW], F32)
    hb = p.sbuf("hb", [128, CT, 2, CW], BF16)
    y_sb = p.sbuf("y_sb", [128, CT, CW], F32)
    ybf = p.sbuf("ybf", [128, CT, CW], BF16)
    z_sb = p.sbuf("z_sb", [128, CT, CW], F32)
    g_bf = p.sbuf("g_bf", [128, CT, CW], BF16)
    sq = p.sbuf("sq", [128, CT, CW], BF16)
    yr_bf = p.sbuf("yr_bf", [32, CW], BF16)
    rm = p.sbuf("rm", [128, CW], F32)
    rr = p.sbuf("rr", [128, CW], F32)

    s_par = p.sem("s_par")
    s_q = p.sem("s_q")
    s_hf = p.sem("s_hf")
    s_la = p.sem("s_la")
    s_lb = p.sem("s_lb")
    s_hb = p.sem("s_hb")
    so = p.sem("so")
    sv = p.sem("v")
    ss = p.sem("s")
    st = p.sem("t")
    sems = dict(v=sv, s=ss, t=st)

    @p.on("sync")
    def _(e):
        dmaw(e, s_par, 0, out=scs[:], in_=qsc)
        dmaw(e, s_par, 16, out=qbc[:], in_=qb)
        dmaw(e, s_par, 32, out=lngs[:], in_=lng)
        dmaw(e, s_par, 48, out=lnbs[:], in_=lnb)
    params_ld = s_par.inc(64)

    g_wT_free = [0, 0]
    g_tb = [0, 0]
    g_bank = {0: ("ss", 0), 1: ("ss", 0), 2: ("sv", 0), 3: ("sv", 0), 4: ("sv", 0)}
    g_sq = 0
    g_hn = [0, 0]
    g_qs = [0, 0]
    g_wbf = [0, 0]
    g_labf = [0]
    g_lbbf = [0]
    store_cnt = [[0] * NCH for _ in range(4)]
    semmap = dict(sv=sv, ss=ss, st=st)

    def prep_weights(i, j):
        wg = g_wT_free[j]
        for ot in range(CT):
            sl = ot % 2

            @p.on("sync")
            def _(e, i=i, ot=ot, sl=sl, need=g_qs[sl], pre=s_q.n):
                if need:
                    e.wait_ge(sv.h, need)
                dmaw(e, s_q, pre, out=qs[:, sl, :], in_=qw[i, ot * 128:(ot + 1) * 128, :])
            qld = s_q.inc(16)

            @p.on("vector")
            def _(v, i=i, ot=ot, sl=sl, qld=qld, entry=sv.n, pl=params_ld, wbg=g_wbf[sl]):
                v.wait_ge(s_q.h, qld)
                v.wait_ge(s_par.h, pl)
                v.wait_ge(sv.h, entry)
                v.tensor_copy(out=qf[:], in_=qs[:, sl, :]).then_inc(sv.h, 1)
                v.wait_ge(sv.h, entry + 1)
                v.tensor_scalar(out=qf[:], in0=qf[:], scalar1=2.0 / 15.0, scalar2=-1.0,
                                op0=ALU.mult, op1=ALU.add).then_inc(sv.h, 1)
                v.wait_ge(sv.h, entry + 2)
                if wbg:
                    v.wait_ge(st.h, wbg)
                v.tensor_tensor(
                    out=wbf[:, sl, :].rearrange("p (g k) -> p g k", g=64),
                    in0=qf[:].rearrange("p (g k) -> p g k", g=64),
                    in1=scs[:, i, ot, :].unsqueeze(2).broadcast_to([128, 64, 16]),
                    op=ALU.mult).then_inc(sv.h, 1)
            deq_done = sv.inc(3)
            g_qs[sl] = deq_done - 2

            tb = ot % 2

            @p.on("tensor")
            def _(t, sl=sl, tb=tb, deq=deq_done, tbg=g_tb[tb], cr=cst["ready"]):
                t.wait_ge(sv.h, deq)
                t.wait_ge(cst["sem"].h, cr)
                if tbg:
                    t.wait_ge(sv.h, tbg)
                ins = None
                for dt in range(CT):
                    ins = t.transpose(p.tbanks[tb][:, dt * 128:(dt + 1) * 128],
                                      wbf[:, sl, dt * 128:(dt + 1) * 128], cst["ident_bf"][:])
                ins.then_inc(st.h, 1)
            tr_done = st.inc()
            g_wbf[sl] = tr_done

            @p.on("vector")
            def _(v, ot=ot, j=j, tb=tb, tr=tr_done, wg=wg):
                v.wait_ge(st.h, tr)
                if wg:
                    v.wait_ge(st.h, wg)
                ins = None
                for dt in range(CT):
                    ins = v.tensor_copy(out=wT[j][:, dt, ot * 128:(ot + 1) * 128],
                                        in_=p.tbanks[tb][:, dt * 128:(dt + 1) * 128])
                ins.then_inc(sv.h, 1)
            g_tb[tb] = sv.inc()

        @p.on("gpsimd")
        def _(e, i=i, need=g_labf[0], pre=s_la.n):
            if need:
                e.wait_ge(st.h, need)
            dmaw(e, s_la, pre, out=labf[:], in_=la[i])
        lald = s_la.inc(16)

        @p.on("tensor")
        def _(t, lald=lald, tbg=g_tb[0], cr=cst["ready"]):
            t.wait_ge(s_la.h, lald)
            t.wait_ge(cst["sem"].h, cr)
            if tbg:
                t.wait_ge(sv.h, tbg)
            ins = None
            for dt in range(CT):
                ins = t.transpose(p.tbanks[0][:, dt * 32:(dt + 1) * 32],
                                  labf[:, dt * 128:(dt + 1) * 128], cst["ident_bf"][0:32, 0:32])
            ins.then_inc(st.h, 1)
        latr = st.inc()
        g_labf[0] = latr

        @p.on("vector")
        def _(v, j=j, latr=latr, wg=wg):
            v.wait_ge(st.h, latr)
            if wg:
                v.wait_ge(st.h, wg)
            ins = None
            for dt in range(CT):
                ins = v.tensor_copy(out=laT[j][:, dt, :], in_=p.tbanks[0][:, dt * 32:(dt + 1) * 32])
            ins.then_inc(sv.h, 1)
        g_tb[0] = sv.inc()

        @p.on("gpsimd")
        def _(e, i=i, need=g_lbbf[0], pre=s_lb.n):
            if need:
                e.wait_ge(st.h, need)
            dmaw(e, s_lb, pre, out=lbbf[:], in_=lb[i].rearrange("(ot pp) r -> pp ot r", pp=128))
        lbld = s_lb.inc(16)

        @p.on("tensor")
        def _(t, lbld=lbld, tbg=g_tb[1]):
            t.wait_ge(s_lb.h, lbld)
            if tbg:
                t.wait_ge(sv.h, tbg)
            ins = None
            for ot in range(CT):
                ins = t.transpose(p.tbanks[1][0:32, ot * 128:(ot + 1) * 128],
                                  lbbf[:, ot, :], cst["ident_bf"][:])
            ins.then_inc(st.h, 1)
        lbtr = st.inc()
        g_lbbf[0] = lbtr

        @p.on("vector")
        def _(v, j=j, lbtr=lbtr, wg=wg):
            v.wait_ge(st.h, lbtr)
            if wg:
                v.wait_ge(st.h, wg)
            v.tensor_copy(out=lbT[:, j, :], in_=p.tbanks[1][0:32, 0:1024]).then_inc(sv.h, 1)
        g_tb[1] = sv.inc()

    h_in_r = xr
    for bi in range(nblocks):
        i0, i1 = 2 * bi, 2 * bi + 1
        prep_weights(i0, 0)
        prep_weights(i1, 1)
        w_ready = sv.n

        last = (bi == nblocks - 1)
        for ch in range(NCH):
            pb = ch % 2
            t0 = ch * CW
            prev_store = store_cnt[bi][ch]

            @p.on("sync")
            def _(e, t0=t0, pb=pb, need=sv.n, ps=max(prev_store, g_hn[pb]), pre=s_hf.n, hin=h_in_r):
                if need:
                    e.wait_ge(sv.h, need)
                if ps:
                    e.wait_ge(so.h, ps)
                dmaw(e, s_hf, pre, out=hf[:, :, pb, :], in_=hin[:, :, t0:t0 + CW])
            lds = s_hf.inc(16)

            @p.on("gpsimd")
            def _(e, t0=t0, pb=pb, need=st.n, ps=prev_store, pre=s_hb.n, hin=h_in_r):
                if need:
                    e.wait_ge(st.h, need)
                if ps:
                    e.wait_ge(so.h, ps)
                dmaw(e, s_hb, pre, out=hb[:, :, pb, :], in_=hin[:, :, t0:t0 + CW])
            ldg = s_hb.inc(16)

            @p.on("vector")
            def _(v, ns=lds, ng=ldg, wr=w_ready):
                v.wait_ge(s_hf.h, ns)
                v.wait_ge(s_hb.h, ng)
                v.wait_ge(sv.h, wr)
                v.memset(p.dumv[:], 0.0).then_inc(sv.h, 1)
            ldbar_v = sv.inc()

            @p.on("scalar")
            def _(s, ns=lds, ng=ldg):
                s.wait_ge(s_hf.h, ns)
                s.wait_ge(s_hb.h, ng)
                s.memzero(p.dums[:]).then_inc(ss.h, 1)
            ldbar_s = ss.inc()

            @p.on("tensor")
            def _(t, pb=pb, bar=ldbar_v, bar_s=ldbar_s, gb2=g_bank[2]):
                t.wait_ge(sv.h, bar)
                t.wait_ge(ss.h, bar_s)
                nm2, val2 = gb2
                if val2:
                    t.wait_ge(semmap[nm2].h, val2)
                ins = None
                for dt in range(CT):
                    ins = t.matmul(p.banks[2][0:32, 0:CW], laT[0][:, dt, :], hb[:, dt, pb, :],
                                   start=(dt == 0), stop=(dt == CT - 1))
                ins.then_inc(st.h, 1)
            yr_done = st.inc()

            @p.on("vector")
            def _(v, yr=yr_done, entry=sv.n):
                v.wait_ge(st.h, yr)
                v.wait_ge(sv.h, entry)
                v.tensor_copy(out=yr_bf[:], in_=p.banks[2][0:32, 0:CW]).then_inc(sv.h, 1)
            yr_ev = sv.inc()
            g_bank[2] = ("sv", yr_ev)

            for ot in range(CT):
                bk = ot % 2

                @p.on("tensor")
                def _(t, ot=ot, bk=bk, pb=pb, yr=yr_ev, gb=g_bank[bk]):
                    if ot == 0:
                        t.wait_ge(sv.h, yr)
                    nm, val = gb
                    if val:
                        t.wait_ge(semmap[nm].h, val)
                    for dt in range(CT):
                        t.matmul(p.banks[bk][:, 0:CW], wT[0][:, dt, ot * 128:(ot + 1) * 128],
                                 hb[:, dt, pb, :], start=(dt == 0), stop=False)
                    t.matmul(p.banks[bk][:, 0:CW], lbT[0:32, 0, ot * 128:(ot + 1) * 128],
                             yr_bf[:], start=False, stop=True).then_inc(st.h, 1)
                mm_done = st.inc()

                @p.on("scalar")
                def _(s, ot=ot, bk=bk, i=i0, mmd=mm_done):
                    s.wait_ge(st.h, mmd)
                    s.activation(out=y_sb[:, ot, :], in_=p.banks[bk][:, 0:CW], func=AF.Identity,
                                 bias=qbc[:, i, ot:ot + 1]).then_inc(ss.h, 1)
                g_bank[bk] = ("ss", ss.inc())
            y1_done_s = ss.n

            r = ln_chunk(
                p, cst, CW,
                xf=lambda ct: y_sb[:, ct, :],
                xb=None,
                sq=lambda ct: sq[:, ct, :],
                tmp=lambda ct: z_sb[:, ct, :],
                rm=rm[:], rr=rr[:], sems=sems,
                bank_a=p.banks[3], bank_b=p.banks[4],
                out_f32=lambda ct: z_sb[:, ct, :],
                bank_guard=max(g_bank[3][1], g_bank[4][1]), sq_guard=g_sq,
                x_dep=(ss.h, y1_done_s),
                xb_scr=lambda ct: ybf[:, ct, :],
            )
            g_bank[3] = ("sv", r["var"])
            g_bank[4] = ("sv", r["var"])
            g_sq = r["stats"]

            @p.on("scalar")
            def _(s, bi=bi, zv=r["v"]):
                s.wait_ge(sv.h, zv)
                ins = None
                for ct in range(CT):
                    ins = s.activation(out=g_bf[:, ct, :], in_=z_sb[:, ct, :], func=AF.Gelu,
                                       scale=lngs[:, bi, ct:ct + 1], bias=lnbs[:, bi, ct:ct + 1])
                ins.then_inc(ss.h, 1)
            gelu_done = ss.inc()

            @p.on("tensor")
            def _(t, gd=gelu_done, gb2=g_bank[2]):
                t.wait_ge(ss.h, gd)
                nm2, val2 = gb2
                if val2:
                    t.wait_ge(semmap[nm2].h, val2)
                ins = None
                for dt in range(CT):
                    ins = t.matmul(p.banks[2][0:32, 0:CW], laT[1][:, dt, :], g_bf[:, dt, :],
                                   start=(dt == 0), stop=(dt == CT - 1))
                ins.then_inc(st.h, 1)
            yr2_done = st.inc()

            @p.on("vector")
            def _(v, yr=yr2_done, entry=sv.n):
                v.wait_ge(st.h, yr)
                v.wait_ge(sv.h, entry)
                v.tensor_copy(out=yr_bf[:], in_=p.banks[2][0:32, 0:CW]).then_inc(sv.h, 1)
            yr2_ev = sv.inc()
            g_bank[2] = ("sv", yr2_ev)

            for ot in range(CT):
                bk = ot % 2

                @p.on("tensor")
                def _(t, ot=ot, bk=bk, yr=yr2_ev, gb=g_bank[bk]):
                    if ot == 0:
                        t.wait_ge(sv.h, yr)
                    nm, val = gb
                    if val:
                        t.wait_ge(semmap[nm].h, val)
                    for dt in range(CT):
                        t.matmul(p.banks[bk][:, 0:CW], wT[1][:, dt, ot * 128:(ot + 1) * 128],
                                 g_bf[:, dt, :], start=(dt == 0), stop=False)
                    t.matmul(p.banks[bk][:, 0:CW], lbT[0:32, 1, ot * 128:(ot + 1) * 128],
                             yr_bf[:], start=False, stop=True).then_inc(st.h, 1)
                mm2_done = st.inc()

                @p.on("vector")
                def _(v, ot=ot, bk=bk, i=i1, pb=pb, mmd=mm2_done, entry=sv.n, gd=gelu_done):
                    v.wait_ge(st.h, mmd)
                    v.wait_ge(sv.h, entry)
                    if ot == 0:
                        v.wait_ge(ss.h, gd)
                    v.tensor_scalar(out=z_sb[:, ot, :], in0=p.banks[bk][:, 0:CW],
                                    scalar1=qbc[:, i, ot:ot + 1], scalar2=None,
                                    op0=ALU.add).then_inc(sv.h, 1)
                    v.wait_ge(sv.h, entry + 1)
                    v.tensor_tensor(out=hf[:, ot, pb, :], in0=z_sb[:, ot, :],
                                    in1=hf[:, ot, pb, :], op=ALU.add).then_inc(sv.h, 1)
                g_bank[bk] = ("sv", sv.inc(2))
            h_done_v = sv.n

            if last:
                @p.on("scalar")
                def _(s, pb=pb, hd=h_done_v):
                    s.wait_ge(sv.h, hd)
                    ins = None
                    for ct in range(CT):
                        ins = s.activation(out=hf[:, ct, pb, :], in_=hf[:, ct, pb, :], func=AF.Silu)
                    ins.then_inc(ss.h, 1)
                sil_done = ss.inc()

                @p.on("sync")
                def _(e, t0=t0, pb=pb, sd_=sil_done, pre=so.n):
                    e.wait_ge(ss.h, sd_)
                    dmaw(e, so, pre, out=gr[:, :, t0:t0 + CW], in_=hf[:, :, pb, :])
            else:
                @p.on("sync")
                def _(e, t0=t0, pb=pb, hd=h_done_v, pre=so.n):
                    e.wait_ge(sv.h, hd)
                    dmaw(e, so, pre, out=hr[:, :, t0:t0 + CW], in_=hf[:, :, pb, :])
            store_cnt[bi + 1][ch] = g_hn[pb] = so.inc(16)

        g_wT_free[0] = st.n
        g_wT_free[1] = st.n
        h_in_r = hr

    @p.on("sync")
    def _(e, need=so.n):
        e.wait_ge(so.h, need)


def build_s2(nblocks=3, nch_lim=None):
    p = Prog(); p.mkdum(); p.init_psum(6, 2)
    xnT = p.dram("xnT", [C, S, T], F32, kind="ExternalInput")
    qw = p.dram("qw", [6, C, C], I32, kind="ExternalInput")
    qsc = p.dram("qsc", [128, 6, CT, 64], F32, kind="ExternalInput")
    qb = p.dram("qb", [128, 6, CT], F32, kind="ExternalInput")
    la = p.dram("la", [6, 32, C], F32, kind="ExternalInput")
    lb = p.dram("lb", [6, C, 32], F32, kind="ExternalInput")
    lng = p.dram("lng", [128, 3, CT], F32, kind="ExternalInput")
    lnb = p.dram("lnb", [128, 3, CT], F32, kind="ExternalInput")
    gT = p.dram("gT", [C, S, T], F32, kind="ExternalOutput")
    hA = p.dram("hA", [C, S, T], F32, kind="Internal")
    cst = emit_consts(p)
    with p.stage():
        emit_s2(p, cst, xnT, qw, qsc, qb, la, lb, lng, lnb, gT, hA, nblocks, nch_lim)
    return p.finish()


def emit_comb(p, cst, xT, sT, gT, x1T, xn1T=None):
    with_ln = xn1T is not None
    xr = xT.rearrange("(ct p) s t -> p ct (s t)", p=128)
    sr = sT.rearrange("(ct p) s t -> p ct (s t)", p=128)
    gr = gT.rearrange("(ct p) s t -> p ct (s t)", p=128)
    x1r = x1T.rearrange("(ct p) s t -> p ct (s t)", p=128)
    if with_ln:
        xnr = xn1T.rearrange("(ct p) s t -> p ct (s t)", p=128)

    CW = 512
    NCH = TOK // CW

    xf = p.sbuf("xf", [128, CT, 2, CW], F32)
    sf = p.sbuf("sf", [128, CT, 2, CW], F32)
    gf = p.sbuf("gf", [128, CT, 2, CW], F32)
    x1f = p.sbuf("x1f", [128, CT, 2, CW], F32)
    xb = p.sbuf("xb", [128, CT, CW], BF16)
    sq = p.sbuf("sq", [128, CT, CW], BF16)
    onf = p.sbuf("onf", [128, CT, 2, CW], F32)
    rm = p.sbuf("rm", [128, CW], F32)
    rr = p.sbuf("rr", [128, CW], F32)

    s_x = p.sem("s_x")
    s_s = p.sem("s_s")
    s_g = p.sem("s_g")
    so = p.sem("so")
    sv = p.sem("v")
    ss = p.sem("s")
    st = p.sem("t")
    sems = dict(v=sv, s=ss, t=st)

    g_bank = 0
    g_sq = 0
    g_out = [0, 0]
    g_slotv = [0, 0]

    for ci in range(NCH):
        pb = ci % 2
        t0 = ci * CW

        @p.on("sync")
        def _(e, t0=t0, pb=pb, need=g_slotv[pb], prex=s_x.n, preg=s_g.n):
            if need:
                e.wait_ge(sv.h, need)
            dmaw(e, s_x, prex, out=xf[:, :, pb, :], in_=xr[:, :, t0:t0 + CW])
            dmaw(e, s_g, preg, out=gf[:, :, pb, :], in_=gr[:, :, t0:t0 + CW])
        ldx = s_x.inc(16)
        ldg = s_g.inc(16)

        @p.on("gpsimd")
        def _(e, t0=t0, pb=pb, need=g_slotv[pb], pres=s_s.n):
            if need:
                e.wait_ge(sv.h, need)
            dmaw(e, s_s, pres, out=sf[:, :, pb, :], in_=sr[:, :, t0:t0 + CW])
        lds = s_s.inc(16)

        @p.on("vector")
        def _(v, pb=pb, nx=ldx, ng=ldg, ns=lds, entry=sv.n, og=g_out[pb]):
            v.wait_ge(s_x.h, nx)
            v.wait_ge(s_g.h, ng)
            v.wait_ge(s_s.h, ns)
            v.wait_ge(sv.h, entry)
            if og:
                v.wait_ge(so.h, og)
            ins = None
            for ct in range(CT):
                ins = v.tensor_tensor(out=x1f[:, ct, pb, :], in0=sf[:, ct, pb, :],
                                      in1=gf[:, ct, pb, :], op=ALU.mult)
            ins.then_inc(sv.h, 1)
            v.wait_ge(sv.h, entry + 1)
            ins = None
            for ct in range(CT):
                ins = v.tensor_tensor(out=x1f[:, ct, pb, :], in0=x1f[:, ct, pb, :],
                                      in1=xf[:, ct, pb, :], op=ALU.add)
            ins.then_inc(sv.h, 1)
        x1_done = sv.inc(2)

        @p.on("sync")
        def _(e, t0=t0, pb=pb, x1d=x1_done, pre=so.n):
            e.wait_ge(sv.h, x1d)
            dmaw(e, so, pre, out=x1r[:, :, t0:t0 + CW], in_=x1f[:, :, pb, :])
        st1 = so.inc(16)

        if with_ln:
            r = ln_chunk(
                p, cst, CW,
                xf=lambda ct, pb=pb: x1f[:, ct, pb, :],
                xb=None,
                sq=lambda ct: sq[:, ct, :],
                tmp=lambda ct, pb=pb: onf[:, ct, pb, :],
                rm=rm[:], rr=rr[:], sems=sems,
                bank_a=p.banks[0 + 2 * pb], bank_b=p.banks[1 + 2 * pb],
                out_f32=lambda ct, pb=pb: onf[:, ct, pb, :],
                bank_guard=g_bank, sq_guard=g_sq,
                out_guard=(so.h, g_out[pb]),
                x_dep=(sv.h, x1_done),
                xb_scr=lambda ct: xb[:, ct, :],
            )
            g_bank = r["var"]
            g_sq = r["stats"]

            @p.on("sync")
            def _(e, t0=t0, pb=pb, need=r["v"], pre=so.n):
                e.wait_ge(sv.h, need)
                dmaw(e, so, pre, out=xnr[:, :, t0:t0 + CW], in_=onf[:, :, pb, :])
            g_out[pb] = so.inc(16)
            g_slotv[pb] = r["v"]
        else:
            g_out[pb] = st1
            g_slotv[pb] = x1_done

    @p.on("sync")
    def _(e, need=so.n):
        e.wait_ge(so.h, need)


def build_comb(with_ln=True):
    p = Prog(); p.mkdum(); p.init_psum(6, 2)
    xT = p.dram("xT", [C, S, T], F32, kind="ExternalInput")
    sT = p.dram("sT", [C, S, T], F32, kind="ExternalInput")
    gT = p.dram("gT", [C, S, T], F32, kind="ExternalInput")
    x1T = p.dram("x1T", [C, S, T], F32, kind="ExternalOutput")
    xn1T = p.dram("xn1T", [C, S, T], F32, kind="ExternalOutput") if with_ln else None
    cst = emit_consts(p)
    with p.stage():
        emit_comb(p, cst, xT, sT, gT, x1T, xn1T)
    return p.finish()


def emit_s5(p, cst, xnT, wgt, wut, wdt, wrt, rb, g1T, nsc_lim=None):
    xr = xnT.rearrange("(ct p) s t -> p ct (s t)", p=128)
    gr = g1T.rearrange("(ct p) s t -> p ct (s t)", p=128)

    SCW = 1024
    SW = 512
    NSC = TOK // SCW if nsc_lim is None else nsc_lim
    NSW = SCW // SW
    NE = 9
    IT = 16

    xb = p.sbuf("xb", [128, CT, NSW, SW], BF16)
    wg_sb = p.sbuf("wg_sb", [128, CT, 2048], BF16)
    wu_sb = p.sbuf("wu_sb", [128, CT, 2048], BF16)
    wd_sb = p.sbuf("wd_sb", [128, IT, C], BF16)
    wr_sb = p.sbuf("wr_sb", [128, CT, 8], BF16)
    rb_sb = p.sbuf("rb_sb", [8, 1], F32)
    e8 = p.sbuf("e8", [8, 8, 128], BF16)
    probsT = p.sbuf("probsT", [8, NSW, SW], F32)
    ptm = p.sbuf("ptm", [128, NSW, 4, 8], F32)
    m1q = p.sbuf("m1q", [128, 4], F32)
    m2q = p.sbuf("m2q", [128, 4], F32)
    pm = p.sbuf("pm", [128, 4, 8], F32)
    cwtm = p.sbuf("cwtm", [128, 4, 8], F32)
    cwT = p.sbuf("cwT", [8, NSW, SW], BF16)
    cwbc = p.sbuf("cwbc", [128, SW], BF16)
    sil = p.sbuf("sil", [128, SW], F32)
    hid = p.sbuf("hid", [128, IT, SW], BF16)
    out_sb = p.sbuf("out_sb", [128, CT, NSW, SW], F32)

    s_x = p.sem("s_x")
    s_wg = p.sem("s_wg")
    s_wu = p.sem("s_wu")
    s_wd = p.sem("s_wd")
    s_pr = p.sem("s_pr")
    so = p.sem("so")
    sv = p.sem("v")
    ss = p.sem("s")
    st = p.sem("t")
    s_e8 = p.sem("s_e8")
    s_rb = p.sem("s_rb")

    @p.on("gpsimd")
    def _(e):
        dmaw(e, s_pr, 0, out=wr_sb[:], in_=wrt.rearrange("(ct p) e -> p ct e", p=128))
        e.iota(e8[:], pattern=[[1, 8], [0, 128]], channel_multiplier=-1,
               allow_small_or_imprecise_dtypes=True).then_inc(s_e8.h, 1)
    pr1 = s_pr.inc(16)
    e8i = s_e8.inc()

    @p.on("sync")
    def _(e):
        dmaw(e, s_rb, 0, out=rb_sb[:], in_=rb)
    pr2 = s_rb.inc(16)

    @p.on("vector")
    def _(v, pr=pr2, cr=cst["ready"], e8i_=e8i, w1=pr1):
        v.wait_ge(s_rb.h, pr)
        v.wait_ge(s_pr.h, w1)
        v.wait_ge(s_e8.h, e8i_)
        v.wait_ge(cst["sem"].h, cr)
        v.tensor_scalar(out=e8[:], in0=e8[:], scalar1=0.0, scalar2=None,
                        op0=ALU.is_equal).then_inc(sv.h, 1)
    e8_done = sv.inc()

    g_xb = 0
    g_w = [0, 0, 0]
    g_hid = 0
    g_out = 0
    g_rbank = ("ss", 0)
    g_AB = [("sv", 0), ("sv", 0)]
    g_D = [("sv", 0), ("sv", 0)]
    g_C = ("sv", 0)
    semmap = dict(sv=sv, ss=ss, st=st)

    def wgd(t, g):
        nm, val = g
        if val:
            t.wait_ge(semmap[nm].h, val)

    for scn in range(NSC):
        t0 = scn * SCW

        @p.on("gpsimd")
        def _(e, t0=t0, need=g_xb, pre=s_x.n):
            if need:
                e.wait_ge(st.h, need)
            dmaw(e, s_x, pre,
                 out=xb[:], in_=xr[:, :, t0:t0 + SCW].rearrange("p ct (w q) -> p ct w q", q=SW))
        xld = s_x.inc(16)

        @p.on("tensor")
        def _(t, xld=xld, ed=e8_done, rb_=g_rbank, gC_=g_C):
            t.wait_ge(s_x.h, xld)
            t.wait_ge(sv.h, ed)
            wgd(t, rb_)
            wgd(t, gC_)
            ins = None
            for w in range(NSW):
                for kt in range(CT):
                    ins = t.matmul(p.banks[4 + w][0:8, 0:SW], wr_sb[:, kt, :],
                                   xb[:, kt, w, :], start=(kt == 0), stop=(kt == CT - 1))
            ins.then_inc(st.h, 1)
        rt_done = st.inc()

        @p.on("scalar")
        def _(s, rt=rt_done):
            s.wait_ge(st.h, rt)
            ins = None
            for w in range(NSW):
                ins = s.activation(out=probsT[:, w, :], in_=p.banks[4 + w][0:8, 0:SW],
                                   func=AF.Sigmoid, bias=rb_sb[:, 0:1])
            ins.then_inc(ss.h, 1)
        sig_done = ss.inc()
        g_rbank = ("ss", sig_done)

        @p.on("tensor")
        def _(t, sg=sig_done, cr=cst["ready"]):
            t.wait_ge(ss.h, sg)
            t.wait_ge(cst["sem"].h, cr)
            ins = None
            for w in range(NSW):
                for q in range(4):
                    ins = t.transpose(p.banks[5][0:128, w * 32 + q * 8:w * 32 + q * 8 + 8],
                                      probsT[:, w, q * 128:(q + 1) * 128], cst["ident_f32"][0:8, 0:8])
            ins.then_inc(st.h, 1)
        ptr_done = st.inc()

        @p.on("vector")
        def _(v, tr=ptr_done, entry=sv.n):
            v.wait_ge(st.h, tr)
            v.wait_ge(sv.h, entry)
            ins = None
            for w in range(NSW):
                ins = v.tensor_copy(out=ptm[:, w, :, :],
                                    in_=p.banks[5][0:128, w * 32:(w + 1) * 32].rearrange("p (q e) -> p q e", e=8))
            ins.then_inc(sv.h, 1)
        ptm_ev = sv.inc()

        cw_done = 0
        g_cwtm_guard = 0
        for w in range(NSW):
            @p.on("vector")
            def _(v, w=w, entry=sv.n, trg=g_cwtm_guard):
                v.wait_ge(sv.h, entry)
                if trg:
                    v.wait_ge(st.h, trg)
                v.reduce_max(out=m1q[:], in_=ptm[:, w, :, :], axis=AXX).then_inc(sv.h, 1)
                v.wait_ge(sv.h, entry + 1)
                ins = None
                for q in range(4):
                    ins = v.tensor_scalar(out=pm[:, q, :], in0=ptm[:, w, q, :],
                                          scalar1=m1q[:, q:q + 1], scalar2=-1e30,
                                          op0=ALU.is_equal, op1=ALU.mult)
                ins.then_inc(sv.h, 1)
                v.wait_ge(sv.h, entry + 2)
                v.tensor_tensor(out=pm[:], in0=pm[:], in1=ptm[:, w, :, :], op=ALU.add).then_inc(sv.h, 1)
                v.wait_ge(sv.h, entry + 3)
                v.reduce_max(out=m2q[:], in_=pm[:], axis=AXX).then_inc(sv.h, 1)
                v.wait_ge(sv.h, entry + 4)
                ins = None
                for q in range(4):
                    ins = v.tensor_scalar(out=cwtm[:, q, :], in0=ptm[:, w, q, :],
                                          scalar1=m2q[:, q:q + 1], scalar2=None, op0=ALU.is_ge)
                ins.then_inc(sv.h, 1)
                v.wait_ge(sv.h, entry + 5)
                v.tensor_tensor(out=cwtm[:], in0=cwtm[:], in1=ptm[:, w, :, :],
                                op=ALU.mult).then_inc(sv.h, 1)
            cw_done = sv.inc(6)

            @p.on("tensor")
            def _(t, w=w, cd=cw_done, bg=g_rbank):
                t.wait_ge(sv.h, cd)
                wgd(t, bg)
                ins = None
                for q in range(4):
                    ins = t.transpose(p.banks[5][0:8, q * 128:(q + 1) * 128],
                                      cwtm[:, q, :], cst["ident_f32"][:])
                ins.then_inc(st.h, 1)
            cwtr = st.inc()

            @p.on("vector")
            def _(v, w=w, tr=cwtr, entry=sv.n):
                v.wait_ge(st.h, tr)
                v.wait_ge(sv.h, entry)
                v.tensor_copy(out=cwT[:, w, :], in_=p.banks[5][0:8, 0:SW]).then_inc(sv.h, 1)
            g_rbank = ("sv", sv.inc())
            g_cwtm_guard = cwtr
        cwT_ev = sv.n

        for e_i in range(NE):
            @p.on("gpsimd")
            def _(e, e_i=e_i, need=g_w[0], pre=s_wg.n):
                if need:
                    e.wait_ge(st.h, need)
                dmaw(e, s_wg, pre, out=wg_sb[:], in_=wgt[e_i].rearrange("(kt p) i -> p kt i", p=128))
            wgld = s_wg.inc(16)

            @p.on("gpsimd")
            def _(e, e_i=e_i, need=g_w[1], pre=s_wu.n):
                if need:
                    e.wait_ge(st.h, need)
                dmaw(e, s_wu, pre, out=wu_sb[:], in_=wut[e_i].rearrange("(kt p) i -> p kt i", p=128))
            wuld = s_wu.inc(16)

            @p.on("gpsimd")
            def _(e, e_i=e_i, need=g_w[2], pre=s_wd.n):
                if need:
                    e.wait_ge(st.h, need)
                dmaw(e, s_wd, pre, out=wd_sb[:], in_=wdt[e_i].rearrange("(kt p) h -> p kt h", p=128))
            wdld = s_wd.inc(16)

            for w in range(NSW):
                if e_i > 0:
                    @p.on("tensor")
                    def _(t, e_i=e_i, w=w, cv=cwT_ev, gC=g_C):
                        t.wait_ge(sv.h, cv)
                        wgd(t, gC)
                        t.matmul(p.banks[4][:, 0:SW], e8[:, e_i - 1, :], cwT[:, w, :],
                                 start=True, stop=True).then_inc(st.h, 1)
                    cwbc_mm = st.inc()

                    @p.on("vector")
                    def _(v, mmd=cwbc_mm, entry=sv.n):
                        v.wait_ge(st.h, mmd)
                        v.wait_ge(sv.h, entry)
                        v.tensor_copy(out=cwbc[:], in_=p.banks[4][:, 0:SW]).then_inc(sv.h, 1)
                    g_C = ("sv", sv.inc())

                for mt in range(IT):
                    @p.on("tensor")
                    def _(t, e_i=e_i, w=w, mt=mt, wg_=wgld, wu_=wuld, gA=g_AB[0], gB=g_AB[1], xld=xld):
                        t.wait_ge(s_wg.h, wg_)
                        t.wait_ge(s_wu.h, wu_)
                        t.wait_ge(s_x.h, xld)
                        wgd(t, gA)
                        for kt in range(CT):
                            t.matmul(p.banks[0][:, 0:SW], wg_sb[:, kt, mt * 128:(mt + 1) * 128],
                                     xb[:, kt, w, :], start=(kt == 0), stop=(kt == CT - 1))
                        wgd(t, gB)
                        ins = None
                        for kt in range(CT):
                            ins = t.matmul(p.banks[1][:, 0:SW], wu_sb[:, kt, mt * 128:(mt + 1) * 128],
                                           xb[:, kt, w, :], start=(kt == 0), stop=(kt == CT - 1))
                        ins.then_inc(st.h, 1)
                    gu_done = st.inc()

                    @p.on("scalar")
                    def _(s, mmd=gu_done, entry=ss.n, vg=sv.n):
                        s.wait_ge(st.h, mmd)
                        s.wait_ge(ss.h, entry)
                        s.wait_ge(sv.h, vg)
                        s.activation(out=sil[:], in_=p.banks[0][:, 0:SW], func=AF.Silu).then_inc(ss.h, 1)
                    sil_done = ss.inc()
                    g_AB[0] = ("ss", sil_done)

                    @p.on("vector")
                    def _(v, e_i=e_i, mt=mt, sd_=sil_done, entry=sv.n, hg=g_hid if mt == 0 else 0):
                        v.wait_ge(ss.h, sd_)
                        v.wait_ge(sv.h, entry)
                        if hg:
                            v.wait_ge(st.h, hg)
                        if e_i == 0:
                            v.tensor_tensor(out=hid[:, mt, :], in0=sil[:], in1=p.banks[1][:, 0:SW],
                                            op=ALU.mult).then_inc(sv.h, 1)
                        else:
                            v.tensor_tensor(out=sil[:], in0=sil[:], in1=p.banks[1][:, 0:SW],
                                            op=ALU.mult).then_inc(sv.h, 1)
                            v.wait_ge(sv.h, entry + 1)
                            v.tensor_tensor(out=hid[:, mt, :], in0=sil[:], in1=cwbc[:],
                                            op=ALU.mult).then_inc(sv.h, 1)
                    nops = 1 if e_i == 0 else 2
                    hid_done = sv.inc(nops)
                    g_AB[1] = ("sv", hid_done)
                g_hid_new = 0

                for ot in range(CT):
                    bk = 2 + (ot % 2)

                    @p.on("tensor")
                    def _(t, e_i=e_i, w=w, ot=ot, bk=bk, hd=sv.n, wd_=wdld, gD=g_D[ot % 2]):
                        t.wait_ge(sv.h, hd)
                        t.wait_ge(s_wd.h, wd_)
                        wgd(t, gD)
                        ins = None
                        for kt in range(IT):
                            ins = t.matmul(p.banks[bk][:, 0:SW], wd_sb[:, kt, ot * 128:(ot + 1) * 128],
                                           hid[:, kt, :], start=(kt == 0), stop=(kt == IT - 1))
                        ins.then_inc(st.h, 1)
                    dn_done = st.inc()
                    g_hid_new = dn_done

                    @p.on("vector")
                    def _(v, e_i=e_i, w=w, ot=ot, bk=bk, mmd=dn_done, entry=sv.n, og=g_out):
                        v.wait_ge(st.h, mmd)
                        v.wait_ge(sv.h, entry)
                        if e_i == 0 and og:
                            v.wait_ge(so.h, og)
                        if e_i == 0:
                            v.tensor_copy(out=out_sb[:, ot, w, :], in_=p.banks[bk][:, 0:SW]).then_inc(sv.h, 1)
                        else:
                            v.tensor_tensor(out=out_sb[:, ot, w, :], in0=p.banks[bk][:, 0:SW],
                                            in1=out_sb[:, ot, w, :], op=ALU.add).then_inc(sv.h, 1)
                    g_D[ot % 2] = ("sv", sv.inc())
                g_hid = g_hid_new
            g_w = [st.n, st.n, st.n]
        acc_done = sv.n

        @p.on("scalar")
        def _(s, ad=acc_done, entry=ss.n):
            s.wait_ge(sv.h, ad)
            s.wait_ge(ss.h, entry)
            ins = None
            for ct in range(CT):
                for w in range(NSW):
                    ins = s.activation(out=out_sb[:, ct, w, :], in_=out_sb[:, ct, w, :], func=AF.Silu)
            ins.then_inc(ss.h, 1)
        osil = ss.inc()

        @p.on("sync")
        def _(e, t0=t0, od=osil, pre=so.n):
            e.wait_ge(ss.h, od)
            dmaw(e, so, pre,
                 out=gr[:, :, t0:t0 + SCW].rearrange("p ct (w q) -> p ct w q", q=SW), in_=out_sb[:])
        g_out = so.inc(16)
        g_xb = st.n

    @p.on("sync")
    def _(e, need=so.n):
        e.wait_ge(so.h, need)


def build_s5(nsc_lim=None):
    p = Prog(); p.mkdum(); p.init_psum(6, 2)
    xnT = p.dram("xnT", [C, S, T], F32, kind="ExternalInput")
    wgt = p.dram("wgt", [9, C, 2048], F32, kind="ExternalInput")
    wut = p.dram("wut", [9, C, 2048], F32, kind="ExternalInput")
    wdt = p.dram("wdt", [9, 2048, C], F32, kind="ExternalInput")
    wrt = p.dram("wrt", [C, 8], F32, kind="ExternalInput")
    rb = p.dram("rb", [8, 1], F32, kind="ExternalInput")
    g1T = p.dram("g1T", [C, S, T], F32, kind="ExternalOutput")
    cst = emit_consts(p)
    with p.stage():
        emit_s5(p, cst, xnT, wgt, wut, wdt, wrt, rb, g1T, nsc_lim)
    return p.finish()


def emit_s4(p, cst, xnT, qkvw, outw, w1, w2, qkvb, outb, b1c, b2c, g1c, be1c, g2c, be2c,
            sT, nseq=S, ncph=None):
    xr = xnT.rearrange("(ct p) s t -> p ct s t", p=128)
    sr = sT.rearrange("(ct p) s t -> p ct s t", p=128)

    CW = 512
    NCH = T // CW if ncph is None else ncph
    NH = 16

    wq = p.sbuf("wq", [128, 8, 1024], BF16)
    wse = p.sbuf("wse", [128, 32, 128], BF16)
    parms = p.sbuf("parms", [128, 104], F32)
    qfT = p.sbuf("qfT", [128, CT, T], BF16)
    ktm = p.sbuf("ktm", [128, 4, C], BF16)
    vtm = p.sbuf("vtm", [128, 4, C], BF16)
    kvbf = p.sbuf("kvbf", [128, 8, 64], BF16)
    zbf = p.sbuf("zbf", [128, CT, 16], BF16)
    e16 = p.sbuf("e16", [16, 8, 2, 64], BF16)
    xf = p.sbuf("xf", [128, CT, CW], F32)
    xbfc = p.sbuf("xbfc", [128, CT, CW], BF16)
    sq = p.sbuf("sq", [128, CT, CW], BF16)
    h1b = p.sbuf("h1b", [128, CT, CW], BF16)
    tmpf = p.sbuf("tmpf", [128, CT, CW], F32)
    rm = p.sbuf("rm", [128, CW], F32)
    rr = p.sbuf("rr", [128, CW], F32)
    emax = rm
    emin = rr
    rden = p.sbuf("rden", [16, CW], BF16)
    rdf = p.sbuf("rdf", [16, CW], F32)
    rbc = p.sbuf("rbc", [128, CW], BF16)
    yT = p.sbuf("yT", [128, CT, CW], BF16)
    xA = p.sbuf("xA", [128, CT, CW], F32)
    h2b = p.sbuf("h2b", [128, CT, CW], BF16)
    ffb = p.sbuf("ffb", [128, 32, CW], BF16)
    kfc = ffb[:, 0:8, :]
    vtc = ffb[:, 8:16, :]
    sff = tmpf

    s_x = p.sem("s_x")
    s_xb = p.sem("s_xb")
    s_w = p.sem("s_w")
    s_w2 = p.sem("s_w2")
    s_pp = p.sem("s_pp")
    so = p.sem("so")
    sv = p.sem("v")
    ss = p.sem("s")
    st = p.sem("t")
    s_e16 = p.sem("s_e16")
    sems = dict(v=sv, s=ss, t=st)
    semmap = dict(sv=sv, ss=ss, st=st)

    def wgd(t, g):
        nm, val = g
        if val:
            t.wait_ge(semmap[nm].h, val)

    OFF_QKVB, OFF_B1, OFF_OUTB, OFF_B2, OFF_G1, OFF_BE1, OFF_G2, OFF_BE2 = 0, 24, 56, 64, 72, 80, 88, 96

    @p.on("sync")
    def _(e):
        dmaw(e, s_pp, 0, out=parms[:, 0:24], in_=qkvb)
        dmaw(e, s_pp, 16, out=parms[:, 24:56], in_=b1c)
        dmaw(e, s_pp, 32, out=parms[:, 56:64], in_=outb)
        dmaw(e, s_pp, 48, out=parms[:, 64:72], in_=b2c)
        dmaw(e, s_pp, 64, out=parms[:, 72:80], in_=g1c)
        dmaw(e, s_pp, 80, out=parms[:, 80:88], in_=be1c)
        dmaw(e, s_pp, 96, out=parms[:, 88:96], in_=g2c)
        dmaw(e, s_pp, 112, out=parms[:, 96:104], in_=be2c)
    pld = s_pp.inc(128)

    @p.on("gpsimd")
    def _(e):
        e.iota(e16[:], pattern=[[2, 8], [1, 2], [0, 64]], channel_multiplier=-1,
               allow_small_or_imprecise_dtypes=True).then_inc(s_e16.h, 1)
    e16i = s_e16.inc()

    @p.on("vector")
    def _(v, ei=e16i, cr=cst["ready"], pl=pld):
        v.wait_ge(s_e16.h, ei)
        v.wait_ge(cst["sem"].h, cr)
        v.wait_ge(s_pp.h, pl)
        v.tensor_scalar(out=e16[:], in0=e16[:], scalar1=0.0, scalar2=None,
                        op0=ALU.is_equal).then_inc(sv.h, 1)
    e16_done = sv.inc()

    pcol = lambda off, i: parms[:, off + i:off + i + 1]

    g_w = ("st", 0)
    g_w2 = ("st", 0)
    g_AB = [("ss", 0), ("ss", 0)]
    g_DEN = ("sv", 0)
    g_BC = ("sv", 0)
    g_T = [("sv", 0), ("sv", 0)]
    g_sq = 0
    g_tm = 0
    g_xslot = ("sv", 0)
    g_store = 0
    g_qf = 0
    g_h1b = 0
    g_h2b = 0
    g_ffb = 0
    g_rden = 0
    g_B23 = [("sv", 0), ("sv", 0)]

    for sqi in range(nseq):
        g_tm = st.n
        for ch in range(NCH):
            t0 = ch * CW

            @p.on("sync")
            def _(e, sqi=sqi, t0=t0, gx=g_xslot, pre=s_x.n):
                nm, val = gx
                if val:
                    e.wait_ge(semmap[nm].h, val)
                dmaw(e, s_x, pre, out=xf[:], in_=xr[:, :, sqi, t0:t0 + CW])
            xld = s_x.inc(16)

            @p.on("gpsimd")
            def _(e, sqi=sqi, t0=t0, need=g_sq, pre=s_xb.n):
                if need:
                    e.wait_ge(st.h, need)
                dmaw(e, s_xb, pre, out=xbfc[:], in_=xr[:, :, sqi, t0:t0 + CW])
            xbld = s_xb.inc(16)

            @p.on("vector")
            def _(v, nx=xld, nb=xbld, ed=e16_done):
                v.wait_ge(s_x.h, nx)
                v.wait_ge(s_xb.h, nb)
                v.wait_ge(sv.h, ed)
                v.memset(p.dumv[:], 0.0).then_inc(sv.h, 1)
            bar_v = sv.inc()

            @p.on("scalar")
            def _(s, nx=xld, nb=xbld):
                s.wait_ge(s_x.h, nx)
                s.wait_ge(s_xb.h, nb)
                s.memzero(p.dums[:]).then_inc(ss.h, 1)
            bar_s = ss.inc()

            r = ln_chunk(
                p, cst, CW,
                xf=lambda ct: xf[:, ct, :],
                xb=lambda ct: xbfc[:, ct, :],
                sq=lambda ct: sq[:, ct, :],
                tmp=lambda ct: tmpf[:, ct, :],
                rm=rm[:], rr=rr[:], sems=sems,
                bank_a=p.banks[4], bank_b=p.banks[5],
                out_bf=lambda ct: h1b[:, ct, :],
                gcol=lambda ct: pcol(OFF_G1, ct), bcol=lambda ct: pcol(OFF_BE1, ct),
                bank_guard=max(g_DEN[1], g_BC[1]), sq_guard=g_sq,
                x_dep=(ss.h, bar_s),
                out_guard=[(st.h, g_h1b), (so.h, g_store)],
            )
            g_DEN = ("sv", r["var"]); g_BC = ("sv", r["var"])
            g_sq = r["stats"]
            h1_done_s = r["s"]
            g_xslot = ("sv", r["v"])

            for part in range(3):
                @p.on("gpsimd")
                def _(e, part=part, gw=g_w, pre=s_w.n):
                    nm, val = gw
                    if val:
                        e.wait_ge(semmap[nm].h, val)
                    dmaw(e, s_w, pre, out=wq[:],
                         in_=qkvw.rearrange("(kt p) o -> p kt o", p=128)[:, :, part * 1024:(part + 1) * 1024])
                wld = s_w.inc(16)

                for m8 in range(CT):
                    mt = part * 8 + m8
                    bk = 0

                    @p.on("tensor")
                    def _(t, m8=m8, bk=bk, wld=wld, hd=h1_done_s, gb=g_AB[bk]):
                        t.wait_ge(s_w.h, wld)
                        t.wait_ge(ss.h, hd)
                        wgd(t, gb)
                        ins = None
                        for kt in range(CT):
                            ins = t.matmul(p.banks[bk][:, 0:CW],
                                           wq[:, kt, m8 * 128:(m8 + 1) * 128],
                                           h1b[:, kt, :], start=(kt == 0), stop=(kt == CT - 1))
                        ins.then_inc(st.h, 1)
                    mm_done = st.inc()
                    g_w = ("st", mm_done)

                    if part < 2:
                        if part == 0:
                            dst = (lambda m8=m8, t0=t0: qfT[:, m8, t0:t0 + CW])
                        else:
                            dst = (lambda m8=m8: kfc[:, m8, :])

                        @p.on("vector")
                        def _(v, mt=mt, bk=bk, mmd=mm_done, entry=sv.n,
                              qg=(g_qf if part == 0 else g_tm)):
                            v.wait_ge(st.h, mmd)
                            v.wait_ge(sv.h, entry)
                            if qg:
                                v.wait_ge(st.h, qg)
                            v.tensor_scalar(out=emax[:], in0=p.banks[bk][:, 0:CW],
                                            scalar1=pcol(OFF_QKVB, mt), scalar2=0.0,
                                            op0=ALU.add, op1=ALU.max).then_inc(sv.h, 1)
                            v.wait_ge(sv.h, entry + 1)
                            v.tensor_scalar(out=emin[:], in0=p.banks[bk][:, 0:CW],
                                            scalar1=pcol(OFF_QKVB, mt), scalar2=0.0,
                                            op0=ALU.add, op1=ALU.min).then_inc(sv.h, 1)
                        fm1 = sv.inc(2)

                        @p.on("scalar")
                        def _(s, fm=fm1, entry=ss.n):
                            s.wait_ge(sv.h, fm)
                            s.wait_ge(ss.h, entry)
                            s.activation(out=emin[:], in_=emin[:], func=AF.Exp).then_inc(ss.h, 1)
                        ex_done = ss.inc()

                        @p.on("vector")
                        def _(v, d=dst, xd=ex_done, entry=sv.n):
                            v.wait_ge(ss.h, xd)
                            v.wait_ge(sv.h, entry)
                            v.tensor_tensor(out=d(), in0=emax[:], in1=emin[:], op=ALU.add).then_inc(sv.h, 1)
                        g_AB[bk] = ("sv", sv.inc())
                    else:
                        @p.on("vector")
                        def _(v, mt=mt, m8=m8, bk=bk, mmd=mm_done, entry=sv.n, tg=g_tm):
                            v.wait_ge(st.h, mmd)
                            v.wait_ge(sv.h, entry)
                            if tg:
                                v.wait_ge(st.h, tg)
                            v.tensor_scalar(out=vtc[:, m8, :], in0=p.banks[bk][:, 0:CW],
                                            scalar1=pcol(OFF_QKVB, mt), scalar2=None,
                                            op0=ALU.add).then_inc(sv.h, 1)
                        g_AB[bk] = ("sv", sv.inc())
            fm_all = sv.n
            g_h1b = st.n

            for which in range(2):
                src = kfc if which == 0 else vtc
                dtm = ktm if which == 0 else vtm
                for tt in range(4):
                    tb = (which * 4 + tt) % 2

                    @p.on("tensor")
                    def _(t, src=src, tt=tt, tb=tb, fa=fm_all, gT_=g_T[tb], cr=cst["ready"]):
                        t.wait_ge(sv.h, fa)
                        wgd(t, gT_)
                        ins = None
                        for ct in range(CT):
                            ins = t.transpose(p.tbanks[tb][:, ct * 128:(ct + 1) * 128],
                                              src[:, ct, tt * 128:(tt + 1) * 128], cst["ident_bf"][:])
                        ins.then_inc(st.h, 1)
                    trd = st.inc()

                    @p.on("vector")
                    def _(v, dtm=dtm, tt=tt, tb=tb, trd=trd, entry=sv.n, tg=g_tm):
                        v.wait_ge(st.h, trd)
                        v.wait_ge(sv.h, entry)
                        if tg:
                            v.wait_ge(st.h, tg)
                        v.tensor_copy(out=dtm[:, tt, :], in_=p.tbanks[tb][:, 0:1024]).then_inc(sv.h, 1)
                    g_T[tb] = ("sv", sv.inc())
            tm_done = sv.n

            @p.on("tensor")
            def _(t, ch=ch, td=tm_done, gKV=g_B23[0], gZ=g_B23[1], gZ1=g_AB[1], cr=cst["ready"]):
                t.wait_ge(sv.h, td)
                if ch == 0:
                    wgd(t, gKV)
                    wgd(t, gZ)
                    wgd(t, gZ1)
                ins = None
                for j in range(CT):
                    bkv = 2 if j < 4 else 3
                    c0 = (j % 4) * 128
                    for tt in range(4):
                        t.matmul(p.banks[bkv][:, c0:c0 + 128],
                                 ktm[:, tt, j * 128:(j + 1) * 128],
                                 vtm[:, tt, j * 128:(j + 1) * 128],
                                 start=(ch == 0 and tt == 0 and j % 4 == 0),
                                 stop=(ch == NCH - 1 and tt == 3 and j % 4 == 3))
                        ins = t.matmul(p.banks[1][:, j:j + 1],
                                       ktm[:, tt, j * 128:(j + 1) * 128],
                                       cst["ones_tok"][:],
                                       start=(ch == 0 and tt == 0 and j == 0),
                                       stop=(ch == NCH - 1 and tt == 3 and j == CT - 1))
                ins.then_inc(st.h, 1)
            g_tm = kv_mm = st.inc()

        @p.on("vector")
        def _(v, kd=kv_mm, entry=sv.n):
            v.wait_ge(st.h, kd)
            v.wait_ge(sv.h, entry)
            ins = None
            for h in range(NH):
                j = h // 2
                bkv = 2 if j < 4 else 3
                c0 = (j % 4) * 128 + (h % 2) * 64
                pb_ = (h % 2) * 64
                ins = v.tensor_copy(out=kvbf[pb_:pb_ + 64, j, :],
                                    in_=p.banks[bkv][pb_:pb_ + 64, c0:c0 + 64])
            ins.then_inc(sv.h, 1)
            v.wait_ge(sv.h, entry + 1)
            v.memset(zbf[:], 0.0).then_inc(sv.h, 1)
            v.wait_ge(sv.h, entry + 2)
            ins = None
            for h in range(NH):
                pb_ = (h % 2) * 64
                j = h // 2
                ins = v.tensor_copy(out=zbf[pb_:pb_ + 64, j, h:h + 1], in_=p.banks[1][pb_:pb_ + 64, j:j + 1])
            ins.then_inc(sv.h, 1)
        kvz_ev = sv.inc(3)
        g_B23 = [("sv", kvz_ev), ("sv", kvz_ev)]
        g_AB[1] = ("sv", kvz_ev)

        for ch in range(NCH):
            t0 = ch * CW

            @p.on("sync")
            def _(e, sqi=sqi, t0=t0, gx=g_xslot, pre=s_x.n, kv=kvz_ev):
                nm, val = gx
                if val:
                    e.wait_ge(semmap[nm].h, val)
                e.wait_ge(sv.h, kv)
                dmaw(e, s_x, pre, out=xf[:], in_=xr[:, :, sqi, t0:t0 + CW])
            xld = s_x.inc(16)

            @p.on("tensor")
            def _(t, t0=t0, xld=xld, kv=kvz_ev, gD=g_DEN):
                t.wait_ge(sv.h, kv)
                wgd(t, gD)
                ins = None
                for dt in range(CT):
                    ins = t.matmul(p.banks[4][0:16, 0:CW], zbf[:, dt, :],
                                   qfT[:, dt, t0:t0 + CW], start=(dt == 0), stop=(dt == CT - 1))
                ins.then_inc(st.h, 1)
            den_mm = st.inc()

            @p.on("vector")
            def _(v, dm=den_mm, entry=sv.n, rg=g_rden):
                v.wait_ge(st.h, dm)
                v.wait_ge(sv.h, entry)
                if rg:
                    v.wait_ge(st.h, rg)
                v.tensor_scalar(out=rdf[:], in0=p.banks[4][0:16, 0:CW], scalar1=1e-6,
                                scalar2=None, op0=ALU.max).then_inc(sv.h, 1)
                v.wait_ge(sv.h, entry + 1)
                v.reciprocal(out=rdf[:], in_=rdf[:]).then_inc(sv.h, 1)
                v.wait_ge(sv.h, entry + 2)
                v.tensor_copy(out=rden[:], in_=rdf[:]).then_inc(sv.h, 1)
            rden_done = sv.inc(3)
            g_DEN = ("sv", rden_done)

            for j in range(CT):
                bkY = 2 + (j % 2)

                @p.on("tensor")
                def _(t, j=j, t0=t0, bkY=bkY, rd=rden_done, gY=g_B23[j % 2], gBC=g_BC):
                    t.wait_ge(sv.h, rd)
                    wgd(t, gY)
                    wgd(t, gBC)
                    for hh in range(2):
                        h = 2 * j + hh
                        t.matmul(p.banks[bkY][hh * 64:hh * 64 + 64, 0:CW],
                                 kvbf[hh * 64:hh * 64 + 64, j, :],
                                 qfT[hh * 64:hh * 64 + 64, j, t0:t0 + CW],
                                 start=True, stop=True)
                    t.matmul(p.banks[5][:, 0:CW], e16[:, j, :, :].rearrange("p a b -> p (a b)"),
                             rden[:], start=True, stop=True).then_inc(st.h, 1)
                y_mm = st.inc()

                @p.on("vector")
                def _(v, j=j, bkY=bkY, ym=y_mm, entry=sv.n):
                    v.wait_ge(st.h, ym)
                    v.wait_ge(sv.h, entry)
                    v.tensor_copy(out=rbc[:], in_=p.banks[5][:, 0:CW]).then_inc(sv.h, 1)
                    v.wait_ge(sv.h, entry + 1)
                    v.tensor_tensor(out=yT[:, j, :], in0=p.banks[bkY][:, 0:CW], in1=rbc[:],
                                    op=ALU.mult).then_inc(sv.h, 1)
                done = sv.inc(2)
                g_B23[j % 2] = ("sv", done)
                g_BC = ("sv", done)
            y_all = sv.n
            g_rden = st.n

            @p.on("gpsimd")
            def _(e, gw=g_w, pre=s_w.n):
                nm, val = gw
                if val:
                    e.wait_ge(semmap[nm].h, val)
                dmaw(e, s_w, pre, out=wq[:], in_=outw.rearrange("(kt p) o -> p kt o", p=128))
            owld = s_w.inc(16)

            ao_done_v = 0
            for mt in range(CT):
                bk = mt % 2

                @p.on("tensor")
                def _(t, mt=mt, bk=bk, ya=y_all, wld=owld, gb=g_AB[bk]):
                    t.wait_ge(sv.h, ya)
                    t.wait_ge(s_w.h, wld)
                    wgd(t, gb)
                    ins = None
                    for kt in range(CT):
                        ins = t.matmul(p.banks[bk][:, 0:CW], wq[:, kt, mt * 128:(mt + 1) * 128],
                                       yT[:, kt, :], start=(kt == 0), stop=(kt == CT - 1))
                    ins.then_inc(st.h, 1)
                ao_mm = st.inc()
                g_w = ("st", ao_mm)

                @p.on("vector")
                def _(v, mt=mt, bk=bk, mmd=ao_mm, entry=sv.n, xl=xld):
                    v.wait_ge(st.h, mmd)
                    v.wait_ge(sv.h, entry)
                    v.wait_ge(s_x.h, xl)
                    v.tensor_scalar(out=xA[:, mt, :], in0=p.banks[bk][:, 0:CW],
                                    scalar1=pcol(OFF_OUTB, mt), scalar2=None,
                                    op0=ALU.add).then_inc(sv.h, 1)
                    v.wait_ge(sv.h, entry + 1)
                    v.tensor_tensor(out=xA[:, mt, :], in0=xA[:, mt, :], in1=xf[:, mt, :],
                                    op=ALU.add).then_inc(sv.h, 1)
                ao_done_v = sv.inc(2)
                g_AB[bk] = ("sv", ao_done_v)
            g_xslot = ("sv", ao_done_v)

            @p.on("scalar")
            def _(s, ad=ao_done_v):
                s.wait_ge(sv.h, ad)
                s.memzero(p.dums[:]).then_inc(ss.h, 1)
            xA_bar_s = ss.inc()

            r2 = ln_chunk(
                p, cst, CW,
                xf=lambda ct: xA[:, ct, :],
                xb=None,
                sq=lambda ct: sq[:, ct, :],
                tmp=lambda ct: tmpf[:, ct, :],
                rm=rm[:], rr=rr[:], sems=sems,
                bank_a=p.banks[4], bank_b=p.banks[5],
                out_bf=lambda ct: h2b[:, ct, :],
                gcol=lambda ct: pcol(OFF_G2, ct), bcol=lambda ct: pcol(OFF_BE2, ct),
                bank_guard=max(g_DEN[1], g_BC[1]), sq_guard=g_sq,
                x_dep=(ss.h, xA_bar_s),
                xb_scr=lambda ct: xbfc[:, ct, :],
                out_guard=[(st.h, g_h2b), (so.h, g_store)],
            )
            g_DEN = ("sv", r2["var"]); g_BC = ("sv", r2["var"])
            g_sq = r2["stats"]
            h2_done_s = r2["s"]

            for mt in range(32):
                bk = mt % 2

                @p.on("gpsimd")
                def _(e, mt=mt, gw=g_w, pre=s_w.n):
                    nm, val = gw
                    if val:
                        e.wait_ge(semmap[nm].h, val)
                    dmaw(e, s_w, pre, out=wq[:, 0:8, 0:128],
                         in_=w1.rearrange("(kt p) o -> p kt o", p=128)[:, :, mt * 128:(mt + 1) * 128])
                w1ld = s_w.inc(16)

                @p.on("tensor")
                def _(t, mt=mt, bk=bk, hd=h2_done_s, wld=w1ld, gb=g_AB[bk]):
                    t.wait_ge(ss.h, hd)
                    t.wait_ge(s_w.h, wld)
                    wgd(t, gb)
                    ins = None
                    for kt in range(CT):
                        ins = t.matmul(p.banks[bk][:, 0:CW], wq[:, kt, 0:128],
                                       h2b[:, kt, :], start=(kt == 0), stop=(kt == CT - 1))
                    ins.then_inc(st.h, 1)
                f1_mm = st.inc()
                g_w = ("st", f1_mm)

                @p.on("scalar")
                def _(s, mt=mt, bk=bk, mmd=f1_mm, entry=ss.n, fg=(g_ffb if mt == 0 else 0)):
                    s.wait_ge(st.h, mmd)
                    s.wait_ge(ss.h, entry)
                    if fg:
                        s.wait_ge(st.h, fg)
                    s.activation(out=ffb[:, mt, :], in_=p.banks[bk][:, 0:CW], func=AF.Gelu,
                                 bias=pcol(OFF_B1, mt)).then_inc(ss.h, 1)
                g_AB[bk] = ("ss", ss.inc())
            ff1_done = ss.n
            g_h2b = st.n

            for mt in range(CT):
                bk = mt % 2

                @p.on("gpsimd")
                def _(e, mt=mt, gw=g_w2, pre=s_w2.n, need=st.n):
                    nm, val = gw
                    if val:
                        e.wait_ge(semmap[nm].h, val)
                    dmaw(e, s_w2, pre, out=wse[:],
                         in_=w2.rearrange("(kt p) o -> p kt o", p=128)[:, :, mt * 128:(mt + 1) * 128])
                w2ld = s_w2.inc(16)

                @p.on("tensor")
                def _(t, mt=mt, bk=bk, fd=ff1_done, wld=w2ld, gb=g_AB[bk]):
                    t.wait_ge(ss.h, fd)
                    t.wait_ge(s_w2.h, wld)
                    wgd(t, gb)
                    ins = None
                    for kt in range(32):
                        ins = t.matmul(p.banks[bk][:, 0:CW], wse[:, kt, :],
                                       ffb[:, kt, :], start=(kt == 0), stop=(kt == 31))
                    ins.then_inc(st.h, 1)
                f2_mm = st.inc()
                g_w2 = ("st", f2_mm)

                @p.on("vector")
                def _(v, mt=mt, bk=bk, mmd=f2_mm, entry=sv.n, sg=g_store):
                    v.wait_ge(st.h, mmd)
                    v.wait_ge(sv.h, entry)
                    if mt == 0 and sg:
                        v.wait_ge(so.h, sg)
                    v.tensor_scalar(out=sff[:, mt, :], in0=p.banks[bk][:, 0:CW],
                                    scalar1=pcol(OFF_B2, mt), scalar2=None,
                                    op0=ALU.add).then_inc(sv.h, 1)
                    v.wait_ge(sv.h, entry + 1)
                    v.tensor_tensor(out=sff[:, mt, :], in0=sff[:, mt, :], in1=xA[:, mt, :],
                                    op=ALU.add).then_inc(sv.h, 1)
                g_AB[bk] = ("sv", sv.inc(2))
            s_done = sv.n

            @p.on("sync")
            def _(e, sqi=sqi, t0=t0, sd_=s_done, pre=so.n):
                e.wait_ge(sv.h, sd_)
                dmaw(e, so, pre, out=sr[:, :, sqi, t0:t0 + CW], in_=sff[:])
            g_store = so.inc(16)
            g_ffb = st.n
        g_qf = st.n

    @p.on("sync")
    def _(e, need=so.n):
        e.wait_ge(so.h, need)


def build_s4(nseq=S, ncph=None):
    p = Prog(); p.mkdum(); p.init_psum(6, 2)
    xnT = p.dram("xnT", [C, S, T], F32, kind="ExternalInput")
    qkvw = p.dram("qkvw", [C, 3072], F32, kind="ExternalInput")
    outw = p.dram("outw", [C, C], F32, kind="ExternalInput")
    w1 = p.dram("w1", [C, 4096], F32, kind="ExternalInput")
    w2 = p.dram("w2", [4096, C], F32, kind="ExternalInput")
    qkvb = p.dram("qkvb", [128, 24], F32, kind="ExternalInput")
    outb = p.dram("outb", [128, CT], F32, kind="ExternalInput")
    b1c = p.dram("b1c", [128, 32], F32, kind="ExternalInput")
    b2c = p.dram("b2c", [128, CT], F32, kind="ExternalInput")
    g1c = p.dram("g1c", [128, CT], F32, kind="ExternalInput")
    be1c = p.dram("be1c", [128, CT], F32, kind="ExternalInput")
    g2c = p.dram("g2c", [128, CT], F32, kind="ExternalInput")
    be2c = p.dram("be2c", [128, CT], F32, kind="ExternalInput")
    sT = p.dram("sT", [C, S, T], F32, kind="ExternalOutput")
    cst = emit_consts(p)
    with p.stage():
        emit_s4(p, cst, xnT, qkvw, outw, w1, w2, qkvb, outb, b1c, b2c, g1c, be1c, g2c, be2c,
                sT, nseq, ncph)
    return p.finish()


def emit_s1(p, cst, xnT, wk, wv, wr, wo, fwk, fwr, fwv, pvec, sT, out1T, s1T, cmT, nseq=S):
    xr = xnT.rearrange("(ct p) s t -> p ct s t", p=128)
    o1r = out1T.rearrange("(ct p) s t -> p ct s t", p=128)
    s1r = s1T.rearrange("(ct p) s t -> p ct s t", p=128)
    cmr = cmT.rearrange("(ct p) s t -> p ct s t", p=128)
    srr = sT.rearrange("(ct p) s t -> p ct s t", p=128)

    CW = 256
    NCH = T // CW
    TC = 256
    NLC = T // TC

    P_LN1S, P_LN1B, P_LN2S, P_LN2B, P_TMK, P_TMV, P_TMR, P_FTMK, P_FTMR, P_WDEC, P_UF = range(11)

    wbig = p.sbuf("wbig", [128, CT, 4096], BF16)
    fwrs = p.sbuf("fwrs", [128, CT, C], BF16)
    fwvs = p.sbuf("fwvs", [128, 32, 128], BF16)
    parms = p.sbuf("parms", [128, 11 * CT], F32)
    acol = p.sbuf("acol", [128, 2 * CT], F32)
    xnf = p.sbuf("xnf", [128, CT, CW], F32)
    xnb = p.sbuf("xnb", [128, CT, CW], BF16)
    sq = p.sbuf("sq", [128, CT, CW], BF16)
    h1f = p.sbuf("h1f", [128, CT, CW + 1], F32)
    scr_bf = p.sbuf("scr_bf", [128, 32, CW], BF16)
    shared = p.sbuf("shared", [128, CT, 5, CW + 1], F32)
    x2b = p.sbuf("x2b", [128, 2 * CT, CW], BF16)
    rm = p.sbuf("rm", [128, CW], F32)
    rr = p.sbuf("rr", [128, CW], F32)
    mst = p.sbuf("mst", [128, 32], F32)
    carr = p.sbuf("carr", [128, CT, 3], F32)

    ekf = lambda ct: shared[:, ct, 0, 0:CW]
    vvf = lambda ct: shared[:, ct, 1, 0:CW]
    bbf = lambda ct: shared[:, ct, 2, 0:CW]
    lifbuf = shared[:, :, 0:S, 0:TC]
    spk = scr_bf.rearrange("p (ct s) t -> p ct s t", ct=CT)
    gatef = lambda ct: shared[:, ct, 0, 0:CW]
    xAf = lambda ct: shared[:, ct, 1, 0:CW]

    s_x = p.sem("s_x")
    s_xb = p.sem("s_xb")
    s_w = p.sem("s_w")
    s_w2 = p.sem("s_w2")
    s_pp = p.sem("s_pp")
    so = p.sem("so")
    sv = p.sem("v")
    ss = p.sem("s")
    st = p.sem("t")
    sems = dict(v=sv, s=ss, t=st)
    semmap = dict(sv=sv, ss=ss, st=st)

    def wgd(t, g):
        nm, val = g
        if val:
            t.wait_ge(semmap[nm].h, val)

    pcol = lambda which, ct: parms[:, which * CT + ct:which * CT + ct + 1]
    a_col = lambda ct: acol[:, ct:ct + 1]
    eu_col = lambda ct: acol[:, CT + ct:CT + ct + 1]

    @p.on("sync")
    def _(e):
        dmaw(e, s_pp, 0, out=parms[:], in_=pvec)
    pld = s_pp.inc(16)

    @p.on("scalar")
    def _(s, pl=pld):
        s.wait_ge(s_pp.h, pl)
        s.activation(out=acol[:, 0:CT], in_=parms[:, P_WDEC * CT:(P_WDEC + 1) * CT], func=AF.Exp).then_inc(ss.h, 1)
        s.wait_ge(ss.h, 1)
        s.activation(out=acol[:, 0:CT], in_=acol[:, 0:CT], func=AF.Exp, scale=-1.0).then_inc(ss.h, 1)
        s.wait_ge(ss.h, 2)
        s.activation(out=acol[:, CT:2 * CT], in_=parms[:, P_UF * CT:(P_UF + 1) * CT], func=AF.Exp).then_inc(ss.h, 1)
    acol_done = ss.inc(3)

    @p.on("gpsimd")
    def _(e):
        dmaw(e, s_w, 0, out=wbig[:, :, 0:1024], in_=wk.rearrange("(kt p) o -> p kt o", p=128))
        dmaw(e, s_w, 16, out=wbig[:, :, 1024:2048], in_=wv.rearrange("(kt p) o -> p kt o", p=128))
        dmaw(e, s_w, 32, out=wbig[:, :, 2048:3072], in_=wr.rearrange("(kt p) o -> p kt o", p=128))
        dmaw(e, s_w, 48, out=wbig[:, :, 3072:4096], in_=wo.rearrange("(kt p) o -> p kt o", p=128))
    w1_ld = s_w.inc(64)

    g_AB = [("ss", 0), ("ss", 0)]
    g_DEN = ("sv", 0)
    g_BC = ("sv", 0)
    g_sq = 0
    g_xslot = ("sv", 0)
    g_store_o1 = 0
    g_scr = 0
    g_shared_t = 0
    WOFF = dict(k=0, v=1024, r=2048, o=3072)

    for sqi in range(nseq):
        for ch in range(NCH):
            t0 = ch * CW
            first = (ch == 0)

            @p.on("sync")
            def _(e, sqi=sqi, t0=t0, gx=g_xslot, pre=s_x.n, sg_=g_store_o1):
                nm, val = gx
                if val:
                    e.wait_ge(semmap[nm].h, val)
                if sg_:
                    e.wait_ge(so.h, sg_)
                dmaw(e, s_x, pre, out=xnf[:], in_=xr[:, :, sqi, t0:t0 + CW])
            xld = s_x.inc(16)

            @p.on("gpsimd")
            def _(e, sqi=sqi, t0=t0, need=g_sq, pre=s_xb.n):
                if need:
                    e.wait_ge(st.h, need)
                dmaw(e, s_xb, pre, out=xnb[:], in_=xr[:, :, sqi, t0:t0 + CW])
            xbld = s_xb.inc(16)

            @p.on("vector")
            def _(v, nx=xld, nb=xbld, ad=acol_done, first=first, entry=sv.n):
                v.wait_ge(s_x.h, nx)
                v.wait_ge(s_xb.h, nb)
                v.wait_ge(ss.h, ad)
                v.wait_ge(sv.h, entry)
                if first:
                    v.memset(carr[:], 0.0).then_inc(sv.h, 1)
                else:
                    v.tensor_copy(out=h1f[:, :, 0:1], in_=carr[:, :, 0:1]).then_inc(sv.h, 1)
            bar_v = sv.inc()

            @p.on("scalar")
            def _(s, nx=xld, nb=xbld):
                s.wait_ge(s_x.h, nx)
                s.wait_ge(s_xb.h, nb)
                s.memzero(p.dums[:]).then_inc(ss.h, 1)
            bar_s = ss.inc()

            if first:
                @p.on("vector")
                def _(v, e_=bar_v):
                    v.wait_ge(sv.h, e_)
                    v.memset(h1f[:, :, 0:1], 0.0).then_inc(sv.h, 1)
                sv.inc()

            r = ln_chunk(
                p, cst, CW,
                xf=lambda ct: xnf[:, ct, :],
                xb=lambda ct: xnb[:, ct, :],
                sq=lambda ct: sq[:, ct, :],
                tmp=lambda ct: h1f[:, ct, 1:CW + 1],
                rm=rm[:], rr=rr[:], sems=sems,
                bank_a=p.banks[4], bank_b=p.banks[5],
                out_f32=lambda ct: h1f[:, ct, 1:CW + 1],
                gcol=lambda ct: pcol(P_LN1S, ct), bcol=lambda ct: pcol(P_LN1B, ct),
                bank_guard=max(g_DEN[1], g_BC[1]), sq_guard=g_sq,
                x_dep=(ss.h, bar_s),
                out_guard=[(st.h, g_shared_t)],
            )
            g_DEN = ("sv", r["var"]); g_BC = ("sv", r["var"])
            g_sq = r["stats"]
            h1_done = r["s"]

            @p.on("vector")
            def _(v, hd=h1_done, entry=sv.n, gs=g_scr):
                v.wait_ge(ss.h, hd)
                v.wait_ge(sv.h, entry)
                if gs:
                    v.wait_ge(st.h, gs)
                v.tensor_copy(out=carr[:, :, 0:1], in_=h1f[:, :, CW:CW + 1]).then_inc(sv.h, 1)
                v.wait_ge(sv.h, entry + 1)
                ins = None
                for ct in range(CT):
                    ins = v.tensor_tensor(out=xnf[:, ct, :], in0=h1f[:, ct, 1:CW + 1],
                                          in1=h1f[:, ct, 0:CW], op=ALU.subtract)
                ins.then_inc(sv.h, 1)
                v.wait_ge(sv.h, entry + 2)
                ins = None
                for ct in range(CT):
                    ins = v.scalar_tensor_tensor(out=scr_bf[:, ct, :], in0=xnf[:, ct, :],
                                                 scalar=pcol(P_TMK, ct), in1=h1f[:, ct, 0:CW],
                                                 op0=ALU.mult, op1=ALU.add)
                ins.then_inc(sv.h, 1)
                v.wait_ge(sv.h, entry + 3)
                ins = None
                for ct in range(CT):
                    ins = v.scalar_tensor_tensor(out=scr_bf[:, 8 + ct, :], in0=xnf[:, ct, :],
                                                 scalar=pcol(P_TMV, ct), in1=h1f[:, ct, 0:CW],
                                                 op0=ALU.mult, op1=ALU.add)
                ins.then_inc(sv.h, 1)
                v.wait_ge(sv.h, entry + 4)
                ins = None
                for ct in range(CT):
                    ins = v.scalar_tensor_tensor(out=scr_bf[:, 16 + ct, :], in0=xnf[:, ct, :],
                                                 scalar=pcol(P_TMR, ct), in1=h1f[:, ct, 0:CW],
                                                 op0=ALU.mult, op1=ALU.add)
                ins.then_inc(sv.h, 1)
            mix_done = sv.inc(5)

            for wname, which, col0 in (("k", 0, WOFF["k"]), ("v", 1, WOFF["v"])):
                for mt in range(CT):
                    bk = mt % 2

                    @p.on("tensor")
                    def _(t, which=which, col0=col0, mt=mt, bk=bk, md=mix_done, wl=w1_ld, gb=g_AB[bk]):
                        t.wait_ge(sv.h, md)
                        t.wait_ge(s_w.h, wl)
                        wgd(t, gb)
                        ins = None
                        for kt in range(CT):
                            ins = t.matmul(p.banks[bk][:, 0:CW],
                                           wbig[:, kt, col0 + mt * 128:col0 + (mt + 1) * 128],
                                           scr_bf[:, which * 8 + kt, :],
                                           start=(kt == 0), stop=(kt == CT - 1))
                        ins.then_inc(st.h, 1)
                    mmd = st.inc()

                    @p.on("scalar")
                    def _(s, which=which, mt=mt, bk=bk, mmd=mmd, entry=ss.n):
                        s.wait_ge(st.h, mmd)
                        s.wait_ge(ss.h, entry)
                        if which == 0:
                            s.activation(out=ekf(mt), in_=p.banks[bk][:, 0:CW], func=AF.Exp).then_inc(ss.h, 1)
                        else:
                            s.activation(out=vvf(mt), in_=p.banks[bk][:, 0:CW], func=AF.Identity).then_inc(ss.h, 1)
                    g_AB[bk] = ("ss", ss.inc())
            kv_ev = ss.n

            @p.on("vector")
            def _(v, kv=kv_ev, entry=sv.n, first=first):
                v.wait_ge(ss.h, kv)
                v.wait_ge(sv.h, entry)
                if first:
                    v.memset(shared[:, :, 3:5, 0:1], 0.0)
                else:
                    v.tensor_copy(out=shared[:, :, 3, 0:1], in_=carr[:, :, 1:2])
                    v.tensor_copy(out=shared[:, :, 4, 0:1], in_=carr[:, :, 2:3])
                ins = None
                for ct in range(CT):
                    ins = v.tensor_tensor(out=bbf(ct), in0=ekf(ct), in1=vvf(ct), op=ALU.mult)
                ins.then_inc(sv.h, 1)
                v.wait_ge(sv.h, entry + 1)
                for ct in range(CT):
                    v.tensor_tensor_scan(out=shared[:, ct, 3, 1:CW + 1],
                                         data0=a_col(ct).broadcast_to([128, CW]),
                                         data1=bbf(ct),
                                         initial=(0.0 if first else carr[:, ct, 1:2]),
                                         op0=ALU.mult, op1=ALU.add)
                ins = None
                for ct in range(CT):
                    ins = v.tensor_tensor_scan(out=shared[:, ct, 4, 1:CW + 1],
                                               data0=a_col(ct).broadcast_to([128, CW]),
                                               data1=ekf(ct),
                                               initial=(0.0 if first else carr[:, ct, 2:3]),
                                               op0=ALU.mult, op1=ALU.add)
                ins.then_inc(sv.h, 1)
                v.wait_ge(sv.h, entry + 2)
                v.tensor_copy(out=carr[:, :, 1:2], in_=shared[:, :, 3, CW:CW + 1])
                ins = v.tensor_copy(out=carr[:, :, 2:3], in_=shared[:, :, 4, CW:CW + 1])
                ins.then_inc(sv.h, 1)
                v.wait_ge(sv.h, entry + 3)
                ins = None
                for ct in range(CT):
                    ins = v.tensor_scalar(out=ekf(ct), in0=ekf(ct), scalar1=eu_col(ct),
                                          scalar2=None, op0=ALU.mult)
                ins.then_inc(sv.h, 1)
                v.wait_ge(sv.h, entry + 4)
                ins = None
                for ct in range(CT):
                    ins = v.tensor_tensor(out=vvf(ct), in0=ekf(ct), in1=vvf(ct), op=ALU.mult)
                ins.then_inc(sv.h, 1)
                v.wait_ge(sv.h, entry + 5)
                ins = None
                for ct in range(CT):
                    ins = v.tensor_tensor(out=vvf(ct), in0=vvf(ct), in1=shared[:, ct, 3, 0:CW], op=ALU.add)
                ins.then_inc(sv.h, 1)
                v.wait_ge(sv.h, entry + 6)
                ins = None
                for ct in range(CT):
                    ins = v.tensor_tensor(out=ekf(ct), in0=ekf(ct), in1=shared[:, ct, 4, 0:CW], op=ALU.add)
                ins.then_inc(sv.h, 1)
                v.wait_ge(sv.h, entry + 7)
                ins = None
                for ct in range(CT):
                    ins = v.reciprocal(out=ekf(ct), in_=ekf(ct))
                ins.then_inc(sv.h, 1)
                v.wait_ge(sv.h, entry + 8)
                ins = None
                for ct in range(CT):
                    ins = v.tensor_tensor(out=vvf(ct), in0=vvf(ct), in1=ekf(ct), op=ALU.mult)
                ins.then_inc(sv.h, 1)
            wkv_done = sv.inc(9)

            for mt in range(CT):
                bk = mt % 2

                @p.on("tensor")
                def _(t, mt=mt, bk=bk, wl=w1_ld, gb=g_AB[bk], wd=wkv_done):
                    t.wait_ge(sv.h, wd)
                    t.wait_ge(s_w.h, wl)
                    wgd(t, gb)
                    ins = None
                    for kt in range(CT):
                        ins = t.matmul(p.banks[bk][:, 0:CW],
                                       wbig[:, kt, WOFF["r"] + mt * 128:WOFF["r"] + (mt + 1) * 128],
                                       scr_bf[:, 16 + kt, :],
                                       start=(kt == 0), stop=(kt == CT - 1))
                    ins.then_inc(st.h, 1)
                mmd = st.inc()

                @p.on("scalar")
                def _(s, mt=mt, bk=bk, mmd=mmd, entry=ss.n):
                    s.wait_ge(st.h, mmd)
                    s.wait_ge(ss.h, entry)
                    s.activation(out=bbf(mt), in_=p.banks[bk][:, 0:CW], func=AF.Sigmoid).then_inc(ss.h, 1)
                g_AB[bk] = ("ss", ss.inc())
            r_ev = ss.n

            @p.on("vector")
            def _(v, re_=r_ev, entry=sv.n, gs=g_scr):
                v.wait_ge(ss.h, re_)
                v.wait_ge(sv.h, entry)
                ins = None
                for ct in range(CT):
                    ins = v.tensor_tensor(out=scr_bf[:, 24 + ct, :], in0=vvf(ct), in1=bbf(ct), op=ALU.mult)
                ins.then_inc(sv.h, 1)
            ry_done = sv.inc()

            o1_ev = 0
            for mt in range(CT):
                bk = mt % 2

                @p.on("tensor")
                def _(t, mt=mt, bk=bk, rd=ry_done, wl=w1_ld, gb=g_AB[bk]):
                    t.wait_ge(sv.h, rd)
                    t.wait_ge(s_w.h, wl)
                    wgd(t, gb)
                    ins = None
                    for kt in range(CT):
                        ins = t.matmul(p.banks[bk][:, 0:CW],
                                       wbig[:, kt, WOFF["o"] + mt * 128:WOFF["o"] + (mt + 1) * 128],
                                       scr_bf[:, 24 + kt, :],
                                       start=(kt == 0), stop=(kt == CT - 1))
                    ins.then_inc(st.h, 1)
                mmd = st.inc()

                @p.on("vector")
                def _(v, mt=mt, bk=bk, mmd=mmd, entry=sv.n, og=g_store_o1):
                    v.wait_ge(st.h, mmd)
                    v.wait_ge(sv.h, entry)
                    if mt == 0 and og:
                        v.wait_ge(so.h, og)
                    v.tensor_copy(out=xnf[:, mt, :], in_=p.banks[bk][:, 0:CW]).then_inc(sv.h, 1)
                o1_ev = sv.inc()
                g_AB[bk] = ("sv", o1_ev)

            @p.on("sync")
            def _(e, sqi=sqi, t0=t0, od=o1_ev, pre=so.n):
                e.wait_ge(sv.h, od)
                dmaw(e, so, pre, out=o1r[:, :, sqi, t0:t0 + CW], in_=xnf[:])
            g_store_o1 = so.inc(16)
            g_xslot = ("sv", o1_ev)
            g_scr = st.n
            g_shared_t = st.n

    def lif_phase(src_r, dst_r, tag, src_stored):
        lsem = p.sem(f"lb_{tag}")
        g_lb = 0
        g_spk = 0

        @p.on("vector")
        def _(v, entry=sv.n):
            v.wait_ge(sv.h, entry)
            v.memset(mst[:], 0.0).then_inc(sv.h, 1)
        mz_done = sv.inc()

        for lc in range(NLC):
            t0 = lc * TC

            @p.on("sync")
            def _(e, t0=t0, need=g_lb, pre=lsem.n, mz=mz_done, sst=src_stored):
                if need:
                    e.wait_ge(sv.h, need)
                e.wait_ge(sv.h, mz)
                e.wait_ge(so.h, sst)
                dmaw4(e, lsem, pre, lifbuf[:, :, 0:nseq, :], src_r[:, :, 0:nseq, t0:t0 + TC], nseq)
            ld = lsem.inc(16 * nseq)

            @p.on("vector")
            def _(v, ld=ld, entry=sv.n, gs=g_spk, nseq=nseq):
                v.wait_ge(lsem.h, ld)
                v.wait_ge(sv.h, entry)
                ins = v.tensor_scalar(out=lifbuf[:, :, 0:nseq, :], in0=lifbuf[:, :, 0:nseq, :],
                                      scalar1=0.5, scalar2=None, op0=ALU.mult)
                ins.then_inc(sv.h, 1)
                c = entry + 1
                for tt in range(TC):
                    v.wait_ge(sv.h, c)
                    v.scalar_tensor_tensor(out=lifbuf[:, :, 0:nseq, tt],
                                           in0=mst[:].rearrange("p (ct s) -> p ct s", ct=CT)[:, :, 0:nseq],
                                           scalar=0.5, in1=lifbuf[:, :, 0:nseq, tt],
                                           op0=ALU.mult, op1=ALU.add).then_inc(sv.h, 1)
                    c += 1
                    v.wait_ge(sv.h, c)
                    v.scalar_tensor_tensor(out=mst[:].rearrange("p (ct s) -> p ct s", ct=CT)[:, :, 0:nseq],
                                           in0=lifbuf[:, :, 0:nseq, tt], scalar=1.0,
                                           in1=lifbuf[:, :, 0:nseq, tt],
                                           op0=ALU.is_lt, op1=ALU.mult).then_inc(sv.h, 1)
                    c += 1
                v.wait_ge(sv.h, c)
                if gs:
                    v.wait_ge(so.h, gs)
                v.tensor_scalar(out=spk[:, :, 0:nseq, :], in0=lifbuf[:, :, 0:nseq, :],
                                scalar1=1.0, scalar2=None, op0=ALU.is_ge).then_inc(sv.h, 1)
            done = sv.inc(2 + 2 * TC)
            g_lb = done

            @p.on("sync")
            def _(e, t0=t0, dn=done, pre=so.n):
                e.wait_ge(sv.h, dn)
                dmaw4(e, so, pre, dst_r[:, :, 0:nseq, t0:t0 + TC], spk[:, :, 0:nseq, :], nseq)
            g_spk = so.inc(16 * nseq)
        return g_spk

    s1_stored = lif_phase(o1r, s1r, "l1", g_store_o1)

    @p.on("gpsimd")
    def _(e, gw=st.n, pre=s_w.n):
        e.wait_ge(st.h, gw)
        dmaw(e, s_w, pre, out=wbig[:], in_=fwk.rearrange("(kt p) o -> p kt o", p=128))
        dmaw(e, s_w, pre + 16, out=fwrs[:], in_=fwr.rearrange("(kt p) o -> p kt o", p=128))
    w3_ld = s_w.inc(32)

    g_x2b = 0
    g_kk = 0
    g_store_cm = 0
    for sqi in range(nseq):
        for ch in range(NCH):
            t0 = ch * CW
            first = (ch == 0)

            @p.on("sync")
            def _(e, sqi=sqi, t0=t0, gx=g_xslot, pre=s_x.n, sp=max(s1_stored, g_store_cm)):
                nm, val = gx
                if val:
                    e.wait_ge(semmap[nm].h, val)
                e.wait_ge(so.h, sp)
                dmaw(e, s_x, pre, out=xnf[:], in_=xr[:, :, sqi, t0:t0 + CW])
            xld = s_x.inc(16)

            @p.on("gpsimd")
            def _(e, sqi=sqi, t0=t0, need=g_sq, pre=s_xb.n, sp=s1_stored):
                if need:
                    e.wait_ge(st.h, need)
                e.wait_ge(so.h, sp)
                dmaw(e, s_xb, pre, out=sq[:], in_=s1r[:, :, sqi, t0:t0 + CW])
            sld = s_xb.inc(16)

            @p.on("vector")
            def _(v, nx=xld, ns=sld, entry=sv.n, gsh=g_shared_t, first=first):
                v.wait_ge(s_x.h, nx)
                v.wait_ge(s_xb.h, ns)
                v.wait_ge(sv.h, entry)
                if gsh:
                    v.wait_ge(st.h, gsh)
                ins = None
                for ct in range(CT):
                    ins = v.tensor_tensor(out=xAf(ct), in0=xnf[:, ct, :], in1=sq[:, ct, :], op=ALU.add)
                ins.then_inc(sv.h, 1)
                v.wait_ge(sv.h, entry + 1)
                ins = None
                for ct in range(CT):
                    ins = v.tensor_copy(out=xnb[:, ct, :], in_=xAf(ct))
                ins.then_inc(sv.h, 1)
                v.wait_ge(sv.h, entry + 2)
                if first:
                    v.memset(h1f[:, :, 0:1], 0.0).then_inc(sv.h, 1)
                else:
                    v.tensor_copy(out=h1f[:, :, 0:1], in_=carr[:, :, 0:1]).then_inc(sv.h, 1)
            xa_done = sv.inc(3)

            @p.on("scalar")
            def _(s, xa=xa_done):
                s.wait_ge(sv.h, xa)
                s.memzero(p.dums[:]).then_inc(ss.h, 1)
            bar_s = ss.inc()

            r3 = ln_chunk(
                p, cst, CW,
                xf=lambda ct: xAf(ct),
                xb=lambda ct: xnb[:, ct, :],
                sq=lambda ct: sq[:, ct, :],
                tmp=lambda ct: h1f[:, ct, 1:CW + 1],
                rm=rm[:], rr=rr[:], sems=sems,
                bank_a=p.banks[4], bank_b=p.banks[5],
                out_f32=lambda ct: h1f[:, ct, 1:CW + 1],
                gcol=lambda ct: pcol(P_LN2S, ct), bcol=lambda ct: pcol(P_LN2B, ct),
                bank_guard=max(g_DEN[1], g_BC[1]), sq_guard=g_sq,
                x_dep=(ss.h, bar_s),
            )
            g_DEN = ("sv", r3["var"]); g_BC = ("sv", r3["var"])
            g_sq = r3["stats"]
            h2_done = r3["s"]

            @p.on("vector")
            def _(v, hd=h2_done, entry=sv.n, gx2=g_x2b):
                v.wait_ge(ss.h, hd)
                v.wait_ge(sv.h, entry)
                if gx2:
                    v.wait_ge(st.h, gx2)
                v.tensor_copy(out=carr[:, :, 0:1], in_=h1f[:, :, CW:CW + 1]).then_inc(sv.h, 1)
                v.wait_ge(sv.h, entry + 1)
                ins = None
                for ct in range(CT):
                    ins = v.tensor_tensor(out=xnf[:, ct, :], in0=h1f[:, ct, 1:CW + 1],
                                          in1=h1f[:, ct, 0:CW], op=ALU.subtract)
                ins.then_inc(sv.h, 1)
                v.wait_ge(sv.h, entry + 2)
                ins = None
                for ct in range(CT):
                    ins = v.scalar_tensor_tensor(out=x2b[:, ct, :], in0=xnf[:, ct, :],
                                                 scalar=pcol(P_FTMK, ct), in1=h1f[:, ct, 0:CW],
                                                 op0=ALU.mult, op1=ALU.add)
                ins.then_inc(sv.h, 1)
                v.wait_ge(sv.h, entry + 3)
                ins = None
                for ct in range(CT):
                    ins = v.scalar_tensor_tensor(out=x2b[:, 8 + ct, :], in0=xnf[:, ct, :],
                                                 scalar=pcol(P_FTMR, ct), in1=h1f[:, ct, 0:CW],
                                                 op0=ALU.mult, op1=ALU.add)
                ins.then_inc(sv.h, 1)
            mix2_done = sv.inc(4)

            for mt in range(32):
                bk = mt % 2

                @p.on("tensor")
                def _(t, mt=mt, bk=bk, md=mix2_done, wl=w3_ld, gb=g_AB[bk]):
                    t.wait_ge(sv.h, md)
                    t.wait_ge(s_w.h, wl)
                    wgd(t, gb)
                    ins = None
                    for kt in range(CT):
                        ins = t.matmul(p.banks[bk][:, 0:CW],
                                       wbig[:, kt, mt * 128:(mt + 1) * 128],
                                       x2b[:, kt, :], start=(kt == 0), stop=(kt == CT - 1))
                    ins.then_inc(st.h, 1)
                mmd = st.inc()

                @p.on("scalar")
                def _(s, mt=mt, bk=bk, mmd=mmd, entry=ss.n, gk=(g_kk if mt == 0 else 0)):
                    s.wait_ge(st.h, mmd)
                    s.wait_ge(ss.h, entry)
                    if gk:
                        s.wait_ge(st.h, gk)
                    s.activation(out=sq[:, mt % CT, :], in_=p.banks[bk][:, 0:CW],
                                 func=AF.Relu).then_inc(ss.h, 1)
                    s.wait_ge(ss.h, entry + 1)
                    s.activation(out=scr_bf[:, mt, :], in_=sq[:, mt % CT, :], func=AF.Square).then_inc(ss.h, 1)
                g_AB[bk] = ("ss", ss.inc(2))
            kk_done = ss.n

            for mt in range(CT):
                bk = mt % 2

                @p.on("tensor")
                def _(t, mt=mt, bk=bk, md=mix2_done, wl=w3_ld, gb=g_AB[bk]):
                    t.wait_ge(sv.h, md)
                    t.wait_ge(s_w.h, wl)
                    wgd(t, gb)
                    ins = None
                    for kt in range(CT):
                        ins = t.matmul(p.banks[bk][:, 0:CW],
                                       fwrs[:, kt, mt * 128:(mt + 1) * 128],
                                       x2b[:, 8 + kt, :], start=(kt == 0), stop=(kt == CT - 1))
                    ins.then_inc(st.h, 1)
                mmd = st.inc()

                @p.on("scalar")
                def _(s, mt=mt, bk=bk, mmd=mmd, entry=ss.n):
                    s.wait_ge(st.h, mmd)
                    s.wait_ge(ss.h, entry)
                    s.activation(out=gatef(mt), in_=p.banks[bk][:, 0:CW], func=AF.Sigmoid).then_inc(ss.h, 1)
                g_AB[bk] = ("ss", ss.inc())
            gate_done = ss.n

            cm_ev = 0
            for mt in range(CT):
                bk = mt % 2

                @p.on("gpsimd")
                def _(e, mt=mt, pre=s_w2.n, need=st.n):
                    e.wait_ge(st.h, need)
                    dmaw(e, s_w2, pre, out=fwvs[:],
                         in_=fwv.rearrange("(kt p) o -> p kt o", p=128)[:, :, mt * 128:(mt + 1) * 128])
                wvld = s_w2.inc(16)

                @p.on("tensor")
                def _(t, mt=mt, bk=bk, kd=kk_done, wl=wvld, gb=g_AB[bk]):
                    t.wait_ge(ss.h, kd)
                    t.wait_ge(s_w2.h, wl)
                    wgd(t, gb)
                    ins = None
                    for kt in range(32):
                        ins = t.matmul(p.banks[bk][:, 0:CW], fwvs[:, kt, :],
                                       scr_bf[:, kt, :], start=(kt == 0), stop=(kt == 31))
                    ins.then_inc(st.h, 1)
                mmd = st.inc()

                @p.on("vector")
                def _(v, mt=mt, bk=bk, mmd=mmd, entry=sv.n, gd=gate_done, og=g_store_cm):
                    v.wait_ge(st.h, mmd)
                    v.wait_ge(sv.h, entry)
                    v.wait_ge(ss.h, gd)
                    if mt == 0 and og:
                        v.wait_ge(so.h, og)
                    v.tensor_tensor(out=xnf[:, mt, :], in0=p.banks[bk][:, 0:CW],
                                    in1=gatef(mt), op=ALU.mult).then_inc(sv.h, 1)
                cm_ev = sv.inc()
                g_AB[bk] = ("sv", cm_ev)

            @p.on("sync")
            def _(e, sqi=sqi, t0=t0, cd=cm_ev, pre=so.n):
                e.wait_ge(sv.h, cd)
                dmaw(e, so, pre, out=cmr[:, :, sqi, t0:t0 + CW], in_=xnf[:])
            g_store_cm = so.inc(16)
            g_xslot = ("sv", cm_ev)
            g_x2b = st.n
            g_kk = st.n
            g_shared_t = st.n

    lsem4 = p.sem("lb4")
    g_lb4 = 0
    g_out = 0

    @p.on("vector")
    def _(v, entry=sv.n):
        v.wait_ge(sv.h, entry)
        v.memset(mst[:], 0.0).then_inc(sv.h, 1)
    mz4 = sv.inc()

    for lc in range(NLC):
        t0 = lc * TC

        @p.on("sync")
        def _(e, t0=t0, need=g_lb4, pre=lsem4.n, mz=mz4, cs=max(g_store_cm, g_out)):
            if need:
                e.wait_ge(sv.h, need)
            e.wait_ge(sv.h, mz)
            e.wait_ge(so.h, cs)
            dmaw4(e, lsem4, pre, lifbuf[:, :, 0:nseq, :], cmr[:, :, 0:nseq, t0:t0 + TC], nseq)
        ld = lsem4.inc(16 * nseq)

        @p.on("vector")
        def _(v, ld=ld, entry=sv.n, nseq=nseq):
            v.wait_ge(lsem4.h, ld)
            v.wait_ge(sv.h, entry)
            ins = v.tensor_scalar(out=lifbuf[:, :, 0:nseq, :], in0=lifbuf[:, :, 0:nseq, :],
                                  scalar1=0.5, scalar2=None, op0=ALU.mult)
            ins.then_inc(sv.h, 1)
            c = entry + 1
            for tt in range(TC):
                v.wait_ge(sv.h, c)
                v.scalar_tensor_tensor(out=lifbuf[:, :, 0:nseq, tt],
                                       in0=mst[:].rearrange("p (ct s) -> p ct s", ct=CT)[:, :, 0:nseq],
                                       scalar=0.5, in1=lifbuf[:, :, 0:nseq, tt],
                                       op0=ALU.mult, op1=ALU.add).then_inc(sv.h, 1)
                c += 1
                v.wait_ge(sv.h, c)
                v.scalar_tensor_tensor(out=mst[:].rearrange("p (ct s) -> p ct s", ct=CT)[:, :, 0:nseq],
                                       in0=lifbuf[:, :, 0:nseq, tt], scalar=1.0,
                                       in1=lifbuf[:, :, 0:nseq, tt],
                                       op0=ALU.is_lt, op1=ALU.mult).then_inc(sv.h, 1)
                c += 1
            v.wait_ge(sv.h, c)
            v.tensor_scalar(out=spk[:, :, 0:nseq, :], in0=lifbuf[:, :, 0:nseq, :],
                            scalar1=1.0, scalar2=None, op0=ALU.is_ge).then_inc(sv.h, 1)
        lif_done = sv.inc(2 + 2 * TC)

        @p.on("sync")
        def _(e, t0=t0, dn=lif_done, pre=s_x.n):
            e.wait_ge(sv.h, dn)
            dmaw4(e, s_x, pre, lifbuf[:, :, 0:nseq, :], xr[:, :, 0:nseq, t0:t0 + TC], nseq)
        xld = s_x.inc(16 * nseq)

        @p.on("vector")
        def _(v, ld=xld, entry=sv.n):
            v.wait_ge(s_x.h, ld)
            v.wait_ge(sv.h, entry)
            v.tensor_tensor(out=lifbuf[:, :, 0:nseq, :], in0=lifbuf[:, :, 0:nseq, :],
                            in1=spk[:, :, 0:nseq, :], op=ALU.add).then_inc(sv.h, 1)
        c1_done = sv.inc()

        @p.on("gpsimd")
        def _(e, t0=t0, dn=c1_done, pre=s_xb.n):
            e.wait_ge(sv.h, dn)
            dmaw4(e, s_xb, pre, spk[:, :, 0:nseq, :], s1r[:, :, 0:nseq, t0:t0 + TC], nseq)
        s1ld = s_xb.inc(16 * nseq)

        @p.on("vector")
        def _(v, ld=s1ld, entry=sv.n, og=g_out):
            v.wait_ge(s_xb.h, ld)
            v.wait_ge(sv.h, entry)
            if og:
                v.wait_ge(so.h, og)
            v.tensor_tensor(out=lifbuf[:, :, 0:nseq, :], in0=lifbuf[:, :, 0:nseq, :],
                            in1=spk[:, :, 0:nseq, :], op=ALU.add).then_inc(sv.h, 1)
        s_done = sv.inc()
        g_lb4 = s_done

        @p.on("sync")
        def _(e, t0=t0, sd_=s_done, pre=so.n):
            e.wait_ge(sv.h, sd_)
            dmaw4(e, so, pre, srr[:, :, 0:nseq, t0:t0 + TC], lifbuf[:, :, 0:nseq, :], nseq)
        g_out = so.inc(16 * nseq)

    @p.on("sync")
    def _(e, need=so.n):
        e.wait_ge(so.h, need)


def build_s0(barrier=False, strong=()):
    p = Prog(); p.mkdum(); p.init_psum(6, 2)
    xT = p.dram("xT", [C, S, T], F32, kind="ExternalInput")
    xnT = p.dram("xnT", [C, S, T], F32, kind="ExternalOutput")
    cst = emit_consts(p)
    with p.stage():
        emit_s0(p, cst, xT, xnT, barrier=barrier, strong=strong)
    return p.finish()



def to_cm(x):
    return np.ascontiguousarray(np.transpose(x, (2, 0, 1)))

def from_cm(xT):
    return np.transpose(xT, (1, 2, 0))


_PROGS = {}

def _prog(name, builder):
    if name not in _PROGS:
        _PROGS[name] = builder()
    return _PROGS[name]


def _col(v, n=CT):
    return np.ascontiguousarray(np.asarray(v).reshape(n, 128).T)


def build_mega():
    p = Prog(); p.mkdum(); p.init_psum(6, 2)
    F = F32
    xT = p.dram("xT", [C, S, T], F, kind="ExternalInput")
    wk = p.dram("wk", [C, C], F, kind="ExternalInput")
    wv = p.dram("wv", [C, C], F, kind="ExternalInput")
    wr = p.dram("wr", [C, C], F, kind="ExternalInput")
    wo = p.dram("wo", [C, C], F, kind="ExternalInput")
    fwk = p.dram("fwk", [C, 4096], F, kind="ExternalInput")
    fwr = p.dram("fwr", [C, C], F, kind="ExternalInput")
    fwv = p.dram("fwv", [4096, C], F, kind="ExternalInput")
    pvec = p.dram("pvec", [128, 11 * CT], F, kind="ExternalInput")
    qw = p.dram("qw", [6, C, C], I32, kind="ExternalInput")
    qsc = p.dram("qsc", [128, 6, CT, 64], F, kind="ExternalInput")
    qb = p.dram("qb", [128, 6, CT], F, kind="ExternalInput")
    la = p.dram("la", [6, 32, C], F, kind="ExternalInput")
    lb = p.dram("lb", [6, C, 32], F, kind="ExternalInput")
    lng = p.dram("lng", [128, 3, CT], F, kind="ExternalInput")
    lnb = p.dram("lnb", [128, 3, CT], F, kind="ExternalInput")
    qkvw = p.dram("qkvw", [C, 3072], F, kind="ExternalInput")
    outw = p.dram("outw", [C, C], F, kind="ExternalInput")
    w1 = p.dram("w1", [C, 4096], F, kind="ExternalInput")
    w2 = p.dram("w2", [4096, C], F, kind="ExternalInput")
    qkvb = p.dram("qkvb", [128, 24], F, kind="ExternalInput")
    outb = p.dram("outb", [128, CT], F, kind="ExternalInput")
    b1c = p.dram("b1c", [128, 32], F, kind="ExternalInput")
    b2c = p.dram("b2c", [128, CT], F, kind="ExternalInput")
    g1c = p.dram("g1c", [128, CT], F, kind="ExternalInput")
    be1c = p.dram("be1c", [128, CT], F, kind="ExternalInput")
    g2c = p.dram("g2c", [128, CT], F, kind="ExternalInput")
    be2c = p.dram("be2c", [128, CT], F, kind="ExternalInput")
    wgt = p.dram("wgt", [9, C, 2048], F, kind="ExternalInput")
    wut = p.dram("wut", [9, C, 2048], F, kind="ExternalInput")
    wdt = p.dram("wdt", [9, 2048, C], F, kind="ExternalInput")
    wrt = p.dram("wrt", [C, 8], F, kind="ExternalInput")
    rb = p.dram("rb", [8, 1], F, kind="ExternalInput")
    outT = p.dram("outT", [C, S, T], F, kind="ExternalOutput")
    xn0T = p.dram("xn0T", [C, S, T], F, kind="Internal")
    s0T = p.dram("s0T", [C, S, T], F, kind="Internal")
    g0T = p.dram("g0T", [C, S, T], F, kind="Internal")
    hA = p.dram("hA", [C, S, T], F, kind="Internal")
    x1T = p.dram("x1T", [C, S, T], F, kind="Internal")
    xn1T = p.dram("xn1T", [C, S, T], F, kind="Internal")
    s1aT = p.dram("s1aT", [C, S, T], F, kind="Internal")
    g1T = p.dram("g1T", [C, S, T], F, kind="Internal")
    o1i = p.dram("o1i", [C, S, T], F, kind="Internal")
    s1i = p.dram("s1i", [C, S, T], BF16, kind="Internal")
    cmi = p.dram("cmi", [C, S, T], F, kind="Internal")

    cst = emit_consts(p)
    with p.stage():
        emit_s0(p, cst, xT, xn0T)
    p.barrier()
    with p.stage():
        emit_s1(p, cst, xn0T, wk, wv, wr, wo, fwk, fwr, fwv, pvec, s0T, o1i, s1i, cmi)
    p.barrier()
    with p.stage():
        emit_s2(p, cst, xn0T, qw, qsc, qb, la, lb, lng, lnb, g0T, hA)
    p.barrier()
    with p.stage():
        emit_comb(p, cst, xT, s0T, g0T, x1T, xn1T)
    p.barrier()
    with p.stage():
        emit_s4(p, cst, xn1T, qkvw, outw, w1, w2, qkvb, outb, b1c, b2c, g1c, be1c, g2c, be2c, s1aT)
    p.barrier()
    with p.stage():
        emit_s5(p, cst, xn1T, wgt, wut, wdt, wrt, rb, g1T)
    p.barrier()
    with p.stage():
        emit_comb(p, cst, x1T, s1aT, g1T, outT, None)
    return p.finish()


def _host_inputs(z):
    m = {}
    for nm, key in [("wk", "L0_attn_Wk"), ("wv", "L0_attn_Wv"), ("wr", "L0_attn_Wr"),
                    ("wo", "L0_attn_Wo"), ("fwk", "L0_attn_fWk"), ("fwr", "L0_attn_fWr"),
                    ("fwv", "L0_attn_fWv"), ("qkvw", "L1_attn_qkv_w"), ("outw", "L1_attn_out_w"),
                    ("w1", "L1_attn_w1"), ("w2", "L1_attn_w2")]:
        m[nm] = np.ascontiguousarray(np.asarray(z[key]))
    m["pvec"] = np.ascontiguousarray(np.concatenate([_col(z[k]) for k in [
        "L0_attn_ln1_s", "L0_attn_ln1_b", "L0_attn_ln2_s", "L0_attn_ln2_b",
        "L0_attn_tmk", "L0_attn_tmv", "L0_attn_tmr", "L0_attn_f_tmk", "L0_attn_f_tmr",
        "L0_attn_w_decay", "L0_attn_u_first"]], axis=1))
    m["qw"] = np.ascontiguousarray(np.asarray(z["L0_ffn_qweights"]))
    m["qsc"] = np.ascontiguousarray(np.asarray(z["L0_ffn_scales"]).reshape(6, 8, 128, 64).transpose(2, 0, 1, 3))
    m["qb"] = np.ascontiguousarray(np.asarray(z["L0_ffn_biases"]).reshape(6, 8, 128).transpose(2, 0, 1))
    m["la"] = np.ascontiguousarray(np.asarray(z["L0_ffn_lora_a"]))
    m["lb"] = np.ascontiguousarray(np.asarray(z["L0_ffn_lora_b"]))
    m["lng"] = np.ascontiguousarray(np.asarray(z["L0_ffn_ln_gamma"]).reshape(3, 8, 128).transpose(2, 0, 1))
    m["lnb"] = np.ascontiguousarray(np.asarray(z["L0_ffn_ln_beta"]).reshape(3, 8, 128).transpose(2, 0, 1))
    m["qkvb"] = _col(z["L1_attn_qkv_b"], 24)
    m["outb"] = _col(z["L1_attn_out_b"])
    m["b1c"] = _col(z["L1_attn_b1"], 32)
    m["b2c"] = _col(z["L1_attn_b2"])
    m["g1c"] = _col(z["L1_attn_g1"])
    m["be1c"] = _col(z["L1_attn_be1"])
    m["g2c"] = _col(z["L1_attn_g2"])
    m["be2c"] = _col(z["L1_attn_be2"])
    wg = np.concatenate([np.asarray(z["L1_ffn_Wg_s"])[None], np.asarray(z["L1_ffn_Wg"])], 0)
    wu = np.concatenate([np.asarray(z["L1_ffn_Wu_s"])[None], np.asarray(z["L1_ffn_Wu"])], 0)
    wd = np.concatenate([np.asarray(z["L1_ffn_Wd_s"])[None], np.asarray(z["L1_ffn_Wd"])], 0)
    m["wgt"] = np.ascontiguousarray(wg.transpose(0, 2, 1))
    m["wut"] = np.ascontiguousarray(wu.transpose(0, 2, 1))
    m["wdt"] = np.ascontiguousarray(wd.transpose(0, 2, 1))
    m["wrt"] = np.ascontiguousarray(np.asarray(z["L1_ffn_Wr"]).T)
    m["rb"] = np.ascontiguousarray(np.asarray(z["L1_ffn_rb"]).reshape(8, 1))
    return m


def kernel(**inputs):
    x = np.asarray(inputs["L0_attn_x"])
    base = _host_inputs(inputs)
    nc = _prog("mega", build_mega)
    in_maps = [dict(base, xT=to_cm(x[c * S:(c + 1) * S])) for c in range(NCORE)]
    res = run_bass_kernel_spmd(nc, in_maps, core_ids=list(range(NCORE)))
    outs = [from_cm(np.asarray(rc["outT"])) for rc in res.results]
    return np.ascontiguousarray(np.concatenate(outs, axis=0))


# version 04 (final)
# speedup vs baseline: 1.1314x; 1.1314x over previous
import numpy as np
import ml_dtypes
from contextlib import ExitStack
import concourse.bass as bass
import concourse.mybir as mybir
from concourse.bass_utils import run_bass_kernel_spmd

F32 = mybir.dt.float32
BF16 = mybir.dt.bfloat16
I32 = mybir.dt.int32
AF = mybir.ActivationFunctionType
ALU = mybir.AluOpType
AXX = mybir.AxisListType.X

NCORE = 8
B, T, C = 32, 2048, 1024
S = B // NCORE
TOK = S * T
CT = C // 128
ENGINES = ("sync", "gpsimd", "vector", "scalar", "tensor")

BF_NP = ml_dtypes.bfloat16


class Sem:
    __slots__ = ("h", "n")
    def __init__(self, h):
        self.h = h
        self.n = 0
    def inc(self, k=1):
        self.n += k
        return self.n


def dmaw4(e, sem, pre, out4, in4, nseq):
    for si in range(nseq):
        if pre + si * 16:
            e.wait_ge(sem.h, pre + si * 16)
        e.dma_start(out=out4[:, :, si, :], in_=in4[:, :, si, :]).then_inc(sem.h, 16)


def dmaw(e, sem, pre, out, in_):
    if pre:
        e.wait_ge(sem.h, pre)
    return e.dma_start(out=out, in_=in_).then_inc(sem.h, 16)


class Prog:
    def __init__(self):
        self.nc = bass.Bass("TRN2", target_bir_lowering=False)
        self.ctx = ExitStack()
        self.cur = self.ctx
        self.q = {e: [] for e in ENGINES}
        self._si = 0
        self._ti = 0
        self.banks = None
        self.bar = None
        self.barn = 0

    def stage(self):
        from contextlib import contextmanager

        @contextmanager
        def _scope():
            sub = ExitStack()
            old = self.cur
            self.cur = sub
            try:
                yield
            finally:
                self.cur = old
                sub.close()
        return _scope()

    def barrier(self):
        if self.bar is None:
            self.bar = self.sem("gbar")
        tgt = self.bar.inc(5)

        @self.on("vector")
        def _(v, t=tgt):
            v.memset(self.dumv[:], 0.0).then_inc(self.bar.h, 1)
            v.wait_ge(self.bar.h, t)

        @self.on("scalar")
        def _(e, t=tgt):
            e.memzero(self.dums[:]).then_inc(self.bar.h, 1)
            e.wait_ge(self.bar.h, t)

        @self.on("gpsimd")
        def _(e, t=tgt):
            e.memset(self.dumg[:], 0.0).then_inc(self.bar.h, 1)
            e.wait_ge(self.bar.h, t)

        @self.on("tensor")
        def _(e, t=tgt):
            e.nop().then_inc(self.bar.h, 1)
            e.wait_ge(self.bar.h, t)

        @self.on("sync")
        def _(e, t=tgt):
            e.nop().then_inc(self.bar.h, 1)
            e.wait_ge(self.bar.h, t)

    def sem(self, name=None):
        h = self.ctx.enter_context(self.nc.semaphore(f"{name or 'sm'}_{self._si}"))
        self._si += 1
        return Sem(h)

    def sbuf(self, name, shape, dt):
        self._ti += 1
        return self.cur.enter_context(self.nc.sbuf_tensor(f"{name}_{self._ti}", shape, dt))

    def dram(self, name, shape, dt, kind="Internal"):
        return self.nc.dram_tensor(name, shape, dt, kind=kind).ap()

    def mkdum(self):
        self.dumv = self.sbuf("dumv", [1, 2], F32)
        self.dums = self.sbuf("dums", [1, 2], F32)
        self.dumg = self.sbuf("dumg", [1, 2], F32)

    def init_psum(self, n_f32=8, n_bf16=0):
        self.banks = [self.ctx.enter_context(self.nc.psum_tensor(f"pb{i}", [128, 512], F32))
                      for i in range(n_f32)]
        self.tbanks = [self.ctx.enter_context(self.nc.psum_tensor(f"pt{i}", [128, 1024], BF16))
                       for i in range(n_bf16)]

    def on(self, eng):
        def deco(fn):
            self.q[eng].append(fn)
            return None
        return deco

    def finish(self):
        with self.nc.Block() as blk:
            for eng in ENGINES:
                def body(e, _eng=eng):
                    for fn in self.q[_eng]:
                        fn(e)
                getattr(blk, eng)(body)
        self.ctx.close()
        return self.nc


def emit_consts(p):
    ones_bf = p.sbuf("ones_bf", [128, 128], BF16)
    ones_tok = p.sbuf("ones_tok", [128, 1], BF16)
    ident_bf = p.sbuf("ident_bf", [128, 128], BF16)
    ident_f32 = p.sbuf("ident_f32", [128, 128], F32)
    fmp = p.sbuf("fmp", [128, 128], F32)
    eps5 = p.sbuf("eps5", [128, 1], F32)
    sc = p.sem("consts")

    @p.on("gpsimd")
    def _(g):
        g.iota(fmp[:], pattern=[[1, 128]], channel_multiplier=-1,
               allow_small_or_imprecise_dtypes=True).then_inc(sc.h, 1)
    t_iota = sc.inc()

    @p.on("vector")
    def _(v):
        v.wait_ge(sc.h, t_iota)
        v.memset(ones_bf[:], 1.0)
        v.memset(ones_tok[:], 1.0)
        v.memset(eps5[:], 1e-5)
        v.tensor_scalar(out=ident_bf[:], in0=fmp[:], scalar1=0.0, scalar2=None,
                        op0=ALU.is_equal)
        v.tensor_scalar(out=ident_f32[:], in0=fmp[:], scalar1=0.0, scalar2=None,
                        op0=ALU.is_equal).then_inc(sc.h, 1)
    sc.inc(1)
    return dict(ones_bf=ones_bf, ones_tok=ones_tok, ident_bf=ident_bf, ident_f32=ident_f32,
                eps5=eps5, fmp=fmp, sem=sc, ready=sc.n)


def ln_chunk(p, cst, W, xf, xb, sq, tmp, rm, rr, sems, bank_a, bank_b,
             out_bf=None, out_f32=None, gcol=None, bcol=None,
             bank_guard=0, sq_guard=0, out_guard=(None, 0), x_dep=(None, 0),
             xb_scr=None):
    sv, ss, st = sems["v"], sems["s"], sems["t"]
    entry_sv = sv.n
    dep_sem, dep_val = x_dep

    if xb is None:
        @p.on("vector")
        def _(v, g=sq_guard, entry=entry_sv):
            v.wait_ge(sv.h, entry)
            if dep_sem is not None:
                v.wait_ge(dep_sem, dep_val)
            v.wait_ge(st.h, g)
            ins = None
            for ct in range(CT):
                ins = v.tensor_copy(out=xb_scr(ct), in_=xf(ct))
            ins.then_inc(sv.h, 1)
        sv.inc()
        xb = xb_scr
        entry_sv = sv.n

    @p.on("scalar")
    def _(s, g=sq_guard):
        s.wait_ge(st.h, g)
        if dep_sem is not None:
            s.wait_ge(dep_sem, dep_val)
        ins = None
        for ct in range(CT):
            ins = s.activation(out=sq(ct), in_=xf(ct), func=AF.Square)
        ins.then_inc(ss.h, 1)
    sq_done = ss.inc()

    @p.on("tensor")
    def _(t, g=bank_guard, sq_done=sq_done, xbv=entry_sv):
        t.wait_ge(ss.h, sq_done)
        t.wait_ge(sv.h, xbv)
        if g:
            t.wait_ge(sv.h, g)
        for ct in range(CT):
            t.matmul(bank_a[:, 0:W], cst["ones_bf"][:], xb(ct), start=(ct == 0), stop=(ct == CT - 1))
        ins = None
        for ct in range(CT):
            ins = t.matmul(bank_b[:, 0:W], cst["ones_bf"][:], sq(ct), start=(ct == 0), stop=(ct == CT - 1))
        ins.then_inc(st.h, 1)
    stats_done = st.inc()

    @p.on("vector")
    def _(v, entry=entry_sv, stats_done=stats_done):
        v.wait_ge(st.h, stats_done)
        v.wait_ge(sv.h, entry)
        v.tensor_scalar(out=rm, in0=bank_a[:, 0:W], scalar1=1.0 / C, scalar2=None,
                        op0=ALU.mult).then_inc(sv.h, 1)
        v.wait_ge(sv.h, entry + 1)
        v.tensor_tensor(out=rr, in0=rm, in1=rm, op=ALU.mult).then_inc(sv.h, 1)
        v.wait_ge(sv.h, entry + 2)
        v.scalar_tensor_tensor(out=rr, in0=bank_b[:, 0:W], scalar=1.0 / C, in1=rr,
                               op0=ALU.mult, op1=ALU.subtract).then_inc(sv.h, 1)
        v.wait_ge(sv.h, entry + 3)
        v.tensor_scalar(out=rr, in0=rr, scalar1=0.0, scalar2=None,
                        op0=ALU.max).then_inc(sv.h, 1)
    var_done = sv.inc(4)

    @p.on("scalar")
    def _(s, var_done=var_done):
        s.wait_ge(sv.h, var_done)
        s.activation(out=rr, in_=rr, func=AF.Sqrt, bias=cst["eps5"][:, 0:1]).then_inc(ss.h, 1)
    sd_done = ss.inc()

    @p.on("vector")
    def _(v, sd_done=sd_done, var_done=var_done):
        v.wait_ge(ss.h, sd_done)
        v.reciprocal(out=rr, in_=rr).then_inc(sv.h, 1)
        v.wait_ge(sv.h, var_done + 1)
        v.tensor_tensor(out=rm, in0=rm, in1=rr, op=ALU.mult).then_inc(sv.h, 1)
    rows_done = sv.inc(2)

    ogs = out_guard if isinstance(out_guard, list) else [out_guard]
    ogs = [(a, b) for (a, b) in ogs if a is not None and b]
    og_sem = ogs[0][0] if ogs else None
    og_val = ogs[0][1] if ogs else 0

    @p.on("vector")
    def _(v, rows_done=rows_done):
        v.wait_ge(sv.h, rows_done)
        for _sem, _val in ogs:
            v.wait_ge(_sem, _val)
        ins = None
        for ct in range(CT):
            ins = v.tensor_tensor(out=tmp(ct), in0=xf(ct), in1=rr, op=ALU.mult)
        ins.then_inc(sv.h, 1)
        v.wait_ge(sv.h, rows_done + 1)
        ins = None
        for ct in range(CT):
            dst = out_f32(ct) if (out_f32 is not None and gcol is None) else tmp(ct)
            ins = v.tensor_tensor(out=dst, in0=tmp(ct), in1=rm, op=ALU.subtract)
        ins.then_inc(sv.h, 1)
    z_done = sv.inc(2)
    v_end = z_done
    s_end = ss.n

    if gcol is None:
        if out_bf is not None:
            src = out_f32 if out_f32 is not None else tmp
            @p.on("vector")
            def _(v, z_done=z_done):
                v.wait_ge(sv.h, z_done)
                ins = None
                for ct in range(CT):
                    ins = v.tensor_copy(out=out_bf(ct), in_=src(ct))
                ins.then_inc(sv.h, 1)
            v_end = sv.inc()
    else:
        @p.on("scalar")
        def _(s, z_done=z_done):
            s.wait_ge(sv.h, z_done)
            for _sem, _val in ogs:
                s.wait_ge(_sem, _val)
            ins = None
            for ct in range(CT):
                dst = out_bf(ct) if out_bf is not None else out_f32(ct)
                ins = s.activation(out=dst, in_=tmp(ct), func=AF.Identity,
                                   scale=gcol(ct), bias=bcol(ct))
            ins.then_inc(ss.h, 1)
        s_end = ss.inc()
        if out_bf is not None and out_f32 is not None:
            @p.on("vector")
            def _(v, s_end=s_end):
                v.wait_ge(ss.h, s_end)
                ins = None
                for ct in range(CT):
                    ins = v.tensor_copy(out=out_f32(ct), in_=out_bf(ct))
                ins.then_inc(sv.h, 1)
            v_end = sv.inc()

    return dict(v=v_end, s=s_end, stats=stats_done, var=var_done)


def emit_s0(p, cst, xT, xnT, barrier=False, strong=()):
    CW = 512
    NCH = TOK // CW
    xr = xT.rearrange("(ct p) s t -> p ct (s t)", p=128)
    onr = xnT.rearrange("(ct p) s t -> p ct (s t)", p=128)

    xf = p.sbuf("xf", [128, CT, 2, CW], F32)
    xb = p.sbuf("xb", [128, CT, 2, CW], BF16)
    sq = p.sbuf("sq", [128, CT, CW], BF16)
    of = p.sbuf("of", [128, CT, 2, CW], F32)
    rm = p.sbuf("rm", [128, CW], F32)
    rr = p.sbuf("rr", [128, CW], F32)

    sd = p.sem("dma_sync")
    sg = p.sem("dma_gp")
    so = p.sem("outd")
    sv = p.sem("v")
    ss = p.sem("s")
    st = p.sem("t")
    sems = dict(v=sv, s=ss, t=st)

    sbar = p.sem("bar")
    guard_bank = [0, 0]
    guard_sq = 0
    guard_out = [0, 0]
    prev_v_end = [0, 0]
    prev_stats = [0, 0]

    for ci in range(NCH):
        pb = ci % 2
        t0 = ci * CW

        if "loads" in strong:
            lneed_v, lneed_t = prev_v_end[1 - pb], prev_stats[1 - pb]
        else:
            lneed_v, lneed_t = prev_v_end[pb], prev_stats[pb]
        @p.on("sync")
        def _(e, t0=t0, pb=pb, need=lneed_v, pre=sd.n):
            if need:
                e.wait_ge(sv.h, need)
            dmaw(e, sd, pre, out=xf[:, :, pb, :], in_=xr[:, :, t0:t0 + CW])
        loads_s = sd.inc(16)

        @p.on("gpsimd")
        def _(e, t0=t0, pb=pb, need=lneed_t, pre=sg.n):
            if need:
                e.wait_ge(st.h, need)
            dmaw(e, sg, pre, out=xb[:, :, pb, :], in_=xr[:, :, t0:t0 + CW])
        loads_g = sg.inc(16)

        @p.on("vector")
        def _(v, ns=loads_s, ng=loads_g):
            v.wait_ge(sd.h, ns)
            v.wait_ge(sg.h, ng)
            v.memset(p.dumv[:], 0.0).then_inc(sv.h, 1)
        @p.on("scalar")
        def _(s, ns=loads_s, ng=loads_g):
            s.wait_ge(sd.h, ns)
            s.wait_ge(sg.h, ng)
            s.memzero(p.dums[:]).then_inc(ss.h, 1)
        sv.inc(); ss.inc()

        r = ln_chunk(
            p, cst, CW,
            xf=lambda ct, pb=pb: xf[:, ct, pb, :],
            xb=lambda ct, pb=pb: xb[:, ct, pb, :],
            sq=lambda ct: sq[:, ct, :],
            tmp=lambda ct, pb=pb: of[:, ct, pb, :],
            rm=rm[:], rr=rr[:], sems=sems,
            bank_a=p.banks[2 * pb], bank_b=p.banks[2 * pb + 1],
            out_f32=lambda ct, pb=pb: of[:, ct, pb, :],
            bank_guard=guard_bank[pb], sq_guard=guard_sq,
            out_guard=(so.h, guard_out[pb]),
        )
        guard_bank[pb] = r["v"] if "bank" in strong else r["var"]
        guard_sq = r["stats"]
        prev_v_end[pb] = r["v"]
        prev_stats[pb] = r["stats"]

        @p.on("sync")
        def _(e, t0=t0, pb=pb, need=r["v"], pre=so.n):
            e.wait_ge(sv.h, need)
            dmaw(e, so, pre, out=onr[:, :, t0:t0 + CW], in_=of[:, :, pb, :])
        guard_out[pb] = so.inc(16)

        if barrier:
            tgt = sbar.inc(4)
            for eng in ("vector", "scalar", "tensor", "gpsimd"):
                @p.on(eng)
                def _(e, tgt=tgt, need=so.n):
                    if e is not None:
                        pass
                    e.wait_ge(so.h, need)
                    e.then_inc(sbar.h, 1) if False else e.nop().then_inc(sbar.h, 1)
                    e.wait_ge(sbar.h, tgt)

    @p.on("sync")
    def _(e, need=so.n):
        e.wait_ge(so.h, need)





def emit_s2(p, cst, xnT, qw, qsc, qb, la, lb, lng, lnb, gT, hA, nblocks=3, nch_lim=None):
    gr = gT.rearrange("(ct p) s t -> p ct (s t)", p=128)
    xr = xnT.rearrange("(ct p) s t -> p ct (s t)", p=128)
    hr = hA.rearrange("(ct p) s t -> p ct (s t)", p=128)

    CW = 512
    NCH = TOK // CW if nch_lim is None else nch_lim

    wT = [p.sbuf(f"wT{j}", [128, CT, C], BF16) for j in range(2)]
    laT = [p.sbuf(f"laT{j}", [128, CT, 32], BF16) for j in range(2)]
    lbT = p.sbuf("lbT", [32, 2, C], BF16)
    qs = p.sbuf("qs", [128, 2, C], I32)
    qf = p.sbuf("qf", [128, C], F32)
    wbf = p.sbuf("wbf", [128, 2, C], BF16)
    labf = p.sbuf("labf", [32, C], BF16)
    lbbf = p.sbuf("lbbf", [128, CT, 32], BF16)
    scs = p.sbuf("scs", [128, 6, CT, 64], F32)
    qbc = p.sbuf("qbc", [128, 6, CT], F32)
    lngs = p.sbuf("lngs", [128, 3, CT], F32)
    lnbs = p.sbuf("lnbs", [128, 3, CT], F32)

    hf = p.sbuf("hf", [128, CT, 2, CW], F32)
    hb = p.sbuf("hb", [128, CT, 2, CW], BF16)
    y_sb = p.sbuf("y_sb", [128, CT, CW], F32)
    ybf = p.sbuf("ybf", [128, CT, CW], BF16)
    z_sb = p.sbuf("z_sb", [128, CT, CW], F32)
    g_bf = p.sbuf("g_bf", [128, CT, CW], BF16)
    sq = p.sbuf("sq", [128, CT, CW], BF16)
    yr_bf = p.sbuf("yr_bf", [32, CW], BF16)
    rm = p.sbuf("rm", [128, CW], F32)
    rr = p.sbuf("rr", [128, CW], F32)

    s_par = p.sem("s_par")
    s_q = p.sem("s_q")
    s_hf = p.sem("s_hf")
    s_la = p.sem("s_la")
    s_lb = p.sem("s_lb")
    s_hb = p.sem("s_hb")
    so = p.sem("so")
    sv = p.sem("v")
    ss = p.sem("s")
    st = p.sem("t")
    sems = dict(v=sv, s=ss, t=st)

    @p.on("sync")
    def _(e):
        dmaw(e, s_par, 0, out=scs[:], in_=qsc)
        dmaw(e, s_par, 16, out=qbc[:], in_=qb)
        dmaw(e, s_par, 32, out=lngs[:], in_=lng)
        dmaw(e, s_par, 48, out=lnbs[:], in_=lnb)
    params_ld = s_par.inc(64)

    g_wT_free = [0, 0]
    g_tb = [0, 0]
    g_bank = {0: ("ss", 0), 1: ("ss", 0), 2: ("sv", 0), 3: ("sv", 0), 4: ("sv", 0), 5: ("ss", 0)}
    g_sq = 0
    g_hn = [0, 0]
    g_qs = [0, 0]
    g_wbf = [0, 0]
    g_labf = [0]
    g_lbbf = [0]
    store_cnt = [[0] * NCH for _ in range(4)]
    semmap = dict(sv=sv, ss=ss, st=st)

    def prep_weights(i, j):
        wg = g_wT_free[j]
        for ot in range(CT):
            sl = ot % 2

            @p.on("sync")
            def _(e, i=i, ot=ot, sl=sl, need=g_qs[sl], pre=s_q.n):
                if need:
                    e.wait_ge(sv.h, need)
                dmaw(e, s_q, pre, out=qs[:, sl, :], in_=qw[i, ot * 128:(ot + 1) * 128, :])
            qld = s_q.inc(16)

            @p.on("vector")
            def _(v, i=i, ot=ot, sl=sl, qld=qld, entry=sv.n, pl=params_ld, wbg=g_wbf[sl]):
                v.wait_ge(s_q.h, qld)
                v.wait_ge(s_par.h, pl)
                v.wait_ge(sv.h, entry)
                v.tensor_copy(out=qf[:], in_=qs[:, sl, :]).then_inc(sv.h, 1)
                v.wait_ge(sv.h, entry + 1)
                v.tensor_scalar(out=qf[:], in0=qf[:], scalar1=2.0 / 15.0, scalar2=-1.0,
                                op0=ALU.mult, op1=ALU.add).then_inc(sv.h, 1)
                v.wait_ge(sv.h, entry + 2)
                if wbg:
                    v.wait_ge(st.h, wbg)
                v.tensor_tensor(
                    out=wbf[:, sl, :].rearrange("p (g k) -> p g k", g=64),
                    in0=qf[:].rearrange("p (g k) -> p g k", g=64),
                    in1=scs[:, i, ot, :].unsqueeze(2).broadcast_to([128, 64, 16]),
                    op=ALU.mult).then_inc(sv.h, 1)
            deq_done = sv.inc(3)
            g_qs[sl] = deq_done - 2

            tb = ot % 2

            @p.on("tensor")
            def _(t, sl=sl, tb=tb, deq=deq_done, tbg=g_tb[tb], cr=cst["ready"]):
                t.wait_ge(sv.h, deq)
                t.wait_ge(cst["sem"].h, cr)
                if tbg:
                    t.wait_ge(sv.h, tbg)
                ins = None
                for dt in range(CT):
                    ins = t.transpose(p.tbanks[tb][:, dt * 128:(dt + 1) * 128],
                                      wbf[:, sl, dt * 128:(dt + 1) * 128], cst["ident_bf"][:])
                ins.then_inc(st.h, 1)
            tr_done = st.inc()
            g_wbf[sl] = tr_done

            @p.on("vector")
            def _(v, ot=ot, j=j, tb=tb, tr=tr_done, wg=wg):
                v.wait_ge(st.h, tr)
                if wg:
                    v.wait_ge(st.h, wg)
                ins = None
                for dt in range(CT):
                    ins = v.tensor_copy(out=wT[j][:, dt, ot * 128:(ot + 1) * 128],
                                        in_=p.tbanks[tb][:, dt * 128:(dt + 1) * 128])
                ins.then_inc(sv.h, 1)
            g_tb[tb] = sv.inc()

        @p.on("gpsimd")
        def _(e, i=i, need=g_labf[0], pre=s_la.n):
            if need:
                e.wait_ge(st.h, need)
            dmaw(e, s_la, pre, out=labf[:], in_=la[i])
        lald = s_la.inc(16)

        @p.on("tensor")
        def _(t, lald=lald, tbg=g_tb[0], cr=cst["ready"]):
            t.wait_ge(s_la.h, lald)
            t.wait_ge(cst["sem"].h, cr)
            if tbg:
                t.wait_ge(sv.h, tbg)
            ins = None
            for dt in range(CT):
                ins = t.transpose(p.tbanks[0][:, dt * 32:(dt + 1) * 32],
                                  labf[:, dt * 128:(dt + 1) * 128], cst["ident_bf"][0:32, 0:32])
            ins.then_inc(st.h, 1)
        latr = st.inc()
        g_labf[0] = latr

        @p.on("vector")
        def _(v, j=j, latr=latr, wg=wg):
            v.wait_ge(st.h, latr)
            if wg:
                v.wait_ge(st.h, wg)
            ins = None
            for dt in range(CT):
                ins = v.tensor_copy(out=laT[j][:, dt, :], in_=p.tbanks[0][:, dt * 32:(dt + 1) * 32])
            ins.then_inc(sv.h, 1)
        g_tb[0] = sv.inc()

        @p.on("gpsimd")
        def _(e, i=i, need=g_lbbf[0], pre=s_lb.n):
            if need:
                e.wait_ge(st.h, need)
            dmaw(e, s_lb, pre, out=lbbf[:], in_=lb[i].rearrange("(ot pp) r -> pp ot r", pp=128))
        lbld = s_lb.inc(16)

        @p.on("tensor")
        def _(t, lbld=lbld, tbg=g_tb[1]):
            t.wait_ge(s_lb.h, lbld)
            if tbg:
                t.wait_ge(sv.h, tbg)
            ins = None
            for ot in range(CT):
                ins = t.transpose(p.tbanks[1][0:32, ot * 128:(ot + 1) * 128],
                                  lbbf[:, ot, :], cst["ident_bf"][:])
            ins.then_inc(st.h, 1)
        lbtr = st.inc()
        g_lbbf[0] = lbtr

        @p.on("vector")
        def _(v, j=j, lbtr=lbtr, wg=wg):
            v.wait_ge(st.h, lbtr)
            if wg:
                v.wait_ge(st.h, wg)
            v.tensor_copy(out=lbT[:, j, :], in_=p.tbanks[1][0:32, 0:1024]).then_inc(sv.h, 1)
        g_tb[1] = sv.inc()

    h_in_r = xr
    for bi in range(nblocks):
        i0, i1 = 2 * bi, 2 * bi + 1
        prep_weights(i0, 0)
        prep_weights(i1, 1)
        w_ready = sv.n

        last = (bi == nblocks - 1)
        for ch in range(NCH):
            pb = ch % 2
            t0 = ch * CW
            prev_store = store_cnt[bi][ch]

            @p.on("sync")
            def _(e, t0=t0, pb=pb, need=sv.n, ps=max(prev_store, g_hn[pb]), pre=s_hf.n, hin=h_in_r):
                if need:
                    e.wait_ge(sv.h, need)
                if ps:
                    e.wait_ge(so.h, ps)
                dmaw(e, s_hf, pre, out=hf[:, :, pb, :], in_=hin[:, :, t0:t0 + CW])
            lds = s_hf.inc(16)

            @p.on("gpsimd")
            def _(e, t0=t0, pb=pb, need=st.n, ps=prev_store, pre=s_hb.n, hin=h_in_r):
                if need:
                    e.wait_ge(st.h, need)
                if ps:
                    e.wait_ge(so.h, ps)
                dmaw(e, s_hb, pre, out=hb[:, :, pb, :], in_=hin[:, :, t0:t0 + CW])
            ldg = s_hb.inc(16)

            @p.on("vector")
            def _(v, ns=lds, ng=ldg, wr=w_ready):
                v.wait_ge(s_hf.h, ns)
                v.wait_ge(s_hb.h, ng)
                v.wait_ge(sv.h, wr)
                v.memset(p.dumv[:], 0.0).then_inc(sv.h, 1)
            ldbar_v = sv.inc()

            @p.on("scalar")
            def _(s, ns=lds, ng=ldg):
                s.wait_ge(s_hf.h, ns)
                s.wait_ge(s_hb.h, ng)
                s.memzero(p.dums[:]).then_inc(ss.h, 1)
            ldbar_s = ss.inc()

            @p.on("tensor")
            def _(t, pb=pb, bar=ldbar_v, bar_s=ldbar_s, gb2=g_bank[2]):
                t.wait_ge(sv.h, bar)
                t.wait_ge(ss.h, bar_s)
                nm2, val2 = gb2
                if val2:
                    t.wait_ge(semmap[nm2].h, val2)
                ins = None
                for dt in range(CT):
                    ins = t.matmul(p.banks[2][0:32, 0:CW], laT[0][:, dt, :], hb[:, dt, pb, :],
                                   start=(dt == 0), stop=(dt == CT - 1))
                ins.then_inc(st.h, 1)
            yr_done = st.inc()

            @p.on("vector")
            def _(v, yr=yr_done, entry=sv.n):
                v.wait_ge(st.h, yr)
                v.wait_ge(sv.h, entry)
                v.tensor_copy(out=yr_bf[:], in_=p.banks[2][0:32, 0:CW]).then_inc(sv.h, 1)
            yr_ev = sv.inc()
            g_bank[2] = ("sv", yr_ev)

            for ot in range(CT):
                bk = (0, 1, 5)[ot % 3]

                @p.on("tensor")
                def _(t, ot=ot, bk=bk, pb=pb, yr=yr_ev, gb=g_bank[bk]):
                    if ot == 0:
                        t.wait_ge(sv.h, yr)
                    nm, val = gb
                    if val:
                        t.wait_ge(semmap[nm].h, val)
                    for dt in range(CT):
                        t.matmul(p.banks[bk][:, 0:CW], wT[0][:, dt, ot * 128:(ot + 1) * 128],
                                 hb[:, dt, pb, :], start=(dt == 0), stop=False)
                    t.matmul(p.banks[bk][:, 0:CW], lbT[0:32, 0, ot * 128:(ot + 1) * 128],
                             yr_bf[:], start=False, stop=True).then_inc(st.h, 1)
                mm_done = st.inc()

                @p.on("scalar")
                def _(s, ot=ot, bk=bk, i=i0, mmd=mm_done):
                    s.wait_ge(st.h, mmd)
                    s.activation(out=y_sb[:, ot, :], in_=p.banks[bk][:, 0:CW], func=AF.Identity,
                                 bias=qbc[:, i, ot:ot + 1]).then_inc(ss.h, 1)
                g_bank[bk] = ("ss", ss.inc())
            y1_done_s = ss.n

            r = ln_chunk(
                p, cst, CW,
                xf=lambda ct: y_sb[:, ct, :],
                xb=None,
                sq=lambda ct: sq[:, ct, :],
                tmp=lambda ct: z_sb[:, ct, :],
                rm=rm[:], rr=rr[:], sems=sems,
                bank_a=p.banks[3], bank_b=p.banks[4],
                out_f32=lambda ct: z_sb[:, ct, :],
                bank_guard=max(g_bank[3][1], g_bank[4][1]), sq_guard=g_sq,
                x_dep=(ss.h, y1_done_s),
                xb_scr=lambda ct: ybf[:, ct, :],
            )
            g_bank[3] = ("sv", r["var"])
            g_bank[4] = ("sv", r["var"])
            g_sq = r["stats"]

            @p.on("scalar")
            def _(s, bi=bi, zv=r["v"]):
                s.wait_ge(sv.h, zv)
                ins = None
                for ct in range(CT):
                    ins = s.activation(out=g_bf[:, ct, :], in_=z_sb[:, ct, :], func=AF.Gelu,
                                       scale=lngs[:, bi, ct:ct + 1], bias=lnbs[:, bi, ct:ct + 1])
                ins.then_inc(ss.h, 1)
            gelu_done = ss.inc()

            @p.on("tensor")
            def _(t, gd=gelu_done, gb2=g_bank[2]):
                t.wait_ge(ss.h, gd)
                nm2, val2 = gb2
                if val2:
                    t.wait_ge(semmap[nm2].h, val2)
                ins = None
                for dt in range(CT):
                    ins = t.matmul(p.banks[2][0:32, 0:CW], laT[1][:, dt, :], g_bf[:, dt, :],
                                   start=(dt == 0), stop=(dt == CT - 1))
                ins.then_inc(st.h, 1)
            yr2_done = st.inc()

            @p.on("vector")
            def _(v, yr=yr2_done, entry=sv.n):
                v.wait_ge(st.h, yr)
                v.wait_ge(sv.h, entry)
                v.tensor_copy(out=yr_bf[:], in_=p.banks[2][0:32, 0:CW]).then_inc(sv.h, 1)
            yr2_ev = sv.inc()
            g_bank[2] = ("sv", yr2_ev)

            for ot in range(CT):
                bk = (0, 1, 5)[ot % 3]

                @p.on("tensor")
                def _(t, ot=ot, bk=bk, yr=yr2_ev, gb=g_bank[bk]):
                    if ot == 0:
                        t.wait_ge(sv.h, yr)
                    nm, val = gb
                    if val:
                        t.wait_ge(semmap[nm].h, val)
                    for dt in range(CT):
                        t.matmul(p.banks[bk][:, 0:CW], wT[1][:, dt, ot * 128:(ot + 1) * 128],
                                 g_bf[:, dt, :], start=(dt == 0), stop=False)
                    t.matmul(p.banks[bk][:, 0:CW], lbT[0:32, 1, ot * 128:(ot + 1) * 128],
                             yr_bf[:], start=False, stop=True).then_inc(st.h, 1)
                mm2_done = st.inc()

                @p.on("vector")
                def _(v, ot=ot, bk=bk, i=i1, pb=pb, mmd=mm2_done, entry=sv.n, gd=gelu_done):
                    v.wait_ge(st.h, mmd)
                    v.wait_ge(sv.h, entry)
                    if ot == 0:
                        v.wait_ge(ss.h, gd)
                    v.tensor_scalar(out=z_sb[:, ot, :], in0=p.banks[bk][:, 0:CW],
                                    scalar1=qbc[:, i, ot:ot + 1], scalar2=None,
                                    op0=ALU.add).then_inc(sv.h, 1)
                    v.wait_ge(sv.h, entry + 1)
                    v.tensor_tensor(out=hf[:, ot, pb, :], in0=z_sb[:, ot, :],
                                    in1=hf[:, ot, pb, :], op=ALU.add).then_inc(sv.h, 1)
                g_bank[bk] = ("sv", sv.inc(2))
            h_done_v = sv.n

            if last:
                @p.on("scalar")
                def _(s, pb=pb, hd=h_done_v):
                    s.wait_ge(sv.h, hd)
                    ins = None
                    for ct in range(CT):
                        ins = s.activation(out=hf[:, ct, pb, :], in_=hf[:, ct, pb, :], func=AF.Silu)
                    ins.then_inc(ss.h, 1)
                sil_done = ss.inc()

                @p.on("sync")
                def _(e, t0=t0, pb=pb, sd_=sil_done, pre=so.n):
                    e.wait_ge(ss.h, sd_)
                    dmaw(e, so, pre, out=gr[:, :, t0:t0 + CW], in_=hf[:, :, pb, :])
            else:
                @p.on("sync")
                def _(e, t0=t0, pb=pb, hd=h_done_v, pre=so.n):
                    e.wait_ge(sv.h, hd)
                    dmaw(e, so, pre, out=hr[:, :, t0:t0 + CW], in_=hf[:, :, pb, :])
            store_cnt[bi + 1][ch] = g_hn[pb] = so.inc(16)

        g_wT_free[0] = st.n
        g_wT_free[1] = st.n
        h_in_r = hr

    @p.on("sync")
    def _(e, need=so.n):
        e.wait_ge(so.h, need)


def build_s2(nblocks=3, nch_lim=None):
    p = Prog(); p.mkdum(); p.init_psum(6, 2)
    xnT = p.dram("xnT", [C, S, T], F32, kind="ExternalInput")
    qw = p.dram("qw", [6, C, C], I32, kind="ExternalInput")
    qsc = p.dram("qsc", [128, 6, CT, 64], F32, kind="ExternalInput")
    qb = p.dram("qb", [128, 6, CT], F32, kind="ExternalInput")
    la = p.dram("la", [6, 32, C], F32, kind="ExternalInput")
    lb = p.dram("lb", [6, C, 32], F32, kind="ExternalInput")
    lng = p.dram("lng", [128, 3, CT], F32, kind="ExternalInput")
    lnb = p.dram("lnb", [128, 3, CT], F32, kind="ExternalInput")
    gT = p.dram("gT", [C, S, T], F32, kind="ExternalOutput")
    hA = p.dram("hA", [C, S, T], F32, kind="Internal")
    cst = emit_consts(p)
    with p.stage():
        emit_s2(p, cst, xnT, qw, qsc, qb, la, lb, lng, lnb, gT, hA, nblocks, nch_lim)
    return p.finish()


def emit_comb(p, cst, xT, sT, gT, x1T, xn1T=None):
    with_ln = xn1T is not None
    xr = xT.rearrange("(ct p) s t -> p ct (s t)", p=128)
    sr = sT.rearrange("(ct p) s t -> p ct (s t)", p=128)
    gr = gT.rearrange("(ct p) s t -> p ct (s t)", p=128)
    x1r = x1T.rearrange("(ct p) s t -> p ct (s t)", p=128)
    if with_ln:
        xnr = xn1T.rearrange("(ct p) s t -> p ct (s t)", p=128)

    CW = 512
    NCH = TOK // CW

    xf = p.sbuf("xf", [128, CT, 2, CW], F32)
    sf = p.sbuf("sf", [128, CT, 2, CW], F32)
    gf = p.sbuf("gf", [128, CT, 2, CW], F32)
    x1f = p.sbuf("x1f", [128, CT, 2, CW], F32)
    xb = p.sbuf("xb", [128, CT, CW], BF16)
    sq = p.sbuf("sq", [128, CT, CW], BF16)
    onf = p.sbuf("onf", [128, CT, 2, CW], F32)
    rm = p.sbuf("rm", [128, CW], F32)
    rr = p.sbuf("rr", [128, CW], F32)

    s_x = p.sem("s_x")
    s_s = p.sem("s_s")
    s_g = p.sem("s_g")
    so = p.sem("so")
    sv = p.sem("v")
    ss = p.sem("s")
    st = p.sem("t")
    sems = dict(v=sv, s=ss, t=st)

    g_bank = 0
    g_sq = 0
    g_out = [0, 0]
    g_slotv = [0, 0]

    for ci in range(NCH):
        pb = ci % 2
        t0 = ci * CW

        @p.on("sync")
        def _(e, t0=t0, pb=pb, need=g_slotv[pb], prex=s_x.n, preg=s_g.n):
            if need:
                e.wait_ge(sv.h, need)
            dmaw(e, s_x, prex, out=xf[:, :, pb, :], in_=xr[:, :, t0:t0 + CW])
            dmaw(e, s_g, preg, out=gf[:, :, pb, :], in_=gr[:, :, t0:t0 + CW])
        ldx = s_x.inc(16)
        ldg = s_g.inc(16)

        @p.on("gpsimd")
        def _(e, t0=t0, pb=pb, need=g_slotv[pb], pres=s_s.n):
            if need:
                e.wait_ge(sv.h, need)
            dmaw(e, s_s, pres, out=sf[:, :, pb, :], in_=sr[:, :, t0:t0 + CW])
        lds = s_s.inc(16)

        @p.on("vector")
        def _(v, pb=pb, nx=ldx, ng=ldg, ns=lds, entry=sv.n, og=g_out[pb]):
            v.wait_ge(s_x.h, nx)
            v.wait_ge(s_g.h, ng)
            v.wait_ge(s_s.h, ns)
            v.wait_ge(sv.h, entry)
            if og:
                v.wait_ge(so.h, og)
            ins = None
            for ct in range(CT):
                ins = v.tensor_tensor(out=x1f[:, ct, pb, :], in0=sf[:, ct, pb, :],
                                      in1=gf[:, ct, pb, :], op=ALU.mult)
            ins.then_inc(sv.h, 1)
            v.wait_ge(sv.h, entry + 1)
            ins = None
            for ct in range(CT):
                ins = v.tensor_tensor(out=x1f[:, ct, pb, :], in0=x1f[:, ct, pb, :],
                                      in1=xf[:, ct, pb, :], op=ALU.add)
            ins.then_inc(sv.h, 1)
        x1_done = sv.inc(2)

        @p.on("sync")
        def _(e, t0=t0, pb=pb, x1d=x1_done, pre=so.n):
            e.wait_ge(sv.h, x1d)
            dmaw(e, so, pre, out=x1r[:, :, t0:t0 + CW], in_=x1f[:, :, pb, :])
        st1 = so.inc(16)

        if with_ln:
            r = ln_chunk(
                p, cst, CW,
                xf=lambda ct, pb=pb: x1f[:, ct, pb, :],
                xb=None,
                sq=lambda ct: sq[:, ct, :],
                tmp=lambda ct, pb=pb: onf[:, ct, pb, :],
                rm=rm[:], rr=rr[:], sems=sems,
                bank_a=p.banks[0 + 2 * pb], bank_b=p.banks[1 + 2 * pb],
                out_f32=lambda ct, pb=pb: onf[:, ct, pb, :],
                bank_guard=g_bank, sq_guard=g_sq,
                out_guard=(so.h, g_out[pb]),
                x_dep=(sv.h, x1_done),
                xb_scr=lambda ct: xb[:, ct, :],
            )
            g_bank = r["var"]
            g_sq = r["stats"]

            @p.on("sync")
            def _(e, t0=t0, pb=pb, need=r["v"], pre=so.n):
                e.wait_ge(sv.h, need)
                dmaw(e, so, pre, out=xnr[:, :, t0:t0 + CW], in_=onf[:, :, pb, :])
            g_out[pb] = so.inc(16)
            g_slotv[pb] = r["v"]
        else:
            g_out[pb] = st1
            g_slotv[pb] = x1_done

    @p.on("sync")
    def _(e, need=so.n):
        e.wait_ge(so.h, need)


def build_comb(with_ln=True):
    p = Prog(); p.mkdum(); p.init_psum(6, 2)
    xT = p.dram("xT", [C, S, T], F32, kind="ExternalInput")
    sT = p.dram("sT", [C, S, T], F32, kind="ExternalInput")
    gT = p.dram("gT", [C, S, T], F32, kind="ExternalInput")
    x1T = p.dram("x1T", [C, S, T], F32, kind="ExternalOutput")
    xn1T = p.dram("xn1T", [C, S, T], F32, kind="ExternalOutput") if with_ln else None
    cst = emit_consts(p)
    with p.stage():
        emit_comb(p, cst, xT, sT, gT, x1T, xn1T)
    return p.finish()


def emit_s5(p, cst, xnT, wgt, wut, wdt, wrt, rb, g1T, nsc_lim=None):
    xr = xnT.rearrange("(ct p) s t -> p ct (s t)", p=128)
    gr = g1T.rearrange("(ct p) s t -> p ct (s t)", p=128)

    SCW = 512
    SW = SCW
    NSC = TOK // SCW if nsc_lim is None else nsc_lim
    NE = 9
    IT = 16

    xb = p.sbuf("xb", [128, CT, SW], BF16)
    _wg = p.sbuf("wg_sb", [128, CT, 2048], BF16)
    _wu = p.sbuf("wu_sb", [128, CT, 2048], BF16)
    wg_sb = [_wg, _wg]
    wu_sb = [_wu, _wu]
    wd_sb = [p.sbuf("wd_sb", [128, IT, C], BF16) for _ in range(2)]
    wr_sb = p.sbuf("wr_sb", [128, CT, 8], BF16)
    rb_sb = p.sbuf("rb_sb", [8, 1], F32)
    e8 = p.sbuf("e8", [8, 8, 128], BF16)
    probsT = p.sbuf("probsT", [8, SW], F32)
    ptm = p.sbuf("ptm", [128, 4, 8], F32)
    m1q = p.sbuf("m1q", [128, 4], F32)
    m2q = p.sbuf("m2q", [128, 4], F32)
    pm = p.sbuf("pm", [128, 4, 8], F32)
    cwtm = p.sbuf("cwtm", [128, 4, 8], F32)
    cwT = p.sbuf("cwT", [8, SW], BF16)
    cwba = p.sbuf("cwba", [128, 8, SW], BF16)
    sil = p.sbuf("sil", [128, SW], F32)
    hid = [p.sbuf("hid", [128, IT, SW], BF16) for _ in range(2)]
    out_sb = p.sbuf("out_sb", [128, CT, SW], F32)

    s_x = p.sem("s_x")
    s_wg = p.sem("s_wg")
    s_wu = p.sem("s_wu")
    s_wd = p.sem("s_wd")
    s_pr = p.sem("s_pr")
    so = p.sem("so")
    sv = p.sem("v")
    ss = p.sem("s")
    st = p.sem("t")
    s_e8 = p.sem("s_e8")
    s_rb = p.sem("s_rb")

    @p.on("gpsimd")
    def _(e):
        dmaw(e, s_pr, 0, out=wr_sb[:], in_=wrt.rearrange("(ct p) e -> p ct e", p=128))
        e.iota(e8[:], pattern=[[1, 8], [0, 128]], channel_multiplier=-1,
               allow_small_or_imprecise_dtypes=True).then_inc(s_e8.h, 1)
    pr1 = s_pr.inc(16)
    e8i = s_e8.inc()

    @p.on("sync")
    def _(e):
        dmaw(e, s_rb, 0, out=rb_sb[:], in_=rb)
    pr2 = s_rb.inc(16)

    @p.on("vector")
    def _(v, pr=pr2, cr=cst["ready"], e8i_=e8i, w1=pr1):
        v.wait_ge(s_rb.h, pr)
        v.wait_ge(s_pr.h, w1)
        v.wait_ge(s_e8.h, e8i_)
        v.wait_ge(cst["sem"].h, cr)
        v.tensor_scalar(out=e8[:], in0=e8[:], scalar1=0.0, scalar2=None,
                        op0=ALU.is_equal).then_inc(sv.h, 1)
    e8_done = sv.inc()

    g_xb = 0
    g_wgu = [0, 0]
    g_wd = [0, 0]
    g_hid = [0, 0]
    g_out = 0
    g_rbank = ("ss", 0)
    g_AB = [("sv", 0), ("sv", 0)]
    g_AB2 = [("sv", 0), ("sv", 0)]
    g_D = [("sv", 0), ("sv", 0)]
    g_C = ("sv", 0)
    semmap = dict(sv=sv, ss=ss, st=st)

    def wgd(t, g):
        nm, val = g
        if val:
            t.wait_ge(semmap[nm].h, val)

    for scn in range(NSC):
        t0 = scn * SCW

        @p.on("gpsimd")
        def _(e, t0=t0, need=g_xb, pre=s_x.n):
            if need:
                e.wait_ge(st.h, need)
            dmaw(e, s_x, pre, out=xb[:], in_=xr[:, :, t0:t0 + SCW])
        xld = s_x.inc(16)

        @p.on("tensor")
        def _(t, xld=xld, ed=e8_done, rb_=g_rbank, gC_=g_C, gD4=g_D[0], gD5=g_D[1]):
            t.wait_ge(s_x.h, xld)
            t.wait_ge(sv.h, ed)
            wgd(t, rb_)
            wgd(t, gC_)
            wgd(t, gD4)
            wgd(t, gD5)
            ins = None
            for kt in range(CT):
                ins = t.matmul(p.banks[4][0:8, 0:SW], wr_sb[:, kt, :],
                               xb[:, kt, :], start=(kt == 0), stop=(kt == CT - 1))
            ins.then_inc(st.h, 1)
        rt_done = st.inc()

        @p.on("scalar")
        def _(s, rt=rt_done):
            s.wait_ge(st.h, rt)
            s.activation(out=probsT[:], in_=p.banks[4][0:8, 0:SW],
                         func=AF.Sigmoid, bias=rb_sb[:, 0:1]).then_inc(ss.h, 1)
        sig_done = ss.inc()
        g_rbank = ("ss", sig_done)

        @p.on("tensor")
        def _(t, sg=sig_done, cr=cst["ready"], gC_=g_C):
            t.wait_ge(ss.h, sg)
            t.wait_ge(cst["sem"].h, cr)
            wgd(t, gC_)
            ins = None
            for q in range(4):
                ins = t.transpose(p.banks[5][0:128, q * 8:q * 8 + 8],
                                  probsT[:, q * 128:(q + 1) * 128], cst["ident_f32"][0:8, 0:8])
            ins.then_inc(st.h, 1)
        ptr_done = st.inc()

        @p.on("vector")
        def _(v, tr=ptr_done, entry=sv.n):
            v.wait_ge(st.h, tr)
            v.wait_ge(sv.h, entry)
            v.tensor_copy(out=ptm[:],
                          in_=p.banks[5][0:128, 0:32].rearrange("p (q e) -> p q e", e=8)).then_inc(sv.h, 1)
        ptm_ev = sv.inc()

        @p.on("vector")
        def _(v, entry=sv.n):
            v.wait_ge(sv.h, entry)
            v.reduce_max(out=m1q[:], in_=ptm[:], axis=AXX).then_inc(sv.h, 1)
            v.wait_ge(sv.h, entry + 1)
            ins = None
            for q in range(4):
                ins = v.tensor_scalar(out=pm[:, q, :], in0=ptm[:, q, :],
                                      scalar1=m1q[:, q:q + 1], scalar2=-1e30,
                                      op0=ALU.is_equal, op1=ALU.mult)
            ins.then_inc(sv.h, 1)
            v.wait_ge(sv.h, entry + 2)
            v.tensor_tensor(out=pm[:], in0=pm[:], in1=ptm[:], op=ALU.add).then_inc(sv.h, 1)
            v.wait_ge(sv.h, entry + 3)
            v.reduce_max(out=m2q[:], in_=pm[:], axis=AXX).then_inc(sv.h, 1)
            v.wait_ge(sv.h, entry + 4)
            ins = None
            for q in range(4):
                ins = v.tensor_scalar(out=cwtm[:, q, :], in0=ptm[:, q, :],
                                      scalar1=m2q[:, q:q + 1], scalar2=None, op0=ALU.is_ge)
            ins.then_inc(sv.h, 1)
            v.wait_ge(sv.h, entry + 5)
            v.tensor_tensor(out=cwtm[:], in0=cwtm[:], in1=ptm[:], op=ALU.mult).then_inc(sv.h, 1)
        cw_done = sv.inc(6)

        @p.on("tensor")
        def _(t, cd=cw_done, bg=g_rbank):
            t.wait_ge(sv.h, cd)
            wgd(t, bg)
            ins = None
            for q in range(4):
                ins = t.transpose(p.banks[5][0:8, q * 128:(q + 1) * 128],
                                  cwtm[:, q, :], cst["ident_f32"][:])
            ins.then_inc(st.h, 1)
        cwtr = st.inc()

        @p.on("vector")
        def _(v, tr=cwtr, entry=sv.n):
            v.wait_ge(st.h, tr)
            v.wait_ge(sv.h, entry)
            v.tensor_copy(out=cwT[:], in_=p.banks[5][0:8, 0:SW]).then_inc(sv.h, 1)
        cwT_ev = sv.inc()
        g_rbank = ("sv", cwT_ev)

        ev_prev = 0
        for eb in range(8):
            @p.on("tensor")
            def _(t, eb=eb, cv=cwT_ev, gC=g_C, evp=ev_prev):
                t.wait_ge(sv.h, cv)
                if eb == 0:
                    wgd(t, gC)
                else:
                    t.wait_ge(sv.h, evp)
                t.matmul(p.banks[4][:, 0:SW], e8[:, eb, :], cwT[:],
                         start=True, stop=True).then_inc(st.h, 1)
            bmm = st.inc()

            @p.on("vector")
            def _(v, eb=eb, mmd=bmm, entry=sv.n):
                v.wait_ge(st.h, mmd)
                v.wait_ge(sv.h, entry)
                v.tensor_copy(out=cwba[:, eb, :], in_=p.banks[4][:, 0:SW]).then_inc(sv.h, 1)
            ev_prev = sv.inc()
        g_C = ("sv", ev_prev)
        cwba_done = ev_prev

        def emit_down(e_i):
            hb = e_i % 2
            last_dn = 0
            for ot in range(CT):
                bk = 4 + (ot % 2)

                @p.on("tensor")
                def _(t, e_i=e_i, ot=ot, bk=bk, hb=hb, hd=g_hid_ready[hb], wd_=g_wd_ld[hb],
                      gD=g_D[ot % 2]):
                    t.wait_ge(sv.h, hd)
                    t.wait_ge(s_wd.h, wd_)
                    wgd(t, gD)
                    ins = None
                    for kt in range(IT):
                        ins = t.matmul(p.banks[bk][:, 0:SW], wd_sb[hb][:, kt, ot * 128:(ot + 1) * 128],
                                       hid[hb][:, kt, :], start=(kt == 0), stop=(kt == IT - 1))
                    ins.then_inc(st.h, 1)
                dn_done = st.inc()
                last_dn = dn_done

                @p.on("vector")
                def _(v, e_i=e_i, ot=ot, bk=bk, mmd=dn_done, entry=sv.n, og=g_out):
                    v.wait_ge(st.h, mmd)
                    v.wait_ge(sv.h, entry)
                    if e_i == 0 and og:
                        v.wait_ge(so.h, og)
                    if e_i == 0:
                        v.tensor_copy(out=out_sb[:, ot, :], in_=p.banks[bk][:, 0:SW]).then_inc(sv.h, 1)
                    else:
                        v.tensor_tensor(out=out_sb[:, ot, :], in0=p.banks[bk][:, 0:SW],
                                        in1=out_sb[:, ot, :], op=ALU.add).then_inc(sv.h, 1)
                g_D[ot % 2] = ("sv", sv.inc())
            g_wd[hb] = last_dn
            g_hid[hb] = last_dn

        g_hid_ready = [0, 0]
        g_wd_ld = [0, 0]
        for e_i in range(NE):
            b = e_i % 2

            @p.on("gpsimd")
            def _(e, e_i=e_i, b=b, need=g_wgu[0], pre=s_wg.n):
                if need:
                    e.wait_ge(st.h, need)
                dmaw(e, s_wg, pre, out=wg_sb[b][:], in_=wgt[e_i].rearrange("(kt p) i -> p kt i", p=128))
            wgld = s_wg.inc(16)

            @p.on("gpsimd")
            def _(e, e_i=e_i, b=b, need=g_wgu[0], pre=s_wu.n):
                if need:
                    e.wait_ge(st.h, need)
                dmaw(e, s_wu, pre, out=wu_sb[b][:], in_=wut[e_i].rearrange("(kt p) i -> p kt i", p=128))
            wuld = s_wu.inc(16)

            @p.on("gpsimd")
            def _(e, e_i=e_i, b=b, need=g_wd[b], pre=s_wd.n):
                if need:
                    e.wait_ge(st.h, need)
                dmaw(e, s_wd, pre, out=wd_sb[b][:], in_=wdt[e_i].rearrange("(kt p) h -> p kt h", p=128))
            g_wd_ld[b] = s_wd.inc(16)

            last_hid = 0
            for mt in range(IT):
                bkg = mt % 2
                bku = 2 + mt % 2

                @p.on("tensor")
                def _(t, e_i=e_i, b=b, mt=mt, bkg=bkg, bku=bku, wg_=wgld, wu_=wuld,
                      gA=g_AB[bkg], gB=g_AB2[mt % 2],
                      xld=xld, hg=(g_hid[b] if mt == 0 else 0)):
                    t.wait_ge(s_wg.h, wg_)
                    t.wait_ge(s_wu.h, wu_)
                    t.wait_ge(s_x.h, xld)
                    if hg:
                        t.wait_ge(st.h, hg)
                    wgd(t, gA)
                    for kt in range(CT):
                        t.matmul(p.banks[bkg][:, 0:SW], wg_sb[b][:, kt, mt * 128:(mt + 1) * 128],
                                 xb[:, kt, :], start=(kt == 0), stop=(kt == CT - 1))
                    wgd(t, gB)
                    ins = None
                    for kt in range(CT):
                        ins = t.matmul(p.banks[bku][:, 0:SW], wu_sb[b][:, kt, mt * 128:(mt + 1) * 128],
                                       xb[:, kt, :], start=(kt == 0), stop=(kt == CT - 1))
                    ins.then_inc(st.h, 1)
                gu_done = st.inc()

                @p.on("scalar")
                def _(s, bkg=bkg, mmd=gu_done, entry=ss.n, vg=sv.n):
                    s.wait_ge(st.h, mmd)
                    s.wait_ge(ss.h, entry)
                    s.wait_ge(sv.h, vg)
                    s.activation(out=sil[:], in_=p.banks[bkg][:, 0:SW], func=AF.Silu).then_inc(ss.h, 1)
                sil_done = ss.inc()
                g_AB[bkg] = ("ss", sil_done)

                @p.on("vector")
                def _(v, e_i=e_i, b=b, mt=mt, bku=bku, sd_=sil_done, entry=sv.n, cbd=cwba_done,
                      hg=(g_hid[b] if mt == 0 else 0)):
                    v.wait_ge(ss.h, sd_)
                    v.wait_ge(sv.h, entry)
                    if hg:
                        v.wait_ge(st.h, hg)
                    if e_i == 0:
                        v.tensor_tensor(out=hid[b][:, mt, :], in0=sil[:], in1=p.banks[bku][:, 0:SW],
                                        op=ALU.mult).then_inc(sv.h, 1)
                    else:
                        v.tensor_tensor(out=sil[:], in0=sil[:], in1=p.banks[bku][:, 0:SW],
                                        op=ALU.mult).then_inc(sv.h, 1)
                        v.wait_ge(sv.h, entry + 1)
                        v.tensor_tensor(out=hid[b][:, mt, :], in0=sil[:], in1=cwba[:, e_i - 1, :],
                                        op=ALU.mult).then_inc(sv.h, 1)
                nops = 1 if e_i == 0 else 2
                last_hid = sv.inc(nops)
                g_AB2[mt % 2] = ("sv", last_hid)
            g_wgu[0] = st.n
            g_hid_ready[b] = last_hid

            if e_i > 0:
                emit_down(e_i - 1)
        emit_down(NE - 1)
        acc_done = sv.n

        @p.on("scalar")
        def _(s, ad=acc_done, entry=ss.n):
            s.wait_ge(sv.h, ad)
            s.wait_ge(ss.h, entry)
            ins = None
            for ct in range(CT):
                ins = s.activation(out=out_sb[:, ct, :], in_=out_sb[:, ct, :], func=AF.Silu)
            ins.then_inc(ss.h, 1)
        osil = ss.inc()

        @p.on("sync")
        def _(e, t0=t0, od=osil, pre=so.n):
            e.wait_ge(ss.h, od)
            dmaw(e, so, pre, out=gr[:, :, t0:t0 + SCW], in_=out_sb[:])
        g_out = so.inc(16)
        g_xb = st.n

    @p.on("sync")
    def _(e, need=so.n):
        e.wait_ge(so.h, need)


def build_s5(nsc_lim=None):
    p = Prog(); p.mkdum(); p.init_psum(6, 2)
    xnT = p.dram("xnT", [C, S, T], F32, kind="ExternalInput")
    wgt = p.dram("wgt", [9, C, 2048], F32, kind="ExternalInput")
    wut = p.dram("wut", [9, C, 2048], F32, kind="ExternalInput")
    wdt = p.dram("wdt", [9, 2048, C], F32, kind="ExternalInput")
    wrt = p.dram("wrt", [C, 8], F32, kind="ExternalInput")
    rb = p.dram("rb", [8, 1], F32, kind="ExternalInput")
    g1T = p.dram("g1T", [C, S, T], F32, kind="ExternalOutput")
    cst = emit_consts(p)
    with p.stage():
        emit_s5(p, cst, xnT, wgt, wut, wdt, wrt, rb, g1T, nsc_lim)
    return p.finish()


def emit_s4(p, cst, xnT, qkvw, outw, w1, w2, qkvb, outb, b1c, b2c, g1c, be1c, g2c, be2c,
            sT, nseq=S, ncph=None):
    xr = xnT.rearrange("(ct p) s t -> p ct s t", p=128)
    sr = sT.rearrange("(ct p) s t -> p ct s t", p=128)

    CW = 512
    NCH = T // CW if ncph is None else ncph
    NH = 16

    wq = p.sbuf("wq", [128, 8, 1024], BF16)
    w1v = [wq[:, 0, :].rearrange("p (k c) -> p k c", k=8),
           wq[:, 1, :].rearrange("p (k c) -> p k c", k=8)]
    wse = wq[:, 4:8, :].rearrange("p a b -> p (a b)").rearrange("p (k c) -> p k c", k=32)
    parms = p.sbuf("parms", [128, 104], F32)
    qfT = p.sbuf("qfT", [128, CT, T], BF16)
    ktm = p.sbuf("ktm", [128, 4, C], BF16)
    vtm = p.sbuf("vtm", [128, 4, C], BF16)
    kvbf = p.sbuf("kvbf", [128, 8, 64], BF16)
    zbf = p.sbuf("zbf", [128, CT, 16], BF16)
    z_sb = p.sbuf("z_sb", [128, CT], F32)
    z_pt = p.sbuf("z_pt", [128, CT], F32)
    outw_sb = p.sbuf("outw_sb", [128, CT, C], BF16)
    e16 = p.sbuf("e16", [16, 8, 2, 64], BF16)
    xf = p.sbuf("xf", [128, CT, CW], F32)
    xbfc = p.sbuf("xbfc", [128, CT, CW], BF16)
    sq = p.sbuf("sq", [128, CT, CW], BF16)
    h1b = p.sbuf("h1b", [128, CT, CW], BF16)
    tmpf = p.sbuf("tmpf", [128, CT, CW], F32)
    rm = p.sbuf("rm", [128, CW], F32)
    rr = p.sbuf("rr", [128, CW], F32)
    emax = rm
    emin = rr
    rden = p.sbuf("rden", [16, CW], BF16)
    rdf = p.sbuf("rdf", [16, CW], F32)
    rbc = p.sbuf("rbc", [128, CW], BF16)
    xA = p.sbuf("xA", [128, CT, CW], F32)
    h2b = p.sbuf("h2b", [128, CT, CW], BF16)
    ffb = p.sbuf("ffb", [128, 32, CW], BF16)
    kfc = ffb[:, 0:8, :]
    vtc = ffb[:, 8:16, :]
    yT = ffb[:, 16:24, :]
    sff = tmpf

    s_x = p.sem("s_x")
    s_xb = p.sem("s_xb")
    s_w = p.sem("s_w")
    s_w2 = p.sem("s_w2")
    s_pp = p.sem("s_pp")
    so = p.sem("so")
    sv = p.sem("v")
    ss = p.sem("s")
    st = p.sem("t")
    s_e16 = p.sem("s_e16")
    sems = dict(v=sv, s=ss, t=st)
    semmap = dict(sv=sv, ss=ss, st=st)

    def wgd(t, g):
        nm, val = g
        if val:
            t.wait_ge(semmap[nm].h, val)

    OFF_QKVB, OFF_B1, OFF_OUTB, OFF_B2, OFF_G1, OFF_BE1, OFF_G2, OFF_BE2 = 0, 24, 56, 64, 72, 80, 88, 96

    @p.on("sync")
    def _(e):
        dmaw(e, s_pp, 0, out=parms[:, 0:24], in_=qkvb)
        dmaw(e, s_pp, 16, out=parms[:, 24:56], in_=b1c)
        dmaw(e, s_pp, 32, out=parms[:, 56:64], in_=outb)
        dmaw(e, s_pp, 48, out=parms[:, 64:72], in_=b2c)
        dmaw(e, s_pp, 64, out=parms[:, 72:80], in_=g1c)
        dmaw(e, s_pp, 80, out=parms[:, 80:88], in_=be1c)
        dmaw(e, s_pp, 96, out=parms[:, 88:96], in_=g2c)
        dmaw(e, s_pp, 112, out=parms[:, 96:104], in_=be2c)
    pld = s_pp.inc(128)

    @p.on("gpsimd")
    def _(e):
        e.iota(e16[:], pattern=[[2, 8], [1, 2], [0, 64]], channel_multiplier=-1,
               allow_small_or_imprecise_dtypes=True).then_inc(s_e16.h, 1)
        dmaw(e, s_w2, 0, out=outw_sb[:], in_=outw.rearrange("(kt p) o -> p kt o", p=128))
    e16i = s_e16.inc()
    ow_ld = s_w2.inc(16)

    @p.on("vector")
    def _(v, ei=e16i, cr=cst["ready"], pl=pld):
        v.wait_ge(s_e16.h, ei)
        v.wait_ge(cst["sem"].h, cr)
        v.wait_ge(s_pp.h, pl)
        v.tensor_scalar(out=e16[:], in0=e16[:], scalar1=0.0, scalar2=None,
                        op0=ALU.is_equal).then_inc(sv.h, 1)
    e16_done = sv.inc()

    pcol = lambda off, i: parms[:, off + i:off + i + 1]

    g_w = ("st", 0)
    g_w2 = ("st", 0)
    g_AB = [("ss", 0), ("ss", 0)]
    g_DEN = ("sv", 0)
    g_BC = ("sv", 0)
    g_T = [("sv", 0), ("sv", 0)]
    g_sq = 0
    g_tm = 0
    g_xslot = ("sv", 0)
    g_store = 0
    g_qf = 0
    g_h1b = 0
    g_h2b = 0
    g_ffb = 0
    g_rden = 0
    g_w1s = [0, 0]
    g_B23 = [("sv", 0), ("sv", 0)]

    for sqi in range(nseq):
        g_tm = st.n
        for ch in range(NCH):
            t0 = ch * CW

            @p.on("sync")
            def _(e, sqi=sqi, t0=t0, gx=g_xslot, pre=s_x.n):
                nm, val = gx
                if val:
                    e.wait_ge(semmap[nm].h, val)
                dmaw(e, s_x, pre, out=xf[:], in_=xr[:, :, sqi, t0:t0 + CW])
            xld = s_x.inc(16)

            @p.on("gpsimd")
            def _(e, sqi=sqi, t0=t0, need=g_sq, pre=s_xb.n):
                if need:
                    e.wait_ge(st.h, need)
                dmaw(e, s_xb, pre, out=xbfc[:], in_=xr[:, :, sqi, t0:t0 + CW])
            xbld = s_xb.inc(16)

            @p.on("vector")
            def _(v, nx=xld, nb=xbld, ed=e16_done):
                v.wait_ge(s_x.h, nx)
                v.wait_ge(s_xb.h, nb)
                v.wait_ge(sv.h, ed)
                v.memset(p.dumv[:], 0.0).then_inc(sv.h, 1)
            bar_v = sv.inc()

            @p.on("scalar")
            def _(s, nx=xld, nb=xbld):
                s.wait_ge(s_x.h, nx)
                s.wait_ge(s_xb.h, nb)
                s.memzero(p.dums[:]).then_inc(ss.h, 1)
            bar_s = ss.inc()

            r = ln_chunk(
                p, cst, CW,
                xf=lambda ct: xf[:, ct, :],
                xb=lambda ct: xbfc[:, ct, :],
                sq=lambda ct: sq[:, ct, :],
                tmp=lambda ct: tmpf[:, ct, :],
                rm=rm[:], rr=rr[:], sems=sems,
                bank_a=p.banks[4], bank_b=p.banks[5],
                out_bf=lambda ct: h1b[:, ct, :],
                gcol=lambda ct: pcol(OFF_G1, ct), bcol=lambda ct: pcol(OFF_BE1, ct),
                bank_guard=max(g_DEN[1], g_BC[1]), sq_guard=g_sq,
                x_dep=(ss.h, bar_s),
                out_guard=[(st.h, g_h1b), (so.h, g_store)],
            )
            g_DEN = ("sv", r["var"]); g_BC = ("sv", r["var"])
            g_sq = r["stats"]
            h1_done_s = r["s"]
            g_xslot = ("sv", r["v"])

            for part in range(3):
                @p.on("gpsimd")
                def _(e, part=part, gw=g_w, pre=s_w.n):
                    nm, val = gw
                    if val:
                        e.wait_ge(semmap[nm].h, val)
                    dmaw(e, s_w, pre, out=wq[:],
                         in_=qkvw.rearrange("(kt p) o -> p kt o", p=128)[:, :, part * 1024:(part + 1) * 1024])
                wld = s_w.inc(16)

                for m8 in range(CT):
                    mt = part * 8 + m8
                    bk = m8 % 2

                    @p.on("tensor")
                    def _(t, m8=m8, bk=bk, wld=wld, hd=h1_done_s, gb=g_AB[bk]):
                        t.wait_ge(s_w.h, wld)
                        t.wait_ge(ss.h, hd)
                        wgd(t, gb)
                        ins = None
                        for kt in range(CT):
                            ins = t.matmul(p.banks[bk][:, 0:CW],
                                           wq[:, kt, m8 * 128:(m8 + 1) * 128],
                                           h1b[:, kt, :], start=(kt == 0), stop=(kt == CT - 1))
                        ins.then_inc(st.h, 1)
                    mm_done = st.inc()
                    g_w = ("st", mm_done)

                    if part < 2:
                        if part == 0:
                            dst = (lambda m8=m8, t0=t0: qfT[:, m8, t0:t0 + CW])
                        else:
                            dst = (lambda m8=m8: kfc[:, m8, :])

                        @p.on("vector")
                        def _(v, mt=mt, bk=bk, mmd=mm_done, entry=sv.n,
                              qg=(g_qf if part == 0 else g_tm)):
                            v.wait_ge(st.h, mmd)
                            v.wait_ge(sv.h, entry)
                            if qg:
                                v.wait_ge(st.h, qg)
                            v.tensor_scalar(out=emax[:], in0=p.banks[bk][:, 0:CW],
                                            scalar1=pcol(OFF_QKVB, mt), scalar2=0.0,
                                            op0=ALU.add, op1=ALU.max).then_inc(sv.h, 1)
                            v.wait_ge(sv.h, entry + 1)
                            v.tensor_scalar(out=emin[:], in0=p.banks[bk][:, 0:CW],
                                            scalar1=pcol(OFF_QKVB, mt), scalar2=0.0,
                                            op0=ALU.add, op1=ALU.min).then_inc(sv.h, 1)
                        fm1 = sv.inc(2)

                        @p.on("scalar")
                        def _(s, fm=fm1, entry=ss.n):
                            s.wait_ge(sv.h, fm)
                            s.wait_ge(ss.h, entry)
                            s.activation(out=emin[:], in_=emin[:], func=AF.Exp).then_inc(ss.h, 1)
                        ex_done = ss.inc()

                        @p.on("vector")
                        def _(v, d=dst, xd=ex_done, entry=sv.n):
                            v.wait_ge(ss.h, xd)
                            v.wait_ge(sv.h, entry)
                            v.tensor_tensor(out=d(), in0=emax[:], in1=emin[:], op=ALU.add).then_inc(sv.h, 1)
                        g_AB[bk] = ("sv", sv.inc())
                    else:
                        @p.on("vector")
                        def _(v, mt=mt, m8=m8, bk=bk, mmd=mm_done, entry=sv.n, tg=g_tm):
                            v.wait_ge(st.h, mmd)
                            v.wait_ge(sv.h, entry)
                            if tg:
                                v.wait_ge(st.h, tg)
                            v.tensor_scalar(out=vtc[:, m8, :], in0=p.banks[bk][:, 0:CW],
                                            scalar1=pcol(OFF_QKVB, mt), scalar2=None,
                                            op0=ALU.add).then_inc(sv.h, 1)
                        g_AB[bk] = ("sv", sv.inc())
            fm_all = sv.n
            g_h1b = st.n

            @p.on("vector")
            def _(v, ch=ch, entry=sv.n):
                v.wait_ge(sv.h, entry)
                if ch == 0:
                    v.reduce_sum(out=z_sb[:], in_=kfc[:, :, :], axis=AXX).then_inc(sv.h, 1)
                else:
                    v.reduce_sum(out=z_pt[:], in_=kfc[:, :, :], axis=AXX).then_inc(sv.h, 1)
                    v.wait_ge(sv.h, entry + 1)
                    v.tensor_tensor(out=z_sb[:], in0=z_sb[:], in1=z_pt[:], op=ALU.add).then_inc(sv.h, 1)
            zacc = sv.inc(1 if ch == 0 else 2)

            for which in range(2):
                src = kfc if which == 0 else vtc
                dtm = ktm if which == 0 else vtm
                for tt in range(4):
                    tb = (which * 4 + tt) % 2

                    @p.on("tensor")
                    def _(t, src=src, tt=tt, tb=tb, fa=fm_all, gT_=g_T[tb], cr=cst["ready"]):
                        t.wait_ge(sv.h, fa)
                        wgd(t, gT_)
                        ins = None
                        for ct in range(CT):
                            ins = t.transpose(p.tbanks[tb][:, ct * 128:(ct + 1) * 128],
                                              src[:, ct, tt * 128:(tt + 1) * 128], cst["ident_bf"][:])
                        ins.then_inc(st.h, 1)
                    trd = st.inc()

                    @p.on("vector")
                    def _(v, dtm=dtm, tt=tt, tb=tb, trd=trd, entry=sv.n, tg=g_tm):
                        v.wait_ge(st.h, trd)
                        v.wait_ge(sv.h, entry)
                        if tg:
                            v.wait_ge(st.h, tg)
                        v.tensor_copy(out=dtm[:, tt, :], in_=p.tbanks[tb][:, 0:1024]).then_inc(sv.h, 1)
                    g_T[tb] = ("sv", sv.inc())
            tm_done = sv.n

            @p.on("tensor")
            def _(t, ch=ch, td=tm_done, gKV=g_B23[0], gZ=g_B23[1], cr=cst["ready"]):
                t.wait_ge(sv.h, td)
                if ch == 0:
                    wgd(t, gKV)
                    wgd(t, gZ)
                ins = None
                for j in range(CT):
                    bkv = 2 if j < 4 else 3
                    c0 = (j % 4) * 128
                    for tt in range(4):
                        ins = t.matmul(p.banks[bkv][:, c0:c0 + 128],
                                       ktm[:, tt, j * 128:(j + 1) * 128],
                                       vtm[:, tt, j * 128:(j + 1) * 128],
                                       start=(ch == 0 and tt == 0 and j % 4 == 0),
                                       stop=(ch == NCH - 1 and tt == 3 and j % 4 == 3))
                ins.then_inc(st.h, 1)
            g_tm = kv_mm = st.inc()

        @p.on("vector")
        def _(v, kd=kv_mm, entry=sv.n):
            v.wait_ge(st.h, kd)
            v.wait_ge(sv.h, entry)
            ins = None
            for h in range(NH):
                j = h // 2
                bkv = 2 if j < 4 else 3
                c0 = (j % 4) * 128 + (h % 2) * 64
                pb_ = (h % 2) * 64
                ins = v.tensor_copy(out=kvbf[pb_:pb_ + 64, j, :],
                                    in_=p.banks[bkv][pb_:pb_ + 64, c0:c0 + 64])
            ins.then_inc(sv.h, 1)
            v.wait_ge(sv.h, entry + 1)
            v.memset(zbf[:], 0.0).then_inc(sv.h, 1)
            v.wait_ge(sv.h, entry + 2)
            ins = None
            for h in range(NH):
                pb_ = (h % 2) * 64
                j = h // 2
                ins = v.tensor_copy(out=zbf[pb_:pb_ + 64, j, h:h + 1], in_=z_sb[pb_:pb_ + 64, j:j + 1])
            ins.then_inc(sv.h, 1)
        kvz_ev = sv.inc(3)
        g_B23 = [("sv", kvz_ev), ("sv", kvz_ev)]

        for ch in range(NCH):
            t0 = ch * CW

            @p.on("sync")
            def _(e, sqi=sqi, t0=t0, gx=g_xslot, pre=s_x.n, kv=kvz_ev):
                nm, val = gx
                if val:
                    e.wait_ge(semmap[nm].h, val)
                e.wait_ge(sv.h, kv)
                dmaw(e, s_x, pre, out=xf[:], in_=xr[:, :, sqi, t0:t0 + CW])
            xld = s_x.inc(16)

            @p.on("tensor")
            def _(t, t0=t0, xld=xld, kv=kvz_ev, gD=g_DEN):
                t.wait_ge(sv.h, kv)
                wgd(t, gD)
                ins = None
                for dt in range(CT):
                    ins = t.matmul(p.banks[4][0:16, 0:CW], zbf[:, dt, :],
                                   qfT[:, dt, t0:t0 + CW], start=(dt == 0), stop=(dt == CT - 1))
                ins.then_inc(st.h, 1)
            den_mm = st.inc()

            @p.on("vector")
            def _(v, dm=den_mm, entry=sv.n, rg=g_rden):
                v.wait_ge(st.h, dm)
                v.wait_ge(sv.h, entry)
                if rg:
                    v.wait_ge(st.h, rg)
                v.tensor_scalar(out=rdf[:], in0=p.banks[4][0:16, 0:CW], scalar1=1e-6,
                                scalar2=None, op0=ALU.max).then_inc(sv.h, 1)
                v.wait_ge(sv.h, entry + 1)
                v.reciprocal(out=rdf[:], in_=rdf[:]).then_inc(sv.h, 1)
                v.wait_ge(sv.h, entry + 2)
                v.tensor_copy(out=rden[:], in_=rdf[:]).then_inc(sv.h, 1)
            rden_done = sv.inc(3)
            g_DEN = ("sv", rden_done)

            for j in range(CT):
                bkY = 2 + (j % 2)

                @p.on("tensor")
                def _(t, j=j, t0=t0, bkY=bkY, rd=rden_done, gY=g_B23[j % 2], gBC=g_BC):
                    t.wait_ge(sv.h, rd)
                    wgd(t, gY)
                    wgd(t, gBC)
                    for hh in range(2):
                        h = 2 * j + hh
                        t.matmul(p.banks[bkY][hh * 64:hh * 64 + 64, 0:CW],
                                 kvbf[hh * 64:hh * 64 + 64, j, :],
                                 qfT[hh * 64:hh * 64 + 64, j, t0:t0 + CW],
                                 start=True, stop=True)
                    t.matmul(p.banks[5][:, 0:CW], e16[:, j, :, :].rearrange("p a b -> p (a b)"),
                             rden[:], start=True, stop=True).then_inc(st.h, 1)
                y_mm = st.inc()

                @p.on("vector")
                def _(v, j=j, bkY=bkY, ym=y_mm, entry=sv.n, fg=(g_ffb if j == 0 else 0)):
                    v.wait_ge(st.h, ym)
                    v.wait_ge(sv.h, entry)
                    if fg:
                        v.wait_ge(st.h, fg)
                    v.tensor_copy(out=rbc[:], in_=p.banks[5][:, 0:CW]).then_inc(sv.h, 1)
                    v.wait_ge(sv.h, entry + 1)
                    v.tensor_tensor(out=yT[:, j, :], in0=p.banks[bkY][:, 0:CW], in1=rbc[:],
                                    op=ALU.mult).then_inc(sv.h, 1)
                done = sv.inc(2)
                g_B23[j % 2] = ("sv", done)
                g_BC = ("sv", done)
            y_all = sv.n
            g_rden = st.n

            ao_done_v = 0
            for mt in range(CT):
                bk = mt % 2

                @p.on("tensor")
                def _(t, mt=mt, bk=bk, ya=y_all, wld=ow_ld, gb=g_AB[bk]):
                    t.wait_ge(sv.h, ya)
                    t.wait_ge(s_w2.h, wld)
                    wgd(t, gb)
                    ins = None
                    for kt in range(CT):
                        ins = t.matmul(p.banks[bk][:, 0:CW], outw_sb[:, kt, mt * 128:(mt + 1) * 128],
                                       yT[:, kt, :], start=(kt == 0), stop=(kt == CT - 1))
                    ins.then_inc(st.h, 1)
                ao_mm = st.inc()

                @p.on("vector")
                def _(v, mt=mt, bk=bk, mmd=ao_mm, entry=sv.n, xl=xld):
                    v.wait_ge(st.h, mmd)
                    v.wait_ge(sv.h, entry)
                    v.wait_ge(s_x.h, xl)
                    v.tensor_scalar(out=xA[:, mt, :], in0=p.banks[bk][:, 0:CW],
                                    scalar1=pcol(OFF_OUTB, mt), scalar2=None,
                                    op0=ALU.add).then_inc(sv.h, 1)
                    v.wait_ge(sv.h, entry + 1)
                    v.tensor_tensor(out=xA[:, mt, :], in0=xA[:, mt, :], in1=xf[:, mt, :],
                                    op=ALU.add).then_inc(sv.h, 1)
                ao_done_v = sv.inc(2)
                g_AB[bk] = ("sv", ao_done_v)
            g_xslot = ("sv", ao_done_v)

            @p.on("scalar")
            def _(s, ad=ao_done_v):
                s.wait_ge(sv.h, ad)
                s.memzero(p.dums[:]).then_inc(ss.h, 1)
            xA_bar_s = ss.inc()

            r2 = ln_chunk(
                p, cst, CW,
                xf=lambda ct: xA[:, ct, :],
                xb=None,
                sq=lambda ct: sq[:, ct, :],
                tmp=lambda ct: tmpf[:, ct, :],
                rm=rm[:], rr=rr[:], sems=sems,
                bank_a=p.banks[4], bank_b=p.banks[5],
                out_bf=lambda ct: h2b[:, ct, :],
                gcol=lambda ct: pcol(OFF_G2, ct), bcol=lambda ct: pcol(OFF_BE2, ct),
                bank_guard=max(g_DEN[1], g_BC[1]), sq_guard=g_sq,
                x_dep=(ss.h, xA_bar_s),
                xb_scr=lambda ct: xbfc[:, ct, :],
                out_guard=[(st.h, g_h2b), (so.h, g_store)],
            )
            g_DEN = ("sv", r2["var"]); g_BC = ("sv", r2["var"])
            g_sq = r2["stats"]
            h2_done_s = r2["s"]

            for mt in range(32):
                bk = mt % 2
                sl = mt % 2

                @p.on("gpsimd")
                def _(e, mt=mt, sl=sl, gw=(g_w if mt < 2 else ("st", g_w1s[sl])), pre=s_w.n):
                    nm, val = gw
                    if val:
                        e.wait_ge(semmap[nm].h, val)
                    dmaw(e, s_w, pre, out=w1v[sl][:],
                         in_=w1.rearrange("(kt p) o -> p kt o", p=128)[:, :, mt * 128:(mt + 1) * 128])
                w1ld = s_w.inc(16)

                @p.on("tensor")
                def _(t, mt=mt, bk=bk, sl=sl, hd=h2_done_s, wld=w1ld, gb=g_AB[bk]):
                    t.wait_ge(ss.h, hd)
                    t.wait_ge(s_w.h, wld)
                    wgd(t, gb)
                    ins = None
                    for kt in range(CT):
                        ins = t.matmul(p.banks[bk][:, 0:CW], w1v[sl][:, kt, :],
                                       h2b[:, kt, :], start=(kt == 0), stop=(kt == CT - 1))
                    ins.then_inc(st.h, 1)
                f1_mm = st.inc()
                g_w1s[sl] = f1_mm
                g_w = ("st", f1_mm)

                @p.on("scalar")
                def _(s, mt=mt, bk=bk, mmd=f1_mm, entry=ss.n, fg=(g_ffb if mt == 0 else 0)):
                    s.wait_ge(st.h, mmd)
                    s.wait_ge(ss.h, entry)
                    if fg:
                        s.wait_ge(st.h, fg)
                    s.activation(out=ffb[:, mt, :], in_=p.banks[bk][:, 0:CW], func=AF.Gelu,
                                 bias=pcol(OFF_B1, mt)).then_inc(ss.h, 1)
                g_AB[bk] = ("ss", ss.inc())
            ff1_done = ss.n
            g_h2b = st.n

            for mt in range(CT):
                bk = mt % 2

                @p.on("gpsimd")
                def _(e, mt=mt, gw=g_w2, pre=s_w2.n, need=st.n):
                    nm, val = gw
                    if val:
                        e.wait_ge(semmap[nm].h, val)
                    dmaw(e, s_w2, pre, out=wse[:, :, :],
                         in_=w2.rearrange("(kt p) o -> p kt o", p=128)[:, :, mt * 128:(mt + 1) * 128])
                w2ld = s_w2.inc(16)

                @p.on("tensor")
                def _(t, mt=mt, bk=bk, fd=ff1_done, wld=w2ld, gb=g_AB[bk]):
                    t.wait_ge(ss.h, fd)
                    t.wait_ge(s_w2.h, wld)
                    wgd(t, gb)
                    ins = None
                    for kt in range(32):
                        ins = t.matmul(p.banks[bk][:, 0:CW], wse[:, kt, :],
                                       ffb[:, kt, :], start=(kt == 0), stop=(kt == 31))
                    ins.then_inc(st.h, 1)
                f2_mm = st.inc()
                g_w2 = ("st", f2_mm)

                @p.on("vector")
                def _(v, mt=mt, bk=bk, mmd=f2_mm, entry=sv.n, sg=g_store):
                    v.wait_ge(st.h, mmd)
                    v.wait_ge(sv.h, entry)
                    if mt == 0 and sg:
                        v.wait_ge(so.h, sg)
                    v.tensor_scalar(out=sff[:, mt, :], in0=p.banks[bk][:, 0:CW],
                                    scalar1=pcol(OFF_B2, mt), scalar2=None,
                                    op0=ALU.add).then_inc(sv.h, 1)
                    v.wait_ge(sv.h, entry + 1)
                    v.tensor_tensor(out=sff[:, mt, :], in0=sff[:, mt, :], in1=xA[:, mt, :],
                                    op=ALU.add).then_inc(sv.h, 1)
                g_AB[bk] = ("sv", sv.inc(2))
            s_done = sv.n

            @p.on("sync")
            def _(e, sqi=sqi, t0=t0, sd_=s_done, pre=so.n):
                e.wait_ge(sv.h, sd_)
                dmaw(e, so, pre, out=sr[:, :, sqi, t0:t0 + CW], in_=sff[:])
            g_store = so.inc(16)
            g_ffb = st.n
            g_w = ("st", st.n)
        g_qf = st.n

    @p.on("sync")
    def _(e, need=so.n):
        e.wait_ge(so.h, need)


def build_s4(nseq=S, ncph=None):
    p = Prog(); p.mkdum(); p.init_psum(6, 2)
    xnT = p.dram("xnT", [C, S, T], F32, kind="ExternalInput")
    qkvw = p.dram("qkvw", [C, 3072], F32, kind="ExternalInput")
    outw = p.dram("outw", [C, C], F32, kind="ExternalInput")
    w1 = p.dram("w1", [C, 4096], F32, kind="ExternalInput")
    w2 = p.dram("w2", [4096, C], F32, kind="ExternalInput")
    qkvb = p.dram("qkvb", [128, 24], F32, kind="ExternalInput")
    outb = p.dram("outb", [128, CT], F32, kind="ExternalInput")
    b1c = p.dram("b1c", [128, 32], F32, kind="ExternalInput")
    b2c = p.dram("b2c", [128, CT], F32, kind="ExternalInput")
    g1c = p.dram("g1c", [128, CT], F32, kind="ExternalInput")
    be1c = p.dram("be1c", [128, CT], F32, kind="ExternalInput")
    g2c = p.dram("g2c", [128, CT], F32, kind="ExternalInput")
    be2c = p.dram("be2c", [128, CT], F32, kind="ExternalInput")
    sT = p.dram("sT", [C, S, T], F32, kind="ExternalOutput")
    cst = emit_consts(p)
    with p.stage():
        emit_s4(p, cst, xnT, qkvw, outw, w1, w2, qkvb, outb, b1c, b2c, g1c, be1c, g2c, be2c,
                sT, nseq, ncph)
    return p.finish()


def emit_s1(p, cst, xnT, wk, wv, wr, wo, fwk, fwr, fwv, pvec, sT, out1T, s1T, cmT, nseq=S):
    xr = xnT.rearrange("(ct p) s t -> p ct s t", p=128)
    o1r = out1T.rearrange("(ct p) s t -> p ct s t", p=128)
    s1r = s1T.rearrange("(ct p) s t -> p ct s t", p=128)
    cmr = cmT.rearrange("(ct p) s t -> p ct s t", p=128)
    srr = sT.rearrange("(ct p) s t -> p ct s t", p=128)

    CW = 256
    NCH = T // CW
    TC = 256
    NLC = T // TC

    P_LN1S, P_LN1B, P_LN2S, P_LN2B, P_TMK, P_TMV, P_TMR, P_FTMK, P_FTMR, P_WDEC, P_UF = range(11)

    wbig = p.sbuf("wbig", [128, CT, 4096], BF16)
    fwrs = p.sbuf("fwrs", [128, CT, C], BF16)
    fwvs = p.sbuf("fwvs", [128, 32, 128], BF16)
    parms = p.sbuf("parms", [128, 11 * CT], F32)
    acol = p.sbuf("acol", [128, 2 * CT], F32)
    xnf = p.sbuf("xnf", [128, CT, CW], F32)
    xnb = p.sbuf("xnb", [128, CT, CW], BF16)
    sq = p.sbuf("sq", [128, CT, CW], BF16)
    h1f = p.sbuf("h1f", [128, CT, CW + 1], F32)
    scr_bf = p.sbuf("scr_bf", [128, 32, CW], BF16)
    shared = p.sbuf("shared", [128, CT, 5, CW + 1], F32)
    x2b = p.sbuf("x2b", [128, 2 * CT, CW], BF16)
    rm = p.sbuf("rm", [128, CW], F32)
    rr = p.sbuf("rr", [128, CW], F32)
    mst = p.sbuf("mst", [128, 32], F32)
    carr = p.sbuf("carr", [128, CT, 3], F32)

    ekf = lambda ct: shared[:, ct, 0, 0:CW]
    vvf = lambda ct: shared[:, ct, 1, 0:CW]
    bbf = lambda ct: shared[:, ct, 2, 0:CW]
    lifbuf = shared[:, :, 0:S, 0:TC]
    spk = scr_bf.rearrange("p (ct s) t -> p ct s t", ct=CT)
    gatef = lambda ct: shared[:, ct, 0, 0:CW]
    xAf = lambda ct: shared[:, ct, 1, 0:CW]

    s_x = p.sem("s_x")
    s_xb = p.sem("s_xb")
    s_w = p.sem("s_w")
    s_w2 = p.sem("s_w2")
    s_pp = p.sem("s_pp")
    so = p.sem("so")
    sv = p.sem("v")
    ss = p.sem("s")
    st = p.sem("t")
    sems = dict(v=sv, s=ss, t=st)
    semmap = dict(sv=sv, ss=ss, st=st)

    def wgd(t, g):
        nm, val = g
        if val:
            t.wait_ge(semmap[nm].h, val)

    pcol = lambda which, ct: parms[:, which * CT + ct:which * CT + ct + 1]
    a_col = lambda ct: acol[:, ct:ct + 1]
    eu_col = lambda ct: acol[:, CT + ct:CT + ct + 1]

    @p.on("sync")
    def _(e):
        dmaw(e, s_pp, 0, out=parms[:], in_=pvec)
    pld = s_pp.inc(16)

    @p.on("scalar")
    def _(s, pl=pld):
        s.wait_ge(s_pp.h, pl)
        s.activation(out=acol[:, 0:CT], in_=parms[:, P_WDEC * CT:(P_WDEC + 1) * CT], func=AF.Exp).then_inc(ss.h, 1)
        s.wait_ge(ss.h, 1)
        s.activation(out=acol[:, 0:CT], in_=acol[:, 0:CT], func=AF.Exp, scale=-1.0).then_inc(ss.h, 1)
        s.wait_ge(ss.h, 2)
        s.activation(out=acol[:, CT:2 * CT], in_=parms[:, P_UF * CT:(P_UF + 1) * CT], func=AF.Exp).then_inc(ss.h, 1)
    acol_done = ss.inc(3)

    @p.on("gpsimd")
    def _(e):
        dmaw(e, s_w, 0, out=wbig[:, :, 0:1024], in_=wk.rearrange("(kt p) o -> p kt o", p=128))
        dmaw(e, s_w, 16, out=wbig[:, :, 1024:2048], in_=wv.rearrange("(kt p) o -> p kt o", p=128))
        dmaw(e, s_w, 32, out=wbig[:, :, 2048:3072], in_=wr.rearrange("(kt p) o -> p kt o", p=128))
        dmaw(e, s_w, 48, out=wbig[:, :, 3072:4096], in_=wo.rearrange("(kt p) o -> p kt o", p=128))
    w1_ld = s_w.inc(64)

    g_AB = [("ss", 0), ("ss", 0), ("ss", 0), ("ss", 0)]
    g_DEN = ("sv", 0)
    g_BC = ("sv", 0)
    g_sq = 0
    g_xslot = ("sv", 0)
    g_store_o1 = 0
    g_scr = 0
    g_shared_t = 0
    WOFF = dict(k=0, v=1024, r=2048, o=3072)

    for sqi in range(nseq):
        for ch in range(NCH):
            t0 = ch * CW
            first = (ch == 0)

            @p.on("sync")
            def _(e, sqi=sqi, t0=t0, gx=g_xslot, pre=s_x.n, sg_=g_store_o1):
                nm, val = gx
                if val:
                    e.wait_ge(semmap[nm].h, val)
                if sg_:
                    e.wait_ge(so.h, sg_)
                dmaw(e, s_x, pre, out=xnf[:], in_=xr[:, :, sqi, t0:t0 + CW])
            xld = s_x.inc(16)

            @p.on("gpsimd")
            def _(e, sqi=sqi, t0=t0, need=g_sq, pre=s_xb.n):
                if need:
                    e.wait_ge(st.h, need)
                dmaw(e, s_xb, pre, out=xnb[:], in_=xr[:, :, sqi, t0:t0 + CW])
            xbld = s_xb.inc(16)

            @p.on("vector")
            def _(v, nx=xld, nb=xbld, ad=acol_done, first=first, entry=sv.n):
                v.wait_ge(s_x.h, nx)
                v.wait_ge(s_xb.h, nb)
                v.wait_ge(ss.h, ad)
                v.wait_ge(sv.h, entry)
                if first:
                    v.memset(carr[:], 0.0).then_inc(sv.h, 1)
                else:
                    v.tensor_copy(out=h1f[:, :, 0:1], in_=carr[:, :, 0:1]).then_inc(sv.h, 1)
            bar_v = sv.inc()

            @p.on("scalar")
            def _(s, nx=xld, nb=xbld):
                s.wait_ge(s_x.h, nx)
                s.wait_ge(s_xb.h, nb)
                s.memzero(p.dums[:]).then_inc(ss.h, 1)
            bar_s = ss.inc()

            if first:
                @p.on("vector")
                def _(v, e_=bar_v):
                    v.wait_ge(sv.h, e_)
                    v.memset(h1f[:, :, 0:1], 0.0).then_inc(sv.h, 1)
                sv.inc()

            r = ln_chunk(
                p, cst, CW,
                xf=lambda ct: xnf[:, ct, :],
                xb=lambda ct: xnb[:, ct, :],
                sq=lambda ct: sq[:, ct, :],
                tmp=lambda ct: h1f[:, ct, 1:CW + 1],
                rm=rm[:], rr=rr[:], sems=sems,
                bank_a=p.banks[4], bank_b=p.banks[5],
                out_f32=lambda ct: h1f[:, ct, 1:CW + 1],
                gcol=lambda ct: pcol(P_LN1S, ct), bcol=lambda ct: pcol(P_LN1B, ct),
                bank_guard=max(g_DEN[1], g_BC[1]), sq_guard=g_sq,
                x_dep=(ss.h, bar_s),
                out_guard=[(st.h, g_shared_t)],
            )
            g_DEN = ("sv", r["var"]); g_BC = ("sv", r["var"])
            g_sq = r["stats"]
            h1_done = r["s"]

            @p.on("vector")
            def _(v, hd=h1_done, entry=sv.n, gs=g_scr):
                v.wait_ge(ss.h, hd)
                v.wait_ge(sv.h, entry)
                if gs:
                    v.wait_ge(st.h, gs)
                v.tensor_copy(out=carr[:, :, 0:1], in_=h1f[:, :, CW:CW + 1]).then_inc(sv.h, 1)
                v.wait_ge(sv.h, entry + 1)
                ins = None
                for ct in range(CT):
                    ins = v.tensor_tensor(out=xnf[:, ct, :], in0=h1f[:, ct, 1:CW + 1],
                                          in1=h1f[:, ct, 0:CW], op=ALU.subtract)
                ins.then_inc(sv.h, 1)
                v.wait_ge(sv.h, entry + 2)
                ins = None
                for ct in range(CT):
                    ins = v.scalar_tensor_tensor(out=scr_bf[:, ct, :], in0=xnf[:, ct, :],
                                                 scalar=pcol(P_TMK, ct), in1=h1f[:, ct, 0:CW],
                                                 op0=ALU.mult, op1=ALU.add)
                ins.then_inc(sv.h, 1)
                v.wait_ge(sv.h, entry + 3)
                ins = None
                for ct in range(CT):
                    ins = v.scalar_tensor_tensor(out=scr_bf[:, 8 + ct, :], in0=xnf[:, ct, :],
                                                 scalar=pcol(P_TMV, ct), in1=h1f[:, ct, 0:CW],
                                                 op0=ALU.mult, op1=ALU.add)
                ins.then_inc(sv.h, 1)
                v.wait_ge(sv.h, entry + 4)
                ins = None
                for ct in range(CT):
                    ins = v.scalar_tensor_tensor(out=scr_bf[:, 16 + ct, :], in0=xnf[:, ct, :],
                                                 scalar=pcol(P_TMR, ct), in1=h1f[:, ct, 0:CW],
                                                 op0=ALU.mult, op1=ALU.add)
                ins.then_inc(sv.h, 1)
            mix_done = sv.inc(5)

            for wname, which, col0 in (("k", 0, WOFF["k"]), ("v", 1, WOFF["v"])):
                for mt in range(CT):
                    bk = (which * 8 + mt) % 4

                    @p.on("tensor")
                    def _(t, which=which, col0=col0, mt=mt, bk=bk, md=mix_done, wl=w1_ld, gb=g_AB[bk]):
                        t.wait_ge(sv.h, md)
                        t.wait_ge(s_w.h, wl)
                        wgd(t, gb)
                        ins = None
                        for kt in range(CT):
                            ins = t.matmul(p.banks[bk][:, 0:CW],
                                           wbig[:, kt, col0 + mt * 128:col0 + (mt + 1) * 128],
                                           scr_bf[:, which * 8 + kt, :],
                                           start=(kt == 0), stop=(kt == CT - 1))
                        ins.then_inc(st.h, 1)
                    mmd = st.inc()

                    @p.on("scalar")
                    def _(s, which=which, mt=mt, bk=bk, mmd=mmd, entry=ss.n):
                        s.wait_ge(st.h, mmd)
                        s.wait_ge(ss.h, entry)
                        if which == 0:
                            s.activation(out=ekf(mt), in_=p.banks[bk][:, 0:CW], func=AF.Exp).then_inc(ss.h, 1)
                        else:
                            s.activation(out=vvf(mt), in_=p.banks[bk][:, 0:CW], func=AF.Identity).then_inc(ss.h, 1)
                    g_AB[bk] = ("ss", ss.inc())
            kv_ev = ss.n

            @p.on("vector")
            def _(v, kv=kv_ev, entry=sv.n, first=first):
                v.wait_ge(ss.h, kv)
                v.wait_ge(sv.h, entry)
                if first:
                    v.memset(shared[:, :, 3:5, 0:1], 0.0)
                else:
                    v.tensor_copy(out=shared[:, :, 3, 0:1], in_=carr[:, :, 1:2])
                    v.tensor_copy(out=shared[:, :, 4, 0:1], in_=carr[:, :, 2:3])
                ins = None
                for ct in range(CT):
                    ins = v.tensor_tensor(out=bbf(ct), in0=ekf(ct), in1=vvf(ct), op=ALU.mult)
                ins.then_inc(sv.h, 1)
                v.wait_ge(sv.h, entry + 1)
                for ct in range(CT):
                    v.tensor_tensor_scan(out=shared[:, ct, 3, 1:CW + 1],
                                         data0=a_col(ct).broadcast_to([128, CW]),
                                         data1=bbf(ct),
                                         initial=(0.0 if first else carr[:, ct, 1:2]),
                                         op0=ALU.mult, op1=ALU.add)
                ins = None
                for ct in range(CT):
                    ins = v.tensor_tensor_scan(out=shared[:, ct, 4, 1:CW + 1],
                                               data0=a_col(ct).broadcast_to([128, CW]),
                                               data1=ekf(ct),
                                               initial=(0.0 if first else carr[:, ct, 2:3]),
                                               op0=ALU.mult, op1=ALU.add)
                ins.then_inc(sv.h, 1)
                v.wait_ge(sv.h, entry + 2)
                v.tensor_copy(out=carr[:, :, 1:2], in_=shared[:, :, 3, CW:CW + 1])
                ins = v.tensor_copy(out=carr[:, :, 2:3], in_=shared[:, :, 4, CW:CW + 1])
                ins.then_inc(sv.h, 1)
                v.wait_ge(sv.h, entry + 3)
                ins = None
                for ct in range(CT):
                    ins = v.tensor_scalar(out=ekf(ct), in0=ekf(ct), scalar1=eu_col(ct),
                                          scalar2=None, op0=ALU.mult)
                ins.then_inc(sv.h, 1)
                v.wait_ge(sv.h, entry + 4)
                ins = None
                for ct in range(CT):
                    ins = v.tensor_tensor(out=vvf(ct), in0=ekf(ct), in1=vvf(ct), op=ALU.mult)
                ins.then_inc(sv.h, 1)
                v.wait_ge(sv.h, entry + 5)
                ins = None
                for ct in range(CT):
                    ins = v.tensor_tensor(out=vvf(ct), in0=vvf(ct), in1=shared[:, ct, 3, 0:CW], op=ALU.add)
                ins.then_inc(sv.h, 1)
                v.wait_ge(sv.h, entry + 6)
                ins = None
                for ct in range(CT):
                    ins = v.tensor_tensor(out=ekf(ct), in0=ekf(ct), in1=shared[:, ct, 4, 0:CW], op=ALU.add)
                ins.then_inc(sv.h, 1)
                v.wait_ge(sv.h, entry + 7)
                ins = None
                for ct in range(CT):
                    ins = v.reciprocal(out=ekf(ct), in_=ekf(ct))
                ins.then_inc(sv.h, 1)
                v.wait_ge(sv.h, entry + 8)
                ins = None
                for ct in range(CT):
                    ins = v.tensor_tensor(out=vvf(ct), in0=vvf(ct), in1=ekf(ct), op=ALU.mult)
                ins.then_inc(sv.h, 1)
            wkv_done = sv.inc(9)

            for mt in range(CT):
                bk = mt % 4

                @p.on("tensor")
                def _(t, mt=mt, bk=bk, wl=w1_ld, gb=g_AB[bk], wd=wkv_done):
                    t.wait_ge(sv.h, wd)
                    t.wait_ge(s_w.h, wl)
                    wgd(t, gb)
                    ins = None
                    for kt in range(CT):
                        ins = t.matmul(p.banks[bk][:, 0:CW],
                                       wbig[:, kt, WOFF["r"] + mt * 128:WOFF["r"] + (mt + 1) * 128],
                                       scr_bf[:, 16 + kt, :],
                                       start=(kt == 0), stop=(kt == CT - 1))
                    ins.then_inc(st.h, 1)
                mmd = st.inc()

                @p.on("scalar")
                def _(s, mt=mt, bk=bk, mmd=mmd, entry=ss.n):
                    s.wait_ge(st.h, mmd)
                    s.wait_ge(ss.h, entry)
                    s.activation(out=bbf(mt), in_=p.banks[bk][:, 0:CW], func=AF.Sigmoid).then_inc(ss.h, 1)
                g_AB[bk] = ("ss", ss.inc())
            r_ev = ss.n

            @p.on("vector")
            def _(v, re_=r_ev, entry=sv.n, gs=g_scr):
                v.wait_ge(ss.h, re_)
                v.wait_ge(sv.h, entry)
                ins = None
                for ct in range(CT):
                    ins = v.tensor_tensor(out=scr_bf[:, 24 + ct, :], in0=vvf(ct), in1=bbf(ct), op=ALU.mult)
                ins.then_inc(sv.h, 1)
            ry_done = sv.inc()

            o1_ev = 0
            for mt in range(CT):
                bk = mt % 4

                @p.on("tensor")
                def _(t, mt=mt, bk=bk, rd=ry_done, wl=w1_ld, gb=g_AB[bk]):
                    t.wait_ge(sv.h, rd)
                    t.wait_ge(s_w.h, wl)
                    wgd(t, gb)
                    ins = None
                    for kt in range(CT):
                        ins = t.matmul(p.banks[bk][:, 0:CW],
                                       wbig[:, kt, WOFF["o"] + mt * 128:WOFF["o"] + (mt + 1) * 128],
                                       scr_bf[:, 24 + kt, :],
                                       start=(kt == 0), stop=(kt == CT - 1))
                    ins.then_inc(st.h, 1)
                mmd = st.inc()

                @p.on("vector")
                def _(v, mt=mt, bk=bk, mmd=mmd, entry=sv.n, og=g_store_o1):
                    v.wait_ge(st.h, mmd)
                    v.wait_ge(sv.h, entry)
                    if mt == 0 and og:
                        v.wait_ge(so.h, og)
                    v.tensor_copy(out=xnf[:, mt, :], in_=p.banks[bk][:, 0:CW]).then_inc(sv.h, 1)
                o1_ev = sv.inc()
                g_AB[bk] = ("sv", o1_ev)

            @p.on("sync")
            def _(e, sqi=sqi, t0=t0, od=o1_ev, pre=so.n):
                e.wait_ge(sv.h, od)
                dmaw(e, so, pre, out=o1r[:, :, sqi, t0:t0 + CW], in_=xnf[:])
            g_store_o1 = so.inc(16)
            g_xslot = ("sv", o1_ev)
            g_scr = st.n
            g_shared_t = st.n

    def lif_phase(src_r, dst_r, tag, src_stored):
        lsem = p.sem(f"lb_{tag}")
        g_lb = 0
        g_spk = 0

        @p.on("vector")
        def _(v, entry=sv.n):
            v.wait_ge(sv.h, entry)
            v.memset(mst[:], 0.0).then_inc(sv.h, 1)
        mz_done = sv.inc()

        for lc in range(NLC):
            t0 = lc * TC

            @p.on("sync")
            def _(e, t0=t0, need=g_lb, pre=lsem.n, mz=mz_done, sst=src_stored):
                if need:
                    e.wait_ge(sv.h, need)
                e.wait_ge(sv.h, mz)
                e.wait_ge(so.h, sst)
                dmaw4(e, lsem, pre, lifbuf[:, :, 0:nseq, :], src_r[:, :, 0:nseq, t0:t0 + TC], nseq)
            ld = lsem.inc(16 * nseq)

            @p.on("vector")
            def _(v, ld=ld, entry=sv.n, gs=g_spk, nseq=nseq):
                v.wait_ge(lsem.h, ld)
                v.wait_ge(sv.h, entry)
                ins = v.tensor_scalar(out=lifbuf[:, :, 0:nseq, :], in0=lifbuf[:, :, 0:nseq, :],
                                      scalar1=0.5, scalar2=None, op0=ALU.mult)
                ins.then_inc(sv.h, 1)
                c = entry + 1
                for tt in range(TC):
                    v.wait_ge(sv.h, c)
                    v.scalar_tensor_tensor(out=lifbuf[:, :, 0:nseq, tt],
                                           in0=mst[:].rearrange("p (ct s) -> p ct s", ct=CT)[:, :, 0:nseq],
                                           scalar=0.5, in1=lifbuf[:, :, 0:nseq, tt],
                                           op0=ALU.mult, op1=ALU.add).then_inc(sv.h, 1)
                    c += 1
                    v.wait_ge(sv.h, c)
                    v.scalar_tensor_tensor(out=mst[:].rearrange("p (ct s) -> p ct s", ct=CT)[:, :, 0:nseq],
                                           in0=lifbuf[:, :, 0:nseq, tt], scalar=1.0,
                                           in1=lifbuf[:, :, 0:nseq, tt],
                                           op0=ALU.is_lt, op1=ALU.mult).then_inc(sv.h, 1)
                    c += 1
                v.wait_ge(sv.h, c)
                if gs:
                    v.wait_ge(so.h, gs)
                v.tensor_scalar(out=spk[:, :, 0:nseq, :], in0=lifbuf[:, :, 0:nseq, :],
                                scalar1=1.0, scalar2=None, op0=ALU.is_ge).then_inc(sv.h, 1)
            done = sv.inc(2 + 2 * TC)
            g_lb = done

            @p.on("sync")
            def _(e, t0=t0, dn=done, pre=so.n):
                e.wait_ge(sv.h, dn)
                dmaw4(e, so, pre, dst_r[:, :, 0:nseq, t0:t0 + TC], spk[:, :, 0:nseq, :], nseq)
            g_spk = so.inc(16 * nseq)
        return g_spk

    s1_stored = lif_phase(o1r, s1r, "l1", g_store_o1)

    @p.on("gpsimd")
    def _(e, gw=st.n, pre=s_w.n):
        e.wait_ge(st.h, gw)
        dmaw(e, s_w, pre, out=wbig[:], in_=fwk.rearrange("(kt p) o -> p kt o", p=128))
        dmaw(e, s_w, pre + 16, out=fwrs[:], in_=fwr.rearrange("(kt p) o -> p kt o", p=128))
    w3_ld = s_w.inc(32)

    g_x2b = 0
    g_kk = 0
    g_store_cm = 0
    for sqi in range(nseq):
        for ch in range(NCH):
            t0 = ch * CW
            first = (ch == 0)

            @p.on("sync")
            def _(e, sqi=sqi, t0=t0, gx=g_xslot, pre=s_x.n, sp=max(s1_stored, g_store_cm)):
                nm, val = gx
                if val:
                    e.wait_ge(semmap[nm].h, val)
                e.wait_ge(so.h, sp)
                dmaw(e, s_x, pre, out=xnf[:], in_=xr[:, :, sqi, t0:t0 + CW])
            xld = s_x.inc(16)

            @p.on("gpsimd")
            def _(e, sqi=sqi, t0=t0, need=g_sq, pre=s_xb.n, sp=s1_stored):
                if need:
                    e.wait_ge(st.h, need)
                e.wait_ge(so.h, sp)
                dmaw(e, s_xb, pre, out=sq[:], in_=s1r[:, :, sqi, t0:t0 + CW])
            sld = s_xb.inc(16)

            @p.on("vector")
            def _(v, nx=xld, ns=sld, entry=sv.n, gsh=g_shared_t, first=first):
                v.wait_ge(s_x.h, nx)
                v.wait_ge(s_xb.h, ns)
                v.wait_ge(sv.h, entry)
                if gsh:
                    v.wait_ge(st.h, gsh)
                ins = None
                for ct in range(CT):
                    ins = v.tensor_tensor(out=xAf(ct), in0=xnf[:, ct, :], in1=sq[:, ct, :], op=ALU.add)
                ins.then_inc(sv.h, 1)
                v.wait_ge(sv.h, entry + 1)
                ins = None
                for ct in range(CT):
                    ins = v.tensor_copy(out=xnb[:, ct, :], in_=xAf(ct))
                ins.then_inc(sv.h, 1)
                v.wait_ge(sv.h, entry + 2)
                if first:
                    v.memset(h1f[:, :, 0:1], 0.0).then_inc(sv.h, 1)
                else:
                    v.tensor_copy(out=h1f[:, :, 0:1], in_=carr[:, :, 0:1]).then_inc(sv.h, 1)
            xa_done = sv.inc(3)

            @p.on("scalar")
            def _(s, xa=xa_done):
                s.wait_ge(sv.h, xa)
                s.memzero(p.dums[:]).then_inc(ss.h, 1)
            bar_s = ss.inc()

            r3 = ln_chunk(
                p, cst, CW,
                xf=lambda ct: xAf(ct),
                xb=lambda ct: xnb[:, ct, :],
                sq=lambda ct: sq[:, ct, :],
                tmp=lambda ct: h1f[:, ct, 1:CW + 1],
                rm=rm[:], rr=rr[:], sems=sems,
                bank_a=p.banks[4], bank_b=p.banks[5],
                out_f32=lambda ct: h1f[:, ct, 1:CW + 1],
                gcol=lambda ct: pcol(P_LN2S, ct), bcol=lambda ct: pcol(P_LN2B, ct),
                bank_guard=max(g_DEN[1], g_BC[1]), sq_guard=g_sq,
                x_dep=(ss.h, bar_s),
            )
            g_DEN = ("sv", r3["var"]); g_BC = ("sv", r3["var"])
            g_sq = r3["stats"]
            h2_done = r3["s"]

            @p.on("vector")
            def _(v, hd=h2_done, entry=sv.n, gx2=g_x2b):
                v.wait_ge(ss.h, hd)
                v.wait_ge(sv.h, entry)
                if gx2:
                    v.wait_ge(st.h, gx2)
                v.tensor_copy(out=carr[:, :, 0:1], in_=h1f[:, :, CW:CW + 1]).then_inc(sv.h, 1)
                v.wait_ge(sv.h, entry + 1)
                ins = None
                for ct in range(CT):
                    ins = v.tensor_tensor(out=xnf[:, ct, :], in0=h1f[:, ct, 1:CW + 1],
                                          in1=h1f[:, ct, 0:CW], op=ALU.subtract)
                ins.then_inc(sv.h, 1)
                v.wait_ge(sv.h, entry + 2)
                ins = None
                for ct in range(CT):
                    ins = v.scalar_tensor_tensor(out=x2b[:, ct, :], in0=xnf[:, ct, :],
                                                 scalar=pcol(P_FTMK, ct), in1=h1f[:, ct, 0:CW],
                                                 op0=ALU.mult, op1=ALU.add)
                ins.then_inc(sv.h, 1)
                v.wait_ge(sv.h, entry + 3)
                ins = None
                for ct in range(CT):
                    ins = v.scalar_tensor_tensor(out=x2b[:, 8 + ct, :], in0=xnf[:, ct, :],
                                                 scalar=pcol(P_FTMR, ct), in1=h1f[:, ct, 0:CW],
                                                 op0=ALU.mult, op1=ALU.add)
                ins.then_inc(sv.h, 1)
            mix2_done = sv.inc(4)

            for mt in range(32):
                bk = mt % 4

                @p.on("tensor")
                def _(t, mt=mt, bk=bk, md=mix2_done, wl=w3_ld, gb=g_AB[bk]):
                    t.wait_ge(sv.h, md)
                    t.wait_ge(s_w.h, wl)
                    wgd(t, gb)
                    ins = None
                    for kt in range(CT):
                        ins = t.matmul(p.banks[bk][:, 0:CW],
                                       wbig[:, kt, mt * 128:(mt + 1) * 128],
                                       x2b[:, kt, :], start=(kt == 0), stop=(kt == CT - 1))
                    ins.then_inc(st.h, 1)
                mmd = st.inc()

                @p.on("scalar")
                def _(s, mt=mt, bk=bk, mmd=mmd, entry=ss.n, gk=(g_kk if mt == 0 else 0)):
                    s.wait_ge(st.h, mmd)
                    s.wait_ge(ss.h, entry)
                    if gk:
                        s.wait_ge(st.h, gk)
                    s.activation(out=sq[:, mt % CT, :], in_=p.banks[bk][:, 0:CW],
                                 func=AF.Relu).then_inc(ss.h, 1)
                    s.wait_ge(ss.h, entry + 1)
                    s.activation(out=scr_bf[:, mt, :], in_=sq[:, mt % CT, :], func=AF.Square).then_inc(ss.h, 1)
                g_AB[bk] = ("ss", ss.inc(2))
            kk_done = ss.n

            for mt in range(CT):
                bk = mt % 4

                @p.on("tensor")
                def _(t, mt=mt, bk=bk, md=mix2_done, wl=w3_ld, gb=g_AB[bk]):
                    t.wait_ge(sv.h, md)
                    t.wait_ge(s_w.h, wl)
                    wgd(t, gb)
                    ins = None
                    for kt in range(CT):
                        ins = t.matmul(p.banks[bk][:, 0:CW],
                                       fwrs[:, kt, mt * 128:(mt + 1) * 128],
                                       x2b[:, 8 + kt, :], start=(kt == 0), stop=(kt == CT - 1))
                    ins.then_inc(st.h, 1)
                mmd = st.inc()

                @p.on("scalar")
                def _(s, mt=mt, bk=bk, mmd=mmd, entry=ss.n):
                    s.wait_ge(st.h, mmd)
                    s.wait_ge(ss.h, entry)
                    s.activation(out=gatef(mt), in_=p.banks[bk][:, 0:CW], func=AF.Sigmoid).then_inc(ss.h, 1)
                g_AB[bk] = ("ss", ss.inc())
            gate_done = ss.n

            cm_ev = 0
            for mt in range(CT):
                bk = mt % 4

                @p.on("gpsimd")
                def _(e, mt=mt, pre=s_w2.n, need=st.n):
                    e.wait_ge(st.h, need)
                    dmaw(e, s_w2, pre, out=fwvs[:],
                         in_=fwv.rearrange("(kt p) o -> p kt o", p=128)[:, :, mt * 128:(mt + 1) * 128])
                wvld = s_w2.inc(16)

                @p.on("tensor")
                def _(t, mt=mt, bk=bk, kd=kk_done, wl=wvld, gb=g_AB[bk]):
                    t.wait_ge(ss.h, kd)
                    t.wait_ge(s_w2.h, wl)
                    wgd(t, gb)
                    ins = None
                    for kt in range(32):
                        ins = t.matmul(p.banks[bk][:, 0:CW], fwvs[:, kt, :],
                                       scr_bf[:, kt, :], start=(kt == 0), stop=(kt == 31))
                    ins.then_inc(st.h, 1)
                mmd = st.inc()

                @p.on("vector")
                def _(v, mt=mt, bk=bk, mmd=mmd, entry=sv.n, gd=gate_done, og=g_store_cm):
                    v.wait_ge(st.h, mmd)
                    v.wait_ge(sv.h, entry)
                    v.wait_ge(ss.h, gd)
                    if mt == 0 and og:
                        v.wait_ge(so.h, og)
                    v.tensor_tensor(out=xnf[:, mt, :], in0=p.banks[bk][:, 0:CW],
                                    in1=gatef(mt), op=ALU.mult).then_inc(sv.h, 1)
                cm_ev = sv.inc()
                g_AB[bk] = ("sv", cm_ev)

            @p.on("sync")
            def _(e, sqi=sqi, t0=t0, cd=cm_ev, pre=so.n):
                e.wait_ge(sv.h, cd)
                dmaw(e, so, pre, out=cmr[:, :, sqi, t0:t0 + CW], in_=xnf[:])
            g_store_cm = so.inc(16)
            g_xslot = ("sv", cm_ev)
            g_x2b = st.n
            g_kk = st.n
            g_shared_t = st.n

    lsem4 = p.sem("lb4")
    g_lb4 = 0
    g_out = 0

    @p.on("vector")
    def _(v, entry=sv.n):
        v.wait_ge(sv.h, entry)
        v.memset(mst[:], 0.0).then_inc(sv.h, 1)
    mz4 = sv.inc()

    for lc in range(NLC):
        t0 = lc * TC

        @p.on("sync")
        def _(e, t0=t0, need=g_lb4, pre=lsem4.n, mz=mz4, cs=max(g_store_cm, g_out)):
            if need:
                e.wait_ge(sv.h, need)
            e.wait_ge(sv.h, mz)
            e.wait_ge(so.h, cs)
            dmaw4(e, lsem4, pre, lifbuf[:, :, 0:nseq, :], cmr[:, :, 0:nseq, t0:t0 + TC], nseq)
        ld = lsem4.inc(16 * nseq)

        @p.on("vector")
        def _(v, ld=ld, entry=sv.n, nseq=nseq):
            v.wait_ge(lsem4.h, ld)
            v.wait_ge(sv.h, entry)
            ins = v.tensor_scalar(out=lifbuf[:, :, 0:nseq, :], in0=lifbuf[:, :, 0:nseq, :],
                                  scalar1=0.5, scalar2=None, op0=ALU.mult)
            ins.then_inc(sv.h, 1)
            c = entry + 1
            for tt in range(TC):
                v.wait_ge(sv.h, c)
                v.scalar_tensor_tensor(out=lifbuf[:, :, 0:nseq, tt],
                                       in0=mst[:].rearrange("p (ct s) -> p ct s", ct=CT)[:, :, 0:nseq],
                                       scalar=0.5, in1=lifbuf[:, :, 0:nseq, tt],
                                       op0=ALU.mult, op1=ALU.add).then_inc(sv.h, 1)
                c += 1
                v.wait_ge(sv.h, c)
                v.scalar_tensor_tensor(out=mst[:].rearrange("p (ct s) -> p ct s", ct=CT)[:, :, 0:nseq],
                                       in0=lifbuf[:, :, 0:nseq, tt], scalar=1.0,
                                       in1=lifbuf[:, :, 0:nseq, tt],
                                       op0=ALU.is_lt, op1=ALU.mult).then_inc(sv.h, 1)
                c += 1
            v.wait_ge(sv.h, c)
            v.tensor_scalar(out=spk[:, :, 0:nseq, :], in0=lifbuf[:, :, 0:nseq, :],
                            scalar1=1.0, scalar2=None, op0=ALU.is_ge).then_inc(sv.h, 1)
        lif_done = sv.inc(2 + 2 * TC)

        @p.on("sync")
        def _(e, t0=t0, dn=lif_done, pre=s_x.n):
            e.wait_ge(sv.h, dn)
            dmaw4(e, s_x, pre, lifbuf[:, :, 0:nseq, :], xr[:, :, 0:nseq, t0:t0 + TC], nseq)
        xld = s_x.inc(16 * nseq)

        @p.on("vector")
        def _(v, ld=xld, entry=sv.n):
            v.wait_ge(s_x.h, ld)
            v.wait_ge(sv.h, entry)
            v.tensor_tensor(out=lifbuf[:, :, 0:nseq, :], in0=lifbuf[:, :, 0:nseq, :],
                            in1=spk[:, :, 0:nseq, :], op=ALU.add).then_inc(sv.h, 1)
        c1_done = sv.inc()

        @p.on("gpsimd")
        def _(e, t0=t0, dn=c1_done, pre=s_xb.n):
            e.wait_ge(sv.h, dn)
            dmaw4(e, s_xb, pre, spk[:, :, 0:nseq, :], s1r[:, :, 0:nseq, t0:t0 + TC], nseq)
        s1ld = s_xb.inc(16 * nseq)

        @p.on("vector")
        def _(v, ld=s1ld, entry=sv.n, og=g_out):
            v.wait_ge(s_xb.h, ld)
            v.wait_ge(sv.h, entry)
            if og:
                v.wait_ge(so.h, og)
            v.tensor_tensor(out=lifbuf[:, :, 0:nseq, :], in0=lifbuf[:, :, 0:nseq, :],
                            in1=spk[:, :, 0:nseq, :], op=ALU.add).then_inc(sv.h, 1)
        s_done = sv.inc()
        g_lb4 = s_done

        @p.on("sync")
        def _(e, t0=t0, sd_=s_done, pre=so.n):
            e.wait_ge(sv.h, sd_)
            dmaw4(e, so, pre, srr[:, :, 0:nseq, t0:t0 + TC], lifbuf[:, :, 0:nseq, :], nseq)
        g_out = so.inc(16 * nseq)

    @p.on("sync")
    def _(e, need=so.n):
        e.wait_ge(so.h, need)


def build_s0(barrier=False, strong=()):
    p = Prog(); p.mkdum(); p.init_psum(6, 2)
    xT = p.dram("xT", [C, S, T], F32, kind="ExternalInput")
    xnT = p.dram("xnT", [C, S, T], F32, kind="ExternalOutput")
    cst = emit_consts(p)
    with p.stage():
        emit_s0(p, cst, xT, xnT, barrier=barrier, strong=strong)
    return p.finish()


def build_s1(nseq=S):
    p = Prog(); p.mkdum(); p.init_psum(6, 2)
    xnT = p.dram("xnT", [C, S, T], F32, kind="ExternalInput")
    wk = p.dram("wk", [C, C], F32, kind="ExternalInput")
    wv = p.dram("wv", [C, C], F32, kind="ExternalInput")
    wr = p.dram("wr", [C, C], F32, kind="ExternalInput")
    wo = p.dram("wo", [C, C], F32, kind="ExternalInput")
    fwk = p.dram("fwk", [C, 4096], F32, kind="ExternalInput")
    fwr = p.dram("fwr", [C, C], F32, kind="ExternalInput")
    fwv = p.dram("fwv", [4096, C], F32, kind="ExternalInput")
    pvec = p.dram("pvec", [128, 11 * CT], F32, kind="ExternalInput")
    sT = p.dram("sT", [C, S, T], F32, kind="ExternalOutput")
    out1T = p.dram("out1T", [C, S, T], F32, kind="Internal")
    s1T = p.dram("s1T", [C, S, T], BF16, kind="Internal")
    cmT = p.dram("cmT", [C, S, T], F32, kind="Internal")
    cst = emit_consts(p)
    with p.stage():
        emit_s1(p, cst, xnT, wk, wv, wr, wo, fwk, fwr, fwv, pvec, sT, out1T, s1T, cmT, nseq)
    return p.finish()



def to_cm(x):
    return np.ascontiguousarray(np.transpose(x, (2, 0, 1)))

def from_cm(xT):
    return np.transpose(xT, (1, 2, 0))


_PROGS = {}

def _prog(name, builder):
    if name not in _PROGS:
        _PROGS[name] = builder()
    return _PROGS[name]


def _col(v, n=CT):
    return np.ascontiguousarray(np.asarray(v).reshape(n, 128).T)


def build_mega():
    p = Prog(); p.mkdum(); p.init_psum(6, 2)
    F = F32
    xT = p.dram("xT", [C, S, T], F, kind="ExternalInput")
    wk = p.dram("wk", [C, C], F, kind="ExternalInput")
    wv = p.dram("wv", [C, C], F, kind="ExternalInput")
    wr = p.dram("wr", [C, C], F, kind="ExternalInput")
    wo = p.dram("wo", [C, C], F, kind="ExternalInput")
    fwk = p.dram("fwk", [C, 4096], F, kind="ExternalInput")
    fwr = p.dram("fwr", [C, C], F, kind="ExternalInput")
    fwv = p.dram("fwv", [4096, C], F, kind="ExternalInput")
    pvec = p.dram("pvec", [128, 11 * CT], F, kind="ExternalInput")
    qw = p.dram("qw", [6, C, C], I32, kind="ExternalInput")
    qsc = p.dram("qsc", [128, 6, CT, 64], F, kind="ExternalInput")
    qb = p.dram("qb", [128, 6, CT], F, kind="ExternalInput")
    la = p.dram("la", [6, 32, C], F, kind="ExternalInput")
    lb = p.dram("lb", [6, C, 32], F, kind="ExternalInput")
    lng = p.dram("lng", [128, 3, CT], F, kind="ExternalInput")
    lnb = p.dram("lnb", [128, 3, CT], F, kind="ExternalInput")
    qkvw = p.dram("qkvw", [C, 3072], F, kind="ExternalInput")
    outw = p.dram("outw", [C, C], F, kind="ExternalInput")
    w1 = p.dram("w1", [C, 4096], F, kind="ExternalInput")
    w2 = p.dram("w2", [4096, C], F, kind="ExternalInput")
    qkvb = p.dram("qkvb", [128, 24], F, kind="ExternalInput")
    outb = p.dram("outb", [128, CT], F, kind="ExternalInput")
    b1c = p.dram("b1c", [128, 32], F, kind="ExternalInput")
    b2c = p.dram("b2c", [128, CT], F, kind="ExternalInput")
    g1c = p.dram("g1c", [128, CT], F, kind="ExternalInput")
    be1c = p.dram("be1c", [128, CT], F, kind="ExternalInput")
    g2c = p.dram("g2c", [128, CT], F, kind="ExternalInput")
    be2c = p.dram("be2c", [128, CT], F, kind="ExternalInput")
    wgt = p.dram("wgt", [9, C, 2048], F, kind="ExternalInput")
    wut = p.dram("wut", [9, C, 2048], F, kind="ExternalInput")
    wdt = p.dram("wdt", [9, 2048, C], F, kind="ExternalInput")
    wrt = p.dram("wrt", [C, 8], F, kind="ExternalInput")
    rb = p.dram("rb", [8, 1], F, kind="ExternalInput")
    outT = p.dram("outT", [C, S, T], F, kind="ExternalOutput")
    xn0T = p.dram("xn0T", [C, S, T], F, kind="Internal")
    s0T = p.dram("s0T", [C, S, T], F, kind="Internal")
    g0T = p.dram("g0T", [C, S, T], F, kind="Internal")
    hA = p.dram("hA", [C, S, T], F, kind="Internal")
    x1T = p.dram("x1T", [C, S, T], F, kind="Internal")
    xn1T = p.dram("xn1T", [C, S, T], F, kind="Internal")
    s1aT = p.dram("s1aT", [C, S, T], F, kind="Internal")
    g1T = p.dram("g1T", [C, S, T], F, kind="Internal")
    o1i = p.dram("o1i", [C, S, T], F, kind="Internal")
    s1i = p.dram("s1i", [C, S, T], BF16, kind="Internal")
    cmi = p.dram("cmi", [C, S, T], F, kind="Internal")

    cst = emit_consts(p)
    with p.stage():
        emit_s0(p, cst, xT, xn0T)
    p.barrier()
    with p.stage():
        emit_s1(p, cst, xn0T, wk, wv, wr, wo, fwk, fwr, fwv, pvec, s0T, o1i, s1i, cmi)
    p.barrier()
    with p.stage():
        emit_s2(p, cst, xn0T, qw, qsc, qb, la, lb, lng, lnb, g0T, hA)
    p.barrier()
    with p.stage():
        emit_comb(p, cst, xT, s0T, g0T, x1T, xn1T)
    p.barrier()
    with p.stage():
        emit_s4(p, cst, xn1T, qkvw, outw, w1, w2, qkvb, outb, b1c, b2c, g1c, be1c, g2c, be2c, s1aT)
    p.barrier()
    with p.stage():
        emit_s5(p, cst, xn1T, wgt, wut, wdt, wrt, rb, g1T)
    p.barrier()
    with p.stage():
        emit_comb(p, cst, x1T, s1aT, g1T, outT, None)
    return p.finish()


def _host_inputs(z):
    m = {}
    for nm, key in [("wk", "L0_attn_Wk"), ("wv", "L0_attn_Wv"), ("wr", "L0_attn_Wr"),
                    ("wo", "L0_attn_Wo"), ("fwk", "L0_attn_fWk"), ("fwr", "L0_attn_fWr"),
                    ("fwv", "L0_attn_fWv"), ("qkvw", "L1_attn_qkv_w"), ("outw", "L1_attn_out_w"),
                    ("w1", "L1_attn_w1"), ("w2", "L1_attn_w2")]:
        m[nm] = np.ascontiguousarray(np.asarray(z[key]))
    m["pvec"] = np.ascontiguousarray(np.concatenate([_col(z[k]) for k in [
        "L0_attn_ln1_s", "L0_attn_ln1_b", "L0_attn_ln2_s", "L0_attn_ln2_b",
        "L0_attn_tmk", "L0_attn_tmv", "L0_attn_tmr", "L0_attn_f_tmk", "L0_attn_f_tmr",
        "L0_attn_w_decay", "L0_attn_u_first"]], axis=1))
    m["qw"] = np.ascontiguousarray(np.asarray(z["L0_ffn_qweights"]))
    m["qsc"] = np.ascontiguousarray(np.asarray(z["L0_ffn_scales"]).reshape(6, 8, 128, 64).transpose(2, 0, 1, 3))
    m["qb"] = np.ascontiguousarray(np.asarray(z["L0_ffn_biases"]).reshape(6, 8, 128).transpose(2, 0, 1))
    m["la"] = np.ascontiguousarray(np.asarray(z["L0_ffn_lora_a"]))
    m["lb"] = np.ascontiguousarray(np.asarray(z["L0_ffn_lora_b"]))
    m["lng"] = np.ascontiguousarray(np.asarray(z["L0_ffn_ln_gamma"]).reshape(3, 8, 128).transpose(2, 0, 1))
    m["lnb"] = np.ascontiguousarray(np.asarray(z["L0_ffn_ln_beta"]).reshape(3, 8, 128).transpose(2, 0, 1))
    m["qkvb"] = _col(z["L1_attn_qkv_b"], 24)
    m["outb"] = _col(z["L1_attn_out_b"])
    m["b1c"] = _col(z["L1_attn_b1"], 32)
    m["b2c"] = _col(z["L1_attn_b2"])
    m["g1c"] = _col(z["L1_attn_g1"])
    m["be1c"] = _col(z["L1_attn_be1"])
    m["g2c"] = _col(z["L1_attn_g2"])
    m["be2c"] = _col(z["L1_attn_be2"])
    wg = np.concatenate([np.asarray(z["L1_ffn_Wg_s"])[None], np.asarray(z["L1_ffn_Wg"])], 0)
    wu = np.concatenate([np.asarray(z["L1_ffn_Wu_s"])[None], np.asarray(z["L1_ffn_Wu"])], 0)
    wd = np.concatenate([np.asarray(z["L1_ffn_Wd_s"])[None], np.asarray(z["L1_ffn_Wd"])], 0)
    m["wgt"] = np.ascontiguousarray(wg.transpose(0, 2, 1))
    m["wut"] = np.ascontiguousarray(wu.transpose(0, 2, 1))
    m["wdt"] = np.ascontiguousarray(wd.transpose(0, 2, 1))
    m["wrt"] = np.ascontiguousarray(np.asarray(z["L1_ffn_Wr"]).T)
    m["rb"] = np.ascontiguousarray(np.asarray(z["L1_ffn_rb"]).reshape(8, 1))
    return m


def kernel(**inputs):
    x = np.asarray(inputs["L0_attn_x"])
    base = _host_inputs(inputs)
    nc = _prog("mega", build_mega)
    in_maps = [dict(base, xT=to_cm(x[c * S:(c + 1) * S])) for c in range(NCORE)]
    res = run_bass_kernel_spmd(nc, in_maps, core_ids=list(range(NCORE)))
    outs = [from_cm(np.asarray(rc["outT"])) for rc in res.results]
    return np.ascontiguousarray(np.concatenate(outs, axis=0))


# version 05
# speedup vs baseline: 1.1446x; 1.0117x over previous
import numpy as np
import ml_dtypes
from contextlib import ExitStack
import concourse.bass as bass
import concourse.mybir as mybir
from concourse.bass_utils import run_bass_kernel_spmd

F32 = mybir.dt.float32
BF16 = mybir.dt.bfloat16
I32 = mybir.dt.int32
AF = mybir.ActivationFunctionType
ALU = mybir.AluOpType
AXX = mybir.AxisListType.X

NCORE = 8
B, T, C = 32, 2048, 1024
S = B // NCORE
TOK = S * T
CT = C // 128
ENGINES = ("sync", "gpsimd", "vector", "scalar", "tensor")

BF_NP = ml_dtypes.bfloat16


class Sem:
    __slots__ = ("h", "n")
    def __init__(self, h):
        self.h = h
        self.n = 0
    def inc(self, k=1):
        self.n += k
        return self.n


def dmaw4(e, sem, pre, out4, in4, nseq):
    for si in range(nseq):
        if pre + si * 16:
            e.wait_ge(sem.h, pre + si * 16)
        e.dma_start(out=out4[:, :, si, :], in_=in4[:, :, si, :]).then_inc(sem.h, 16)


def dmaw(e, sem, pre, out, in_):
    if pre:
        e.wait_ge(sem.h, pre)
    return e.dma_start(out=out, in_=in_).then_inc(sem.h, 16)


class Prog:
    def __init__(self):
        self.nc = bass.Bass("TRN2", target_bir_lowering=False)
        self.ctx = ExitStack()
        self.cur = self.ctx
        self.q = {e: [] for e in ENGINES}
        self._si = 0
        self._ti = 0
        self.banks = None
        self.bar = None
        self.barn = 0

    def stage(self):
        from contextlib import contextmanager

        @contextmanager
        def _scope():
            sub = ExitStack()
            old = self.cur
            self.cur = sub
            try:
                yield
            finally:
                self.cur = old
                sub.close()
        return _scope()

    def barrier(self):
        if self.bar is None:
            self.bar = self.sem("gbar")
        tgt = self.bar.inc(5)

        @self.on("vector")
        def _(v, t=tgt):
            v.memset(self.dumv[:], 0.0).then_inc(self.bar.h, 1)
            v.wait_ge(self.bar.h, t)

        @self.on("scalar")
        def _(e, t=tgt):
            e.memzero(self.dums[:]).then_inc(self.bar.h, 1)
            e.wait_ge(self.bar.h, t)

        @self.on("gpsimd")
        def _(e, t=tgt):
            e.memset(self.dumg[:], 0.0).then_inc(self.bar.h, 1)
            e.wait_ge(self.bar.h, t)

        @self.on("tensor")
        def _(e, t=tgt):
            e.nop().then_inc(self.bar.h, 1)
            e.wait_ge(self.bar.h, t)

        @self.on("sync")
        def _(e, t=tgt):
            e.nop().then_inc(self.bar.h, 1)
            e.wait_ge(self.bar.h, t)

    def sem(self, name=None):
        h = self.ctx.enter_context(self.nc.semaphore(f"{name or 'sm'}_{self._si}"))
        self._si += 1
        return Sem(h)

    def sbuf(self, name, shape, dt):
        self._ti += 1
        return self.cur.enter_context(self.nc.sbuf_tensor(f"{name}_{self._ti}", shape, dt))

    def dram(self, name, shape, dt, kind="Internal"):
        return self.nc.dram_tensor(name, shape, dt, kind=kind).ap()

    def mkdum(self):
        self.dumv = self.sbuf("dumv", [1, 2], F32)
        self.dums = self.sbuf("dums", [1, 2], F32)
        self.dumg = self.sbuf("dumg", [1, 2], F32)

    def init_psum(self, n_f32=8, n_bf16=0):
        self.banks = [self.ctx.enter_context(self.nc.psum_tensor(f"pb{i}", [128, 512], F32))
                      for i in range(n_f32)]
        self.tbanks = [self.ctx.enter_context(self.nc.psum_tensor(f"pt{i}", [128, 1024], BF16))
                       for i in range(n_bf16)]

    def on(self, eng):
        def deco(fn):
            self.q[eng].append(fn)
            return None
        return deco

    def finish(self):
        with self.nc.Block() as blk:
            for eng in ENGINES:
                def body(e, _eng=eng):
                    for fn in self.q[_eng]:
                        fn(e)
                getattr(blk, eng)(body)
        self.ctx.close()
        return self.nc


def emit_consts(p):
    ones_bf = p.sbuf("ones_bf", [128, 128], BF16)
    ones_tok = p.sbuf("ones_tok", [128, 1], BF16)
    ident_bf = p.sbuf("ident_bf", [128, 128], BF16)
    ident_f32 = p.sbuf("ident_f32", [128, 128], F32)
    fmp = p.sbuf("fmp", [128, 128], F32)
    eps5 = p.sbuf("eps5", [128, 1], F32)
    sc = p.sem("consts")

    @p.on("gpsimd")
    def _(g):
        g.iota(fmp[:], pattern=[[1, 128]], channel_multiplier=-1,
               allow_small_or_imprecise_dtypes=True).then_inc(sc.h, 1)
    t_iota = sc.inc()

    @p.on("vector")
    def _(v):
        v.wait_ge(sc.h, t_iota)
        v.memset(ones_bf[:], 1.0)
        v.memset(ones_tok[:], 1.0)
        v.memset(eps5[:], 1e-5)
        v.tensor_scalar(out=ident_bf[:], in0=fmp[:], scalar1=0.0, scalar2=None,
                        op0=ALU.is_equal)
        v.tensor_scalar(out=ident_f32[:], in0=fmp[:], scalar1=0.0, scalar2=None,
                        op0=ALU.is_equal).then_inc(sc.h, 1)
    sc.inc(1)
    return dict(ones_bf=ones_bf, ones_tok=ones_tok, ident_bf=ident_bf, ident_f32=ident_f32,
                eps5=eps5, fmp=fmp, sem=sc, ready=sc.n)


def ln_chunk(p, cst, W, xf, xb, sq, tmp, rm, rr, sems, bank_a, bank_b,
             out_bf=None, out_f32=None, gcol=None, bcol=None,
             bank_guard=0, sq_guard=0, out_guard=(None, 0), x_dep=(None, 0),
             xb_scr=None):
    sv, ss, st = sems["v"], sems["s"], sems["t"]
    entry_sv = sv.n
    dep_sem, dep_val = x_dep

    if xb is None:
        @p.on("vector")
        def _(v, g=sq_guard, entry=entry_sv):
            v.wait_ge(sv.h, entry)
            if dep_sem is not None:
                v.wait_ge(dep_sem, dep_val)
            v.wait_ge(st.h, g)
            ins = None
            for ct in range(CT):
                ins = v.tensor_copy(out=xb_scr(ct), in_=xf(ct))
            ins.then_inc(sv.h, 1)
        sv.inc()
        xb = xb_scr
        entry_sv = sv.n

    @p.on("scalar")
    def _(s, g=sq_guard):
        s.wait_ge(st.h, g)
        if dep_sem is not None:
            s.wait_ge(dep_sem, dep_val)
        ins = None
        for ct in range(CT):
            ins = s.activation(out=sq(ct), in_=xf(ct), func=AF.Square)
        ins.then_inc(ss.h, 1)
    sq_done = ss.inc()

    @p.on("tensor")
    def _(t, g=bank_guard, sq_done=sq_done, xbv=entry_sv):
        t.wait_ge(ss.h, sq_done)
        t.wait_ge(sv.h, xbv)
        if g:
            t.wait_ge(sv.h, g)
        for ct in range(CT):
            t.matmul(bank_a[:, 0:W], cst["ones_bf"][:], xb(ct), start=(ct == 0), stop=(ct == CT - 1))
        ins = None
        for ct in range(CT):
            ins = t.matmul(bank_b[:, 0:W], cst["ones_bf"][:], sq(ct), start=(ct == 0), stop=(ct == CT - 1))
        ins.then_inc(st.h, 1)
    stats_done = st.inc()

    @p.on("vector")
    def _(v, entry=entry_sv, stats_done=stats_done):
        v.wait_ge(st.h, stats_done)
        v.wait_ge(sv.h, entry)
        v.tensor_scalar(out=rm, in0=bank_a[:, 0:W], scalar1=1.0 / C, scalar2=None,
                        op0=ALU.mult).then_inc(sv.h, 1)
        v.wait_ge(sv.h, entry + 1)
        v.tensor_tensor(out=rr, in0=rm, in1=rm, op=ALU.mult).then_inc(sv.h, 1)
        v.wait_ge(sv.h, entry + 2)
        v.scalar_tensor_tensor(out=rr, in0=bank_b[:, 0:W], scalar=1.0 / C, in1=rr,
                               op0=ALU.mult, op1=ALU.subtract).then_inc(sv.h, 1)
        v.wait_ge(sv.h, entry + 3)
        v.tensor_scalar(out=rr, in0=rr, scalar1=0.0, scalar2=None,
                        op0=ALU.max).then_inc(sv.h, 1)
    var_done = sv.inc(4)

    @p.on("scalar")
    def _(s, var_done=var_done):
        s.wait_ge(sv.h, var_done)
        s.activation(out=rr, in_=rr, func=AF.Sqrt, bias=cst["eps5"][:, 0:1]).then_inc(ss.h, 1)
    sd_done = ss.inc()

    @p.on("vector")
    def _(v, sd_done=sd_done, var_done=var_done):
        v.wait_ge(ss.h, sd_done)
        v.reciprocal(out=rr, in_=rr).then_inc(sv.h, 1)
        v.wait_ge(sv.h, var_done + 1)
        v.tensor_tensor(out=rm, in0=rm, in1=rr, op=ALU.mult).then_inc(sv.h, 1)
    rows_done = sv.inc(2)

    ogs = out_guard if isinstance(out_guard, list) else [out_guard]
    ogs = [(a, b) for (a, b) in ogs if a is not None and b]
    og_sem = ogs[0][0] if ogs else None
    og_val = ogs[0][1] if ogs else 0

    @p.on("vector")
    def _(v, rows_done=rows_done):
        v.wait_ge(sv.h, rows_done)
        for _sem, _val in ogs:
            v.wait_ge(_sem, _val)
        ins = None
        for ct in range(CT):
            ins = v.tensor_tensor(out=tmp(ct), in0=xf(ct), in1=rr, op=ALU.mult)
        ins.then_inc(sv.h, 1)
        v.wait_ge(sv.h, rows_done + 1)
        ins = None
        for ct in range(CT):
            dst = out_f32(ct) if (out_f32 is not None and gcol is None) else tmp(ct)
            ins = v.tensor_tensor(out=dst, in0=tmp(ct), in1=rm, op=ALU.subtract)
        ins.then_inc(sv.h, 1)
    z_done = sv.inc(2)
    v_end = z_done
    s_end = ss.n

    if gcol is None:
        if out_bf is not None:
            src = out_f32 if out_f32 is not None else tmp
            @p.on("vector")
            def _(v, z_done=z_done):
                v.wait_ge(sv.h, z_done)
                ins = None
                for ct in range(CT):
                    ins = v.tensor_copy(out=out_bf(ct), in_=src(ct))
                ins.then_inc(sv.h, 1)
            v_end = sv.inc()
    else:
        @p.on("scalar")
        def _(s, z_done=z_done):
            s.wait_ge(sv.h, z_done)
            for _sem, _val in ogs:
                s.wait_ge(_sem, _val)
            ins = None
            for ct in range(CT):
                dst = out_bf(ct) if out_bf is not None else out_f32(ct)
                ins = s.activation(out=dst, in_=tmp(ct), func=AF.Identity,
                                   scale=gcol(ct), bias=bcol(ct))
            ins.then_inc(ss.h, 1)
        s_end = ss.inc()
        if out_bf is not None and out_f32 is not None:
            @p.on("vector")
            def _(v, s_end=s_end):
                v.wait_ge(ss.h, s_end)
                ins = None
                for ct in range(CT):
                    ins = v.tensor_copy(out=out_f32(ct), in_=out_bf(ct))
                ins.then_inc(sv.h, 1)
            v_end = sv.inc()

    return dict(v=v_end, s=s_end, stats=stats_done, var=var_done)


def emit_s0(p, cst, xT, xnT, barrier=False, strong=()):
    CW = 512
    NCH = TOK // CW
    xr = xT.rearrange("(ct p) s t -> p ct (s t)", p=128)
    onr = xnT.rearrange("(ct p) s t -> p ct (s t)", p=128)

    xf = p.sbuf("xf", [128, CT, 2, CW], F32)
    xb = p.sbuf("xb", [128, CT, 2, CW], BF16)
    sq = p.sbuf("sq", [128, CT, CW], BF16)
    of = p.sbuf("of", [128, CT, 2, CW], F32)
    rm = p.sbuf("rm", [128, CW], F32)
    rr = p.sbuf("rr", [128, CW], F32)

    sd = p.sem("dma_sync")
    sg = p.sem("dma_gp")
    so = p.sem("outd")
    sv = p.sem("v")
    ss = p.sem("s")
    st = p.sem("t")
    sems = dict(v=sv, s=ss, t=st)

    sbar = p.sem("bar")
    guard_bank = [0, 0]
    guard_sq = 0
    guard_out = [0, 0]
    prev_v_end = [0, 0]
    prev_stats = [0, 0]

    for ci in range(NCH):
        pb = ci % 2
        t0 = ci * CW

        if "loads" in strong:
            lneed_v, lneed_t = prev_v_end[1 - pb], prev_stats[1 - pb]
        else:
            lneed_v, lneed_t = prev_v_end[pb], prev_stats[pb]
        @p.on("sync")
        def _(e, t0=t0, pb=pb, need=lneed_v, pre=sd.n):
            if need:
                e.wait_ge(sv.h, need)
            dmaw(e, sd, pre, out=xf[:, :, pb, :], in_=xr[:, :, t0:t0 + CW])
        loads_s = sd.inc(16)

        @p.on("gpsimd")
        def _(e, t0=t0, pb=pb, need=lneed_t, pre=sg.n):
            if need:
                e.wait_ge(st.h, need)
            dmaw(e, sg, pre, out=xb[:, :, pb, :], in_=xr[:, :, t0:t0 + CW])
        loads_g = sg.inc(16)

        @p.on("vector")
        def _(v, ns=loads_s, ng=loads_g):
            v.wait_ge(sd.h, ns)
            v.wait_ge(sg.h, ng)
            v.memset(p.dumv[:], 0.0).then_inc(sv.h, 1)
        @p.on("scalar")
        def _(s, ns=loads_s, ng=loads_g):
            s.wait_ge(sd.h, ns)
            s.wait_ge(sg.h, ng)
            s.memzero(p.dums[:]).then_inc(ss.h, 1)
        sv.inc(); ss.inc()

        r = ln_chunk(
            p, cst, CW,
            xf=lambda ct, pb=pb: xf[:, ct, pb, :],
            xb=lambda ct, pb=pb: xb[:, ct, pb, :],
            sq=lambda ct: sq[:, ct, :],
            tmp=lambda ct, pb=pb: of[:, ct, pb, :],
            rm=rm[:], rr=rr[:], sems=sems,
            bank_a=p.banks[2 * pb], bank_b=p.banks[2 * pb + 1],
            out_f32=lambda ct, pb=pb: of[:, ct, pb, :],
            bank_guard=guard_bank[pb], sq_guard=guard_sq,
            out_guard=(so.h, guard_out[pb]),
        )
        guard_bank[pb] = r["v"] if "bank" in strong else r["var"]
        guard_sq = r["stats"]
        prev_v_end[pb] = r["v"]
        prev_stats[pb] = r["stats"]

        @p.on("sync")
        def _(e, t0=t0, pb=pb, need=r["v"], pre=so.n):
            e.wait_ge(sv.h, need)
            dmaw(e, so, pre, out=onr[:, :, t0:t0 + CW], in_=of[:, :, pb, :])
        guard_out[pb] = so.inc(16)

        if barrier:
            tgt = sbar.inc(4)
            for eng in ("vector", "scalar", "tensor", "gpsimd"):
                @p.on(eng)
                def _(e, tgt=tgt, need=so.n):
                    if e is not None:
                        pass
                    e.wait_ge(so.h, need)
                    e.then_inc(sbar.h, 1) if False else e.nop().then_inc(sbar.h, 1)
                    e.wait_ge(sbar.h, tgt)

    @p.on("sync")
    def _(e, need=so.n):
        e.wait_ge(so.h, need)





def emit_s2(p, cst, xnT, qw, qsc, qb, la, lb, lng, lnb, gT, hA, nblocks=3, nch_lim=None):
    gr = gT.rearrange("(ct p) s t -> p ct (s t)", p=128)
    xr = xnT.rearrange("(ct p) s t -> p ct (s t)", p=128)
    hr = hA.rearrange("(ct p) s t -> p ct (s t)", p=128)

    CW = 512
    NCH = TOK // CW if nch_lim is None else nch_lim

    wT = [p.sbuf(f"wT{j}", [128, CT, C], BF16) for j in range(2)]
    laT = [p.sbuf(f"laT{j}", [128, CT, 32], BF16) for j in range(2)]
    lbT = p.sbuf("lbT", [32, 2, C], BF16)
    qs = p.sbuf("qs", [128, 2, C], I32)
    qf = p.sbuf("qf", [128, C], F32)
    wbf = p.sbuf("wbf", [128, 2, C], BF16)
    labf = p.sbuf("labf", [32, C], BF16)
    lbbf = p.sbuf("lbbf", [128, CT, 32], BF16)
    scs = p.sbuf("scs", [128, 6, CT, 64], F32)
    qbc = p.sbuf("qbc", [128, 6, CT], F32)
    lngs = p.sbuf("lngs", [128, 3, CT], F32)
    lnbs = p.sbuf("lnbs", [128, 3, CT], F32)

    hf = p.sbuf("hf", [128, CT, 2, CW], F32)
    hb = p.sbuf("hb", [128, CT, 2, CW], BF16)
    y_sb = p.sbuf("y_sb", [128, CT, CW], F32)
    ybf = p.sbuf("ybf", [128, CT, CW], BF16)
    z_sb = p.sbuf("z_sb", [128, CT, CW], F32)
    g_bf = p.sbuf("g_bf", [128, CT, CW], BF16)
    sq = p.sbuf("sq", [128, CT, CW], BF16)
    yr_bf = p.sbuf("yr_bf", [32, CW], BF16)
    rm = p.sbuf("rm", [128, CW], F32)
    rr = p.sbuf("rr", [128, CW], F32)

    s_par = p.sem("s_par")
    s_q = p.sem("s_q")
    s_hf = p.sem("s_hf")
    s_la = p.sem("s_la")
    s_lb = p.sem("s_lb")
    s_hb = p.sem("s_hb")
    so = p.sem("so")
    sv = p.sem("v")
    ss = p.sem("s")
    st = p.sem("t")
    sems = dict(v=sv, s=ss, t=st)

    @p.on("sync")
    def _(e):
        dmaw(e, s_par, 0, out=scs[:], in_=qsc)
        dmaw(e, s_par, 16, out=qbc[:], in_=qb)
        dmaw(e, s_par, 32, out=lngs[:], in_=lng)
        dmaw(e, s_par, 48, out=lnbs[:], in_=lnb)
    params_ld = s_par.inc(64)

    g_wT_free = [0, 0]
    g_tb = [0, 0]
    g_bank = {0: ("ss", 0), 1: ("ss", 0), 2: ("sv", 0), 3: ("sv", 0), 4: ("sv", 0), 5: ("ss", 0)}
    g_sq = 0
    g_hn = [0, 0]
    g_qs = [0, 0]
    g_wbf = [0, 0]
    g_labf = [0]
    g_lbbf = [0]
    store_cnt = [[0] * NCH for _ in range(4)]
    semmap = dict(sv=sv, ss=ss, st=st)

    def prep_weights(i, j):
        wg = g_wT_free[j]
        for ot in range(CT):
            sl = ot % 2

            @p.on("sync")
            def _(e, i=i, ot=ot, sl=sl, need=g_qs[sl], pre=s_q.n):
                if need:
                    e.wait_ge(sv.h, need)
                dmaw(e, s_q, pre, out=qs[:, sl, :], in_=qw[i, ot * 128:(ot + 1) * 128, :])
            qld = s_q.inc(16)

            @p.on("vector")
            def _(v, i=i, ot=ot, sl=sl, qld=qld, entry=sv.n, pl=params_ld, wbg=g_wbf[sl]):
                v.wait_ge(s_q.h, qld)
                v.wait_ge(s_par.h, pl)
                v.wait_ge(sv.h, entry)
                v.tensor_copy(out=qf[:], in_=qs[:, sl, :]).then_inc(sv.h, 1)
                v.wait_ge(sv.h, entry + 1)
                v.tensor_scalar(out=qf[:], in0=qf[:], scalar1=2.0 / 15.0, scalar2=-1.0,
                                op0=ALU.mult, op1=ALU.add).then_inc(sv.h, 1)
                v.wait_ge(sv.h, entry + 2)
                if wbg:
                    v.wait_ge(st.h, wbg)
                v.tensor_tensor(
                    out=wbf[:, sl, :].rearrange("p (g k) -> p g k", g=64),
                    in0=qf[:].rearrange("p (g k) -> p g k", g=64),
                    in1=scs[:, i, ot, :].unsqueeze(2).broadcast_to([128, 64, 16]),
                    op=ALU.mult).then_inc(sv.h, 1)
            deq_done = sv.inc(3)
            g_qs[sl] = deq_done - 2

            tb = ot % 2

            @p.on("tensor")
            def _(t, sl=sl, tb=tb, deq=deq_done, tbg=g_tb[tb], cr=cst["ready"]):
                t.wait_ge(sv.h, deq)
                t.wait_ge(cst["sem"].h, cr)
                if tbg:
                    t.wait_ge(sv.h, tbg)
                ins = None
                for dt in range(CT):
                    ins = t.transpose(p.tbanks[tb][:, dt * 128:(dt + 1) * 128],
                                      wbf[:, sl, dt * 128:(dt + 1) * 128], cst["ident_bf"][:])
                ins.then_inc(st.h, 1)
            tr_done = st.inc()
            g_wbf[sl] = tr_done

            @p.on("vector")
            def _(v, ot=ot, j=j, tb=tb, tr=tr_done, wg=wg):
                v.wait_ge(st.h, tr)
                if wg:
                    v.wait_ge(st.h, wg)
                ins = None
                for dt in range(CT):
                    ins = v.tensor_copy(out=wT[j][:, dt, ot * 128:(ot + 1) * 128],
                                        in_=p.tbanks[tb][:, dt * 128:(dt + 1) * 128])
                ins.then_inc(sv.h, 1)
            g_tb[tb] = sv.inc()

        @p.on("gpsimd")
        def _(e, i=i, need=g_labf[0], pre=s_la.n):
            if need:
                e.wait_ge(st.h, need)
            dmaw(e, s_la, pre, out=labf[:], in_=la[i])
        lald = s_la.inc(16)

        @p.on("tensor")
        def _(t, lald=lald, tbg=g_tb[0], cr=cst["ready"]):
            t.wait_ge(s_la.h, lald)
            t.wait_ge(cst["sem"].h, cr)
            if tbg:
                t.wait_ge(sv.h, tbg)
            ins = None
            for dt in range(CT):
                ins = t.transpose(p.tbanks[0][:, dt * 32:(dt + 1) * 32],
                                  labf[:, dt * 128:(dt + 1) * 128], cst["ident_bf"][0:32, 0:32])
            ins.then_inc(st.h, 1)
        latr = st.inc()
        g_labf[0] = latr

        @p.on("vector")
        def _(v, j=j, latr=latr, wg=wg):
            v.wait_ge(st.h, latr)
            if wg:
                v.wait_ge(st.h, wg)
            ins = None
            for dt in range(CT):
                ins = v.tensor_copy(out=laT[j][:, dt, :], in_=p.tbanks[0][:, dt * 32:(dt + 1) * 32])
            ins.then_inc(sv.h, 1)
        g_tb[0] = sv.inc()

        @p.on("gpsimd")
        def _(e, i=i, need=g_lbbf[0], pre=s_lb.n):
            if need:
                e.wait_ge(st.h, need)
            dmaw(e, s_lb, pre, out=lbbf[:], in_=lb[i].rearrange("(ot pp) r -> pp ot r", pp=128))
        lbld = s_lb.inc(16)

        @p.on("tensor")
        def _(t, lbld=lbld, tbg=g_tb[1]):
            t.wait_ge(s_lb.h, lbld)
            if tbg:
                t.wait_ge(sv.h, tbg)
            ins = None
            for ot in range(CT):
                ins = t.transpose(p.tbanks[1][0:32, ot * 128:(ot + 1) * 128],
                                  lbbf[:, ot, :], cst["ident_bf"][:])
            ins.then_inc(st.h, 1)
        lbtr = st.inc()
        g_lbbf[0] = lbtr

        @p.on("vector")
        def _(v, j=j, lbtr=lbtr, wg=wg):
            v.wait_ge(st.h, lbtr)
            if wg:
                v.wait_ge(st.h, wg)
            v.tensor_copy(out=lbT[:, j, :], in_=p.tbanks[1][0:32, 0:1024]).then_inc(sv.h, 1)
        g_tb[1] = sv.inc()

    h_in_r = xr
    for bi in range(nblocks):
        i0, i1 = 2 * bi, 2 * bi + 1
        prep_weights(i0, 0)
        prep_weights(i1, 1)
        w_ready = sv.n

        last = (bi == nblocks - 1)
        for ch in range(NCH):
            pb = ch % 2
            t0 = ch * CW
            prev_store = store_cnt[bi][ch]

            @p.on("sync")
            def _(e, t0=t0, pb=pb, need=sv.n, ps=max(prev_store, g_hn[pb]), pre=s_hf.n, hin=h_in_r):
                if need:
                    e.wait_ge(sv.h, need)
                if ps:
                    e.wait_ge(so.h, ps)
                dmaw(e, s_hf, pre, out=hf[:, :, pb, :], in_=hin[:, :, t0:t0 + CW])
            lds = s_hf.inc(16)

            @p.on("gpsimd")
            def _(e, t0=t0, pb=pb, need=st.n, ps=prev_store, pre=s_hb.n, hin=h_in_r):
                if need:
                    e.wait_ge(st.h, need)
                if ps:
                    e.wait_ge(so.h, ps)
                dmaw(e, s_hb, pre, out=hb[:, :, pb, :], in_=hin[:, :, t0:t0 + CW])
            ldg = s_hb.inc(16)

            @p.on("vector")
            def _(v, ns=lds, ng=ldg, wr=w_ready):
                v.wait_ge(s_hf.h, ns)
                v.wait_ge(s_hb.h, ng)
                v.wait_ge(sv.h, wr)
                v.memset(p.dumv[:], 0.0).then_inc(sv.h, 1)
            ldbar_v = sv.inc()

            @p.on("scalar")
            def _(s, ns=lds, ng=ldg):
                s.wait_ge(s_hf.h, ns)
                s.wait_ge(s_hb.h, ng)
                s.memzero(p.dums[:]).then_inc(ss.h, 1)
            ldbar_s = ss.inc()

            @p.on("tensor")
            def _(t, pb=pb, bar=ldbar_v, bar_s=ldbar_s, gb2=g_bank[2]):
                t.wait_ge(sv.h, bar)
                t.wait_ge(ss.h, bar_s)
                nm2, val2 = gb2
                if val2:
                    t.wait_ge(semmap[nm2].h, val2)
                ins = None
                for dt in range(CT):
                    ins = t.matmul(p.banks[2][0:32, 0:CW], laT[0][:, dt, :], hb[:, dt, pb, :],
                                   start=(dt == 0), stop=(dt == CT - 1))
                ins.then_inc(st.h, 1)
            yr_done = st.inc()

            @p.on("vector")
            def _(v, yr=yr_done, entry=sv.n):
                v.wait_ge(st.h, yr)
                v.wait_ge(sv.h, entry)
                v.tensor_copy(out=yr_bf[:], in_=p.banks[2][0:32, 0:CW]).then_inc(sv.h, 1)
            yr_ev = sv.inc()
            g_bank[2] = ("sv", yr_ev)

            for ot in range(CT):
                bk = (0, 1, 5)[ot % 3]

                @p.on("tensor")
                def _(t, ot=ot, bk=bk, pb=pb, yr=yr_ev, gb=g_bank[bk]):
                    if ot == 0:
                        t.wait_ge(sv.h, yr)
                    nm, val = gb
                    if val:
                        t.wait_ge(semmap[nm].h, val)
                    for dt in range(CT):
                        t.matmul(p.banks[bk][:, 0:CW], wT[0][:, dt, ot * 128:(ot + 1) * 128],
                                 hb[:, dt, pb, :], start=(dt == 0), stop=False)
                    t.matmul(p.banks[bk][:, 0:CW], lbT[0:32, 0, ot * 128:(ot + 1) * 128],
                             yr_bf[:], start=False, stop=True).then_inc(st.h, 1)
                mm_done = st.inc()

                @p.on("scalar")
                def _(s, ot=ot, bk=bk, i=i0, mmd=mm_done):
                    s.wait_ge(st.h, mmd)
                    s.activation(out=y_sb[:, ot, :], in_=p.banks[bk][:, 0:CW], func=AF.Identity,
                                 bias=qbc[:, i, ot:ot + 1]).then_inc(ss.h, 1)
                g_bank[bk] = ("ss", ss.inc())
            y1_done_s = ss.n

            r = ln_chunk(
                p, cst, CW,
                xf=lambda ct: y_sb[:, ct, :],
                xb=None,
                sq=lambda ct: sq[:, ct, :],
                tmp=lambda ct: z_sb[:, ct, :],
                rm=rm[:], rr=rr[:], sems=sems,
                bank_a=p.banks[3], bank_b=p.banks[4],
                out_f32=lambda ct: z_sb[:, ct, :],
                bank_guard=max(g_bank[3][1], g_bank[4][1]), sq_guard=g_sq,
                x_dep=(ss.h, y1_done_s),
                xb_scr=lambda ct: ybf[:, ct, :],
            )
            g_bank[3] = ("sv", r["var"])
            g_bank[4] = ("sv", r["var"])
            g_sq = r["stats"]

            @p.on("scalar")
            def _(s, bi=bi, zv=r["v"]):
                s.wait_ge(sv.h, zv)
                ins = None
                for ct in range(CT):
                    ins = s.activation(out=g_bf[:, ct, :], in_=z_sb[:, ct, :], func=AF.Gelu,
                                       scale=lngs[:, bi, ct:ct + 1], bias=lnbs[:, bi, ct:ct + 1])
                ins.then_inc(ss.h, 1)
            gelu_done = ss.inc()

            @p.on("tensor")
            def _(t, gd=gelu_done, gb2=g_bank[2]):
                t.wait_ge(ss.h, gd)
                nm2, val2 = gb2
                if val2:
                    t.wait_ge(semmap[nm2].h, val2)
                ins = None
                for dt in range(CT):
                    ins = t.matmul(p.banks[2][0:32, 0:CW], laT[1][:, dt, :], g_bf[:, dt, :],
                                   start=(dt == 0), stop=(dt == CT - 1))
                ins.then_inc(st.h, 1)
            yr2_done = st.inc()

            @p.on("vector")
            def _(v, yr=yr2_done, entry=sv.n):
                v.wait_ge(st.h, yr)
                v.wait_ge(sv.h, entry)
                v.tensor_copy(out=yr_bf[:], in_=p.banks[2][0:32, 0:CW]).then_inc(sv.h, 1)
            yr2_ev = sv.inc()
            g_bank[2] = ("sv", yr2_ev)

            for ot in range(CT):
                bk = (0, 1, 5)[ot % 3]

                @p.on("tensor")
                def _(t, ot=ot, bk=bk, yr=yr2_ev, gb=g_bank[bk]):
                    if ot == 0:
                        t.wait_ge(sv.h, yr)
                    nm, val = gb
                    if val:
                        t.wait_ge(semmap[nm].h, val)
                    for dt in range(CT):
                        t.matmul(p.banks[bk][:, 0:CW], wT[1][:, dt, ot * 128:(ot + 1) * 128],
                                 g_bf[:, dt, :], start=(dt == 0), stop=False)
                    t.matmul(p.banks[bk][:, 0:CW], lbT[0:32, 1, ot * 128:(ot + 1) * 128],
                             yr_bf[:], start=False, stop=True).then_inc(st.h, 1)
                mm2_done = st.inc()

                @p.on("vector")
                def _(v, ot=ot, bk=bk, i=i1, pb=pb, mmd=mm2_done, entry=sv.n, gd=gelu_done):
                    v.wait_ge(st.h, mmd)
                    v.wait_ge(sv.h, entry)
                    if ot == 0:
                        v.wait_ge(ss.h, gd)
                    v.tensor_scalar(out=z_sb[:, ot, :], in0=p.banks[bk][:, 0:CW],
                                    scalar1=qbc[:, i, ot:ot + 1], scalar2=None,
                                    op0=ALU.add).then_inc(sv.h, 1)
                    v.wait_ge(sv.h, entry + 1)
                    v.tensor_tensor(out=hf[:, ot, pb, :], in0=z_sb[:, ot, :],
                                    in1=hf[:, ot, pb, :], op=ALU.add).then_inc(sv.h, 1)
                g_bank[bk] = ("sv", sv.inc(2))
            h_done_v = sv.n

            if last:
                @p.on("scalar")
                def _(s, pb=pb, hd=h_done_v):
                    s.wait_ge(sv.h, hd)
                    ins = None
                    for ct in range(CT):
                        ins = s.activation(out=hf[:, ct, pb, :], in_=hf[:, ct, pb, :], func=AF.Silu)
                    ins.then_inc(ss.h, 1)
                sil_done = ss.inc()

                @p.on("sync")
                def _(e, t0=t0, pb=pb, sd_=sil_done, pre=so.n):
                    e.wait_ge(ss.h, sd_)
                    dmaw(e, so, pre, out=gr[:, :, t0:t0 + CW], in_=hf[:, :, pb, :])
            else:
                @p.on("sync")
                def _(e, t0=t0, pb=pb, hd=h_done_v, pre=so.n):
                    e.wait_ge(sv.h, hd)
                    dmaw(e, so, pre, out=hr[:, :, t0:t0 + CW], in_=hf[:, :, pb, :])
            store_cnt[bi + 1][ch] = g_hn[pb] = so.inc(16)

        g_wT_free[0] = st.n
        g_wT_free[1] = st.n
        h_in_r = hr

    @p.on("sync")
    def _(e, need=so.n):
        e.wait_ge(so.h, need)


def build_s2(nblocks=3, nch_lim=None):
    p = Prog(); p.mkdum(); p.init_psum(6, 2)
    xnT = p.dram("xnT", [C, S, T], F32, kind="ExternalInput")
    qw = p.dram("qw", [6, C, C], I32, kind="ExternalInput")
    qsc = p.dram("qsc", [128, 6, CT, 64], F32, kind="ExternalInput")
    qb = p.dram("qb", [128, 6, CT], F32, kind="ExternalInput")
    la = p.dram("la", [6, 32, C], F32, kind="ExternalInput")
    lb = p.dram("lb", [6, C, 32], F32, kind="ExternalInput")
    lng = p.dram("lng", [128, 3, CT], F32, kind="ExternalInput")
    lnb = p.dram("lnb", [128, 3, CT], F32, kind="ExternalInput")
    gT = p.dram("gT", [C, S, T], F32, kind="ExternalOutput")
    hA = p.dram("hA", [C, S, T], F32, kind="Internal")
    cst = emit_consts(p)
    with p.stage():
        emit_s2(p, cst, xnT, qw, qsc, qb, la, lb, lng, lnb, gT, hA, nblocks, nch_lim)
    return p.finish()


def emit_comb(p, cst, xT, sT, gT, x1T, xn1T=None):
    with_ln = xn1T is not None
    xr = xT.rearrange("(ct p) s t -> p ct (s t)", p=128)
    sr = sT.rearrange("(ct p) s t -> p ct (s t)", p=128)
    gr = gT.rearrange("(ct p) s t -> p ct (s t)", p=128)
    x1r = x1T.rearrange("(ct p) s t -> p ct (s t)", p=128)
    if with_ln:
        xnr = xn1T.rearrange("(ct p) s t -> p ct (s t)", p=128)

    CW = 512
    NCH = TOK // CW

    xf = p.sbuf("xf", [128, CT, 2, CW], F32)
    sf = p.sbuf("sf", [128, CT, 2, CW], F32)
    gf = p.sbuf("gf", [128, CT, 2, CW], F32)
    x1f = p.sbuf("x1f", [128, CT, 2, CW], F32)
    xb = p.sbuf("xb", [128, CT, CW], BF16)
    sq = p.sbuf("sq", [128, CT, CW], BF16)
    onf = p.sbuf("onf", [128, CT, 2, CW], F32)
    rm = p.sbuf("rm", [128, CW], F32)
    rr = p.sbuf("rr", [128, CW], F32)

    s_x = p.sem("s_x")
    s_s = p.sem("s_s")
    s_g = p.sem("s_g")
    so = p.sem("so")
    sv = p.sem("v")
    ss = p.sem("s")
    st = p.sem("t")
    sems = dict(v=sv, s=ss, t=st)

    g_bank = 0
    g_sq = 0
    g_out = [0, 0]
    g_slotv = [0, 0]

    for ci in range(NCH):
        pb = ci % 2
        t0 = ci * CW

        @p.on("sync")
        def _(e, t0=t0, pb=pb, need=g_slotv[pb], prex=s_x.n, preg=s_g.n):
            if need:
                e.wait_ge(sv.h, need)
            dmaw(e, s_x, prex, out=xf[:, :, pb, :], in_=xr[:, :, t0:t0 + CW])
            dmaw(e, s_g, preg, out=gf[:, :, pb, :], in_=gr[:, :, t0:t0 + CW])
        ldx = s_x.inc(16)
        ldg = s_g.inc(16)

        @p.on("gpsimd")
        def _(e, t0=t0, pb=pb, need=g_slotv[pb], pres=s_s.n):
            if need:
                e.wait_ge(sv.h, need)
            dmaw(e, s_s, pres, out=sf[:, :, pb, :], in_=sr[:, :, t0:t0 + CW])
        lds = s_s.inc(16)

        @p.on("vector")
        def _(v, pb=pb, nx=ldx, ng=ldg, ns=lds, entry=sv.n, og=g_out[pb]):
            v.wait_ge(s_x.h, nx)
            v.wait_ge(s_g.h, ng)
            v.wait_ge(s_s.h, ns)
            v.wait_ge(sv.h, entry)
            if og:
                v.wait_ge(so.h, og)
            ins = None
            for ct in range(CT):
                ins = v.tensor_tensor(out=x1f[:, ct, pb, :], in0=sf[:, ct, pb, :],
                                      in1=gf[:, ct, pb, :], op=ALU.mult)
            ins.then_inc(sv.h, 1)
            v.wait_ge(sv.h, entry + 1)
            ins = None
            for ct in range(CT):
                ins = v.tensor_tensor(out=x1f[:, ct, pb, :], in0=x1f[:, ct, pb, :],
                                      in1=xf[:, ct, pb, :], op=ALU.add)
            ins.then_inc(sv.h, 1)
        x1_done = sv.inc(2)

        @p.on("sync")
        def _(e, t0=t0, pb=pb, x1d=x1_done, pre=so.n):
            e.wait_ge(sv.h, x1d)
            dmaw(e, so, pre, out=x1r[:, :, t0:t0 + CW], in_=x1f[:, :, pb, :])
        st1 = so.inc(16)

        if with_ln:
            r = ln_chunk(
                p, cst, CW,
                xf=lambda ct, pb=pb: x1f[:, ct, pb, :],
                xb=None,
                sq=lambda ct: sq[:, ct, :],
                tmp=lambda ct, pb=pb: onf[:, ct, pb, :],
                rm=rm[:], rr=rr[:], sems=sems,
                bank_a=p.banks[0 + 2 * pb], bank_b=p.banks[1 + 2 * pb],
                out_f32=lambda ct, pb=pb: onf[:, ct, pb, :],
                bank_guard=g_bank, sq_guard=g_sq,
                out_guard=(so.h, g_out[pb]),
                x_dep=(sv.h, x1_done),
                xb_scr=lambda ct: xb[:, ct, :],
            )
            g_bank = r["var"]
            g_sq = r["stats"]

            @p.on("sync")
            def _(e, t0=t0, pb=pb, need=r["v"], pre=so.n):
                e.wait_ge(sv.h, need)
                dmaw(e, so, pre, out=xnr[:, :, t0:t0 + CW], in_=onf[:, :, pb, :])
            g_out[pb] = so.inc(16)
            g_slotv[pb] = r["v"]
        else:
            g_out[pb] = st1
            g_slotv[pb] = x1_done

    @p.on("sync")
    def _(e, need=so.n):
        e.wait_ge(so.h, need)


def build_comb(with_ln=True):
    p = Prog(); p.mkdum(); p.init_psum(6, 2)
    xT = p.dram("xT", [C, S, T], F32, kind="ExternalInput")
    sT = p.dram("sT", [C, S, T], F32, kind="ExternalInput")
    gT = p.dram("gT", [C, S, T], F32, kind="ExternalInput")
    x1T = p.dram("x1T", [C, S, T], F32, kind="ExternalOutput")
    xn1T = p.dram("xn1T", [C, S, T], F32, kind="ExternalOutput") if with_ln else None
    cst = emit_consts(p)
    with p.stage():
        emit_comb(p, cst, xT, sT, gT, x1T, xn1T)
    return p.finish()


def emit_s5(p, cst, xnT, wgt, wut, wdt, wrt, rb, g1T, nsc_lim=None):
    xr = xnT.rearrange("(ct p) s t -> p ct (s t)", p=128)
    gr = g1T.rearrange("(ct p) s t -> p ct (s t)", p=128)

    SCW = 512
    SW = SCW
    NSC = TOK // SCW if nsc_lim is None else nsc_lim
    NE = 9
    IT = 16
    pb6 = p.tbanks[0][:].bitcast(F32)
    pb7 = p.tbanks[1][:].bitcast(F32)

    xb = [p.sbuf("xb", [128, CT, SW], BF16) for _ in range(2)]
    _wg = p.sbuf("wg_sb", [128, CT, 2048], BF16)
    _wu = p.sbuf("wu_sb", [128, CT, 2048], BF16)
    wg_sb = [_wg, _wg]
    wu_sb = [_wu, _wu]
    wd_sb = [p.sbuf("wd_sb", [128, IT, C], BF16) for _ in range(2)]
    wr_sb = p.sbuf("wr_sb", [128, CT, 8], BF16)
    rb_sb = p.sbuf("rb_sb", [8, 1], F32)
    e8 = p.sbuf("e8", [8, 8, 128], BF16)
    probsT = p.sbuf("probsT", [8, SW], F32)
    ptm = p.sbuf("ptm", [128, 4, 8], F32)
    m1q = p.sbuf("m1q", [128, 4], F32)
    m2q = p.sbuf("m2q", [128, 4], F32)
    pm = p.sbuf("pm", [128, 4, 8], F32)
    cwtm = p.sbuf("cwtm", [128, 4, 8], F32)
    cwT = p.sbuf("cwT", [8, SW], BF16)
    cwba = p.sbuf("cwba", [128, 8, SW], BF16)
    hid = [p.sbuf("hid", [128, IT, SW], BF16) for _ in range(2)]
    out_sb = p.sbuf("out_sb", [128, CT, SW], F32)

    s_x = p.sem("s_x")
    s_wg = p.sem("s_wg")
    s_wu = p.sem("s_wu")
    s_wd = p.sem("s_wd")
    s_pr = p.sem("s_pr")
    so = p.sem("so")
    sv = p.sem("v")
    ss = p.sem("s")
    st = p.sem("t")
    s_e8 = p.sem("s_e8")
    s_rb = p.sem("s_rb")

    @p.on("gpsimd")
    def _(e):
        dmaw(e, s_pr, 0, out=wr_sb[:], in_=wrt.rearrange("(ct p) e -> p ct e", p=128))
        e.iota(e8[:], pattern=[[1, 8], [0, 128]], channel_multiplier=-1,
               allow_small_or_imprecise_dtypes=True).then_inc(s_e8.h, 1)
    pr1 = s_pr.inc(16)
    e8i = s_e8.inc()

    @p.on("sync")
    def _(e):
        dmaw(e, s_rb, 0, out=rb_sb[:], in_=rb)
    pr2 = s_rb.inc(16)

    @p.on("vector")
    def _(v, pr=pr2, cr=cst["ready"], e8i_=e8i, w1=pr1):
        v.wait_ge(s_rb.h, pr)
        v.wait_ge(s_pr.h, w1)
        v.wait_ge(s_e8.h, e8i_)
        v.wait_ge(cst["sem"].h, cr)
        v.tensor_scalar(out=e8[:], in0=e8[:], scalar1=0.0, scalar2=None,
                        op0=ALU.is_equal).then_inc(sv.h, 1)
    e8_done = sv.inc()

    g_xb = [0, 0]
    g_cwT4 = [0]
    g_pb7 = [0]
    g_wgu = [0]
    g_wd = [0, 0]
    g_hid = [0, 0]
    g_out = 0
    g_p67 = [("sv", 0)]
    g_AB = [("ss", 0), ("ss", 0)]
    g_AB2 = [("sv", 0), ("sv", 0)]
    g_D = [("sv", 0), ("sv", 0)]
    semmap = dict(sv=sv, ss=ss, st=st)

    def wgd(t, g):
        nm, val = g
        if val:
            t.wait_ge(semmap[nm].h, val)

    def prologue(scn):
        slot = scn % 2
        t0 = scn * SCW
        state = {}

        def step1():
            nonlocal_g = None

            @p.on("gpsimd")
            def _(e, t0=t0, slot=slot, need=g_xb[slot], pre=s_x.n):
                if need:
                    e.wait_ge(st.h, need)
                dmaw(e, s_x, pre, out=xb[slot][:], in_=xr[:, :, t0:t0 + SCW])
            state["xld"] = s_x.inc(16)

            @p.on("tensor")
            def _(t, slot=slot, xld=state["xld"], ed=e8_done, g67=g_p67[0]):
                t.wait_ge(s_x.h, xld)
                t.wait_ge(sv.h, ed)
                wgd(t, g67)
                ins = None
                for kt in range(CT):
                    ins = t.matmul(pb6[0:8, 0:SW], wr_sb[:, kt, :],
                                   xb[slot][:, kt, :], start=(kt == 0), stop=(kt == CT - 1))
                ins.then_inc(st.h, 1)
            rt_done = st.inc()

            @p.on("scalar")
            def _(s_, rt=rt_done, entry=ss.n):
                s_.wait_ge(st.h, rt)
                s_.wait_ge(ss.h, entry)
                s_.activation(out=probsT[:], in_=pb6[0:8, 0:SW],
                              func=AF.Sigmoid, bias=rb_sb[:, 0:1]).then_inc(ss.h, 1)
            state["sig"] = ss.inc()

        def step2():
            @p.on("tensor")
            def _(t, sg=state["sig"], cr=cst["ready"], p7=g_pb7[0]):
                t.wait_ge(ss.h, sg)
                t.wait_ge(cst["sem"].h, cr)
                if p7:
                    t.wait_ge(sv.h, p7)
                ins = None
                for q in range(4):
                    ins = t.transpose(pb7[0:128, q * 8:q * 8 + 8],
                                      probsT[:, q * 128:(q + 1) * 128], cst["ident_f32"][0:8, 0:8])
                ins.then_inc(st.h, 1)
            ptr_done = st.inc()

            @p.on("vector")
            def _(v, tr=ptr_done, entry=sv.n):
                v.wait_ge(st.h, tr)
                v.wait_ge(sv.h, entry)
                v.tensor_copy(out=ptm[:],
                              in_=pb7[0:128, 0:32].rearrange("p (q e) -> p q e", e=8)).then_inc(sv.h, 1)
                v.wait_ge(sv.h, entry + 1)
                v.reduce_max(out=m1q[:], in_=ptm[:], axis=AXX).then_inc(sv.h, 1)
                v.wait_ge(sv.h, entry + 2)
                ins = None
                for q in range(4):
                    ins = v.tensor_scalar(out=pm[:, q, :], in0=ptm[:, q, :],
                                          scalar1=m1q[:, q:q + 1], scalar2=-1e30,
                                          op0=ALU.is_equal, op1=ALU.mult)
                ins.then_inc(sv.h, 1)
                v.wait_ge(sv.h, entry + 3)
                v.tensor_tensor(out=pm[:], in0=pm[:], in1=ptm[:], op=ALU.add).then_inc(sv.h, 1)
                v.wait_ge(sv.h, entry + 4)
                v.reduce_max(out=m2q[:], in_=pm[:], axis=AXX).then_inc(sv.h, 1)
                v.wait_ge(sv.h, entry + 5)
                ins = None
                for q in range(4):
                    ins = v.tensor_scalar(out=cwtm[:, q, :], in0=ptm[:, q, :],
                                          scalar1=m2q[:, q:q + 1], scalar2=None, op0=ALU.is_ge)
                ins.then_inc(sv.h, 1)
                v.wait_ge(sv.h, entry + 6)
                v.tensor_tensor(out=cwtm[:], in0=cwtm[:], in1=ptm[:], op=ALU.mult).then_inc(sv.h, 1)
            state["cw"] = sv.inc(7)

        def step3():
            @p.on("tensor")
            def _(t, cd=state["cw"]):
                t.wait_ge(sv.h, cd)
                ins = None
                for q in range(4):
                    ins = t.transpose(pb7[0:8, q * 128:(q + 1) * 128],
                                      cwtm[:, q, :], cst["ident_f32"][:])
                ins.then_inc(st.h, 1)
            cwtr = st.inc()

            @p.on("vector")
            def _(v, tr=cwtr, entry=sv.n, c4=g_cwT4[0]):
                v.wait_ge(st.h, tr)
                v.wait_ge(sv.h, entry)
                if c4:
                    v.wait_ge(st.h, c4)
                v.tensor_copy(out=cwT[:], in_=pb7[0:8, 0:SW]).then_inc(sv.h, 1)
            state["cwT"] = sv.inc()
            g_pb7[0] = state["cwT"]

        def step4():
            ev_prev = 0
            for eb in range(8):
                @p.on("tensor")
                def _(t, eb=eb, cv=state["cwT"], evp=ev_prev):
                    t.wait_ge(sv.h, cv if eb == 0 else evp)
                    t.matmul(pb6[:, 0:SW], e8[:, eb, :], cwT[:],
                             start=True, stop=True).then_inc(st.h, 1)
                bmm = st.inc()

                @p.on("vector")
                def _(v, eb=eb, mmd=bmm, entry=sv.n):
                    v.wait_ge(st.h, mmd)
                    v.wait_ge(sv.h, entry)
                    v.tensor_copy(out=cwba[:, eb, :], in_=pb6[:, 0:SW]).then_inc(sv.h, 1)
                ev_prev = sv.inc()
            g_cwT4[0] = st.n
            g_p67[0] = ("sv", ev_prev)
            state["ready"] = ev_prev

        state["steps"] = [step1, step2, step3]
        state["step4"] = step4
        state["s4"] = False
        return state

    def experts(scn, pro):
        nonlocal g_out
        slot = scn % 2
        t0 = scn * SCW
        while pro["steps"]:
            pro["steps"].pop(0)()
        if not pro["s4"]:
            pro["step4"]()
            pro["s4"] = True
        xld = pro["xld"]
        g_hid_ready = [0, 0]
        g_wd_ld = [0, 0]
        nxt = prologue(scn + 1) if scn + 1 < NSC else None
        if nxt is not None:
            pro_next[0] = nxt

        def emit_down(e_i):
            hb = e_i % 2
            last_dn = 0
            for ot in range(CT):
                bk = 4 + (ot % 2)

                @p.on("tensor")
                def _(t, e_i=e_i, ot=ot, bk=bk, hb=hb, hd=g_hid_ready[hb], wd_=g_wd_ld[hb],
                      gD=g_D[ot % 2]):
                    t.wait_ge(sv.h, hd)
                    t.wait_ge(s_wd.h, wd_)
                    wgd(t, gD)
                    ins = None
                    for kt in range(IT):
                        ins = t.matmul(p.banks[bk][:, 0:SW], wd_sb[hb][:, kt, ot * 128:(ot + 1) * 128],
                                       hid[hb][:, kt, :], start=(kt == 0), stop=(kt == IT - 1))
                    ins.then_inc(st.h, 1)
                dn_done = st.inc()
                last_dn = dn_done

                @p.on("vector")
                def _(v, e_i=e_i, ot=ot, bk=bk, mmd=dn_done, entry=sv.n, og=g_out):
                    v.wait_ge(st.h, mmd)
                    v.wait_ge(sv.h, entry)
                    if e_i == 0 and og:
                        v.wait_ge(so.h, og)
                    if e_i == 0:
                        v.tensor_copy(out=out_sb[:, ot, :], in_=p.banks[bk][:, 0:SW]).then_inc(sv.h, 1)
                    else:
                        v.tensor_tensor(out=out_sb[:, ot, :], in0=p.banks[bk][:, 0:SW],
                                        in1=out_sb[:, ot, :], op=ALU.add).then_inc(sv.h, 1)
                g_D[ot % 2] = ("sv", sv.inc())
            g_wd[hb] = last_dn
            g_hid[hb] = last_dn

        for e_i in range(NE):
            b = e_i % 2

            @p.on("gpsimd")
            def _(e, e_i=e_i, b=b, need=g_wgu[0], pre=s_wg.n):
                if need:
                    e.wait_ge(st.h, need)
                dmaw(e, s_wg, pre, out=wg_sb[b][:], in_=wgt[e_i].rearrange("(kt p) i -> p kt i", p=128))
            wgld = s_wg.inc(16)

            @p.on("gpsimd")
            def _(e, e_i=e_i, b=b, need=g_wgu[0], pre=s_wu.n):
                if need:
                    e.wait_ge(st.h, need)
                dmaw(e, s_wu, pre, out=wu_sb[b][:], in_=wut[e_i].rearrange("(kt p) i -> p kt i", p=128))
            wuld = s_wu.inc(16)

            @p.on("gpsimd")
            def _(e, e_i=e_i, b=b, need=g_wd[b], pre=s_wd.n):
                if need:
                    e.wait_ge(st.h, need)
                dmaw(e, s_wd, pre, out=wd_sb[b][:], in_=wdt[e_i].rearrange("(kt p) h -> p kt h", p=128))
            g_wd_ld[b] = s_wd.inc(16)

            last_hid = 0
            for mt in range(IT):
                bkg = mt % 2
                bku = 2 + mt % 2

                @p.on("tensor")
                def _(t, e_i=e_i, b=b, mt=mt, bkg=bkg, bku=bku, wg_=wgld, wu_=wuld,
                      gA=g_AB[bkg], gB=g_AB2[mt % 2],
                      xld=xld, hg=(g_hid[b] if mt == 0 else 0)):
                    t.wait_ge(s_wg.h, wg_)
                    t.wait_ge(s_wu.h, wu_)
                    t.wait_ge(s_x.h, xld)
                    if hg:
                        t.wait_ge(st.h, hg)
                    wgd(t, gA)
                    for kt in range(CT):
                        t.matmul(p.banks[bkg][:, 0:SW], wg_sb[b][:, kt, mt * 128:(mt + 1) * 128],
                                 xb[slot][:, kt, :], start=(kt == 0), stop=(kt == CT - 1))
                    wgd(t, gB)
                    ins = None
                    for kt in range(CT):
                        ins = t.matmul(p.banks[bku][:, 0:SW], wu_sb[b][:, kt, mt * 128:(mt + 1) * 128],
                                       xb[slot][:, kt, :], start=(kt == 0), stop=(kt == CT - 1))
                    ins.then_inc(st.h, 1)
                gu_done = st.inc()

                @p.on("scalar")
                def _(s, b=b, mt=mt, bkg=bkg, mmd=gu_done, entry=ss.n,
                      hg=(g_hid[b] if mt == 0 else 0)):
                    s.wait_ge(st.h, mmd)
                    s.wait_ge(ss.h, entry)
                    if hg:
                        s.wait_ge(st.h, hg)
                    s.activation(out=hid[b][:, mt, :], in_=p.banks[bkg][:, 0:SW],
                                 func=AF.Silu).then_inc(ss.h, 1)
                sil_done = ss.inc()
                g_AB[bkg] = ("ss", sil_done)

                @p.on("vector")
                def _(v, e_i=e_i, b=b, mt=mt, bku=bku, sd_=sil_done, entry=sv.n):
                    v.wait_ge(ss.h, sd_)
                    v.wait_ge(sv.h, entry)
                    v.tensor_tensor(out=hid[b][:, mt, :], in0=hid[b][:, mt, :],
                                    in1=p.banks[bku][:, 0:SW], op=ALU.mult).then_inc(sv.h, 1)
                    if e_i > 0:
                        v.wait_ge(sv.h, entry + 1)
                        v.tensor_tensor(out=hid[b][:, mt, :], in0=hid[b][:, mt, :],
                                        in1=cwba[:, e_i - 1, :], op=ALU.mult).then_inc(sv.h, 1)
                nops = 1 if e_i == 0 else 2
                last_hid = sv.inc(nops)
                first_done = last_hid - (nops - 1)
                g_AB2[mt % 2] = ("sv", first_done)
            g_wgu[0] = st.n
            g_hid_ready[b] = last_hid

            if e_i > 0:
                emit_down(e_i - 1)
            if nxt is not None and e_i in (1, 3, 5) and nxt["steps"]:
                nxt["steps"].pop(0)()
        emit_down(NE - 1)
        if nxt is not None:
            nxt["step4"]()
            nxt["s4"] = True
        acc_done = sv.n
        g_xb[slot] = st.n

        @p.on("scalar")
        def _(s, ad=acc_done, entry=ss.n):
            s.wait_ge(sv.h, ad)
            s.wait_ge(ss.h, entry)
            ins = None
            for ct in range(CT):
                ins = s.activation(out=out_sb[:, ct, :], in_=out_sb[:, ct, :], func=AF.Silu)
            ins.then_inc(ss.h, 1)
        osil = ss.inc()

        @p.on("sync")
        def _(e, t0=t0, od=osil, pre=so.n):
            e.wait_ge(ss.h, od)
            dmaw(e, so, pre, out=gr[:, :, t0:t0 + SCW], in_=out_sb[:])
        g_out = so.inc(16)

    pro_next = [prologue(0)]
    for scn in range(NSC):
        experts(scn, pro_next[0])

    @p.on("sync")
    def _(e, need=so.n):
        e.wait_ge(so.h, need)


def build_s5(nsc_lim=None):
    p = Prog(); p.mkdum(); p.init_psum(6, 2)
    xnT = p.dram("xnT", [C, S, T], F32, kind="ExternalInput")
    wgt = p.dram("wgt", [9, C, 2048], F32, kind="ExternalInput")
    wut = p.dram("wut", [9, C, 2048], F32, kind="ExternalInput")
    wdt = p.dram("wdt", [9, 2048, C], F32, kind="ExternalInput")
    wrt = p.dram("wrt", [C, 8], F32, kind="ExternalInput")
    rb = p.dram("rb", [8, 1], F32, kind="ExternalInput")
    g1T = p.dram("g1T", [C, S, T], F32, kind="ExternalOutput")
    cst = emit_consts(p)
    with p.stage():
        emit_s5(p, cst, xnT, wgt, wut, wdt, wrt, rb, g1T, nsc_lim)
    return p.finish()


def emit_s4(p, cst, xnT, qkvw, outw, w1, w2, qkvb, outb, b1c, b2c, g1c, be1c, g2c, be2c,
            sT, nseq=S, ncph=None):
    xr = xnT.rearrange("(ct p) s t -> p ct s t", p=128)
    sr = sT.rearrange("(ct p) s t -> p ct s t", p=128)

    CW = 512
    NCH = T // CW if ncph is None else ncph
    NH = 16

    wq = p.sbuf("wq", [128, 8, 1024], BF16)
    w1v = [wq[:, 0, :].rearrange("p (k c) -> p k c", k=8),
           wq[:, 1, :].rearrange("p (k c) -> p k c", k=8)]
    wse = wq[:, 4:8, :].rearrange("p a b -> p (a b)").rearrange("p (k c) -> p k c", k=32)
    parms = p.sbuf("parms", [128, 104], F32)
    qfT = p.sbuf("qfT", [128, CT, T], BF16)
    ktm = p.sbuf("ktm", [128, 4, C], BF16)
    vtm = p.sbuf("vtm", [128, 4, C], BF16)
    kvbf = p.sbuf("kvbf", [128, 8, 64], BF16)
    zbf = p.sbuf("zbf", [128, CT, 16], BF16)
    z_sb = p.sbuf("z_sb", [128, CT], F32)
    z_pt = p.sbuf("z_pt", [128, CT], F32)
    outw_sb = p.sbuf("outw_sb", [128, CT, C], BF16)
    e16 = p.sbuf("e16", [16, 8, 2, 64], BF16)
    xf = p.sbuf("xf", [128, CT, CW], F32)
    xbfc = p.sbuf("xbfc", [128, CT, CW], BF16)
    sq = p.sbuf("sq", [128, CT, CW], BF16)
    h1b = p.sbuf("h1b", [128, CT, CW], BF16)
    tmpf = p.sbuf("tmpf", [128, CT, CW], F32)
    rm = p.sbuf("rm", [128, CW], F32)
    rr = p.sbuf("rr", [128, CW], F32)
    emax = rm
    emin = rr
    rden = p.sbuf("rden", [16, CW], BF16)
    rdf = p.sbuf("rdf", [16, CW], F32)
    rbc = p.sbuf("rbc", [128, CW], BF16)
    xA = p.sbuf("xA", [128, CT, CW], F32)
    h2b = p.sbuf("h2b", [128, CT, CW], BF16)
    ffb = p.sbuf("ffb", [128, 32, CW], BF16)
    kfc = ffb[:, 0:8, :]
    vtc = ffb[:, 8:16, :]
    yT = ffb[:, 16:24, :]
    sff = tmpf

    s_x = p.sem("s_x")
    s_xb = p.sem("s_xb")
    s_w = p.sem("s_w")
    s_w2 = p.sem("s_w2")
    s_pp = p.sem("s_pp")
    so = p.sem("so")
    sv = p.sem("v")
    ss = p.sem("s")
    st = p.sem("t")
    s_e16 = p.sem("s_e16")
    sems = dict(v=sv, s=ss, t=st)
    semmap = dict(sv=sv, ss=ss, st=st)

    def wgd(t, g):
        nm, val = g
        if val:
            t.wait_ge(semmap[nm].h, val)

    OFF_QKVB, OFF_B1, OFF_OUTB, OFF_B2, OFF_G1, OFF_BE1, OFF_G2, OFF_BE2 = 0, 24, 56, 64, 72, 80, 88, 96

    @p.on("sync")
    def _(e):
        dmaw(e, s_pp, 0, out=parms[:, 0:24], in_=qkvb)
        dmaw(e, s_pp, 16, out=parms[:, 24:56], in_=b1c)
        dmaw(e, s_pp, 32, out=parms[:, 56:64], in_=outb)
        dmaw(e, s_pp, 48, out=parms[:, 64:72], in_=b2c)
        dmaw(e, s_pp, 64, out=parms[:, 72:80], in_=g1c)
        dmaw(e, s_pp, 80, out=parms[:, 80:88], in_=be1c)
        dmaw(e, s_pp, 96, out=parms[:, 88:96], in_=g2c)
        dmaw(e, s_pp, 112, out=parms[:, 96:104], in_=be2c)
    pld = s_pp.inc(128)

    @p.on("gpsimd")
    def _(e):
        e.iota(e16[:], pattern=[[2, 8], [1, 2], [0, 64]], channel_multiplier=-1,
               allow_small_or_imprecise_dtypes=True).then_inc(s_e16.h, 1)
        dmaw(e, s_w2, 0, out=outw_sb[:], in_=outw.rearrange("(kt p) o -> p kt o", p=128))
    e16i = s_e16.inc()
    ow_ld = s_w2.inc(16)

    @p.on("vector")
    def _(v, ei=e16i, cr=cst["ready"], pl=pld):
        v.wait_ge(s_e16.h, ei)
        v.wait_ge(cst["sem"].h, cr)
        v.wait_ge(s_pp.h, pl)
        v.tensor_scalar(out=e16[:], in0=e16[:], scalar1=0.0, scalar2=None,
                        op0=ALU.is_equal).then_inc(sv.h, 1)
    e16_done = sv.inc()

    pcol = lambda off, i: parms[:, off + i:off + i + 1]

    g_w = ("st", 0)
    g_w2 = ("st", 0)
    g_AB = [("ss", 0), ("ss", 0)]
    g_DEN = ("sv", 0)
    g_BC = ("sv", 0)
    g_T = [("sv", 0), ("sv", 0)]
    g_sq = 0
    g_tm = 0
    g_xslot = ("sv", 0)
    g_store = 0
    g_qf = 0
    g_h1b = 0
    g_h2b = 0
    g_ffb = 0
    g_rden = 0
    g_w1s = [0, 0]
    g_B23 = [("sv", 0), ("sv", 0)]

    for sqi in range(nseq):
        g_tm = st.n
        for ch in range(NCH):
            t0 = ch * CW

            @p.on("sync")
            def _(e, sqi=sqi, t0=t0, gx=g_xslot, pre=s_x.n):
                nm, val = gx
                if val:
                    e.wait_ge(semmap[nm].h, val)
                dmaw(e, s_x, pre, out=xf[:], in_=xr[:, :, sqi, t0:t0 + CW])
            xld = s_x.inc(16)

            @p.on("gpsimd")
            def _(e, sqi=sqi, t0=t0, need=g_sq, pre=s_xb.n):
                if need:
                    e.wait_ge(st.h, need)
                dmaw(e, s_xb, pre, out=xbfc[:], in_=xr[:, :, sqi, t0:t0 + CW])
            xbld = s_xb.inc(16)

            @p.on("vector")
            def _(v, nx=xld, nb=xbld, ed=e16_done):
                v.wait_ge(s_x.h, nx)
                v.wait_ge(s_xb.h, nb)
                v.wait_ge(sv.h, ed)
                v.memset(p.dumv[:], 0.0).then_inc(sv.h, 1)
            bar_v = sv.inc()

            @p.on("scalar")
            def _(s, nx=xld, nb=xbld):
                s.wait_ge(s_x.h, nx)
                s.wait_ge(s_xb.h, nb)
                s.memzero(p.dums[:]).then_inc(ss.h, 1)
            bar_s = ss.inc()

            r = ln_chunk(
                p, cst, CW,
                xf=lambda ct: xf[:, ct, :],
                xb=lambda ct: xbfc[:, ct, :],
                sq=lambda ct: sq[:, ct, :],
                tmp=lambda ct: tmpf[:, ct, :],
                rm=rm[:], rr=rr[:], sems=sems,
                bank_a=p.banks[4], bank_b=p.banks[5],
                out_bf=lambda ct: h1b[:, ct, :],
                gcol=lambda ct: pcol(OFF_G1, ct), bcol=lambda ct: pcol(OFF_BE1, ct),
                bank_guard=max(g_DEN[1], g_BC[1]), sq_guard=g_sq,
                x_dep=(ss.h, bar_s),
                out_guard=[(st.h, g_h1b), (so.h, g_store)],
            )
            g_DEN = ("sv", r["var"]); g_BC = ("sv", r["var"])
            g_sq = r["stats"]
            h1_done_s = r["s"]
            g_xslot = ("sv", r["v"])

            for part in range(3):
                @p.on("gpsimd")
                def _(e, part=part, gw=g_w, pre=s_w.n):
                    nm, val = gw
                    if val:
                        e.wait_ge(semmap[nm].h, val)
                    dmaw(e, s_w, pre, out=wq[:],
                         in_=qkvw.rearrange("(kt p) o -> p kt o", p=128)[:, :, part * 1024:(part + 1) * 1024])
                wld = s_w.inc(16)

                for m8 in range(CT):
                    mt = part * 8 + m8
                    bk = m8 % 2

                    @p.on("tensor")
                    def _(t, m8=m8, bk=bk, wld=wld, hd=h1_done_s, gb=g_AB[bk]):
                        t.wait_ge(s_w.h, wld)
                        t.wait_ge(ss.h, hd)
                        wgd(t, gb)
                        ins = None
                        for kt in range(CT):
                            ins = t.matmul(p.banks[bk][:, 0:CW],
                                           wq[:, kt, m8 * 128:(m8 + 1) * 128],
                                           h1b[:, kt, :], start=(kt == 0), stop=(kt == CT - 1))
                        ins.then_inc(st.h, 1)
                    mm_done = st.inc()
                    g_w = ("st", mm_done)

                    if part < 2:
                        if part == 0:
                            dst = (lambda m8=m8, t0=t0: qfT[:, m8, t0:t0 + CW])
                        else:
                            dst = (lambda m8=m8: kfc[:, m8, :])

                        @p.on("vector")
                        def _(v, mt=mt, bk=bk, mmd=mm_done, entry=sv.n,
                              qg=(g_qf if part == 0 else g_tm)):
                            v.wait_ge(st.h, mmd)
                            v.wait_ge(sv.h, entry)
                            if qg:
                                v.wait_ge(st.h, qg)
                            v.tensor_scalar(out=emax[:], in0=p.banks[bk][:, 0:CW],
                                            scalar1=pcol(OFF_QKVB, mt), scalar2=0.0,
                                            op0=ALU.add, op1=ALU.max).then_inc(sv.h, 1)
                            v.wait_ge(sv.h, entry + 1)
                            v.tensor_scalar(out=emin[:], in0=p.banks[bk][:, 0:CW],
                                            scalar1=pcol(OFF_QKVB, mt), scalar2=0.0,
                                            op0=ALU.add, op1=ALU.min).then_inc(sv.h, 1)
                        fm1 = sv.inc(2)

                        @p.on("scalar")
                        def _(s, fm=fm1, entry=ss.n):
                            s.wait_ge(sv.h, fm)
                            s.wait_ge(ss.h, entry)
                            s.activation(out=emin[:], in_=emin[:], func=AF.Exp).then_inc(ss.h, 1)
                        ex_done = ss.inc()

                        @p.on("vector")
                        def _(v, d=dst, xd=ex_done, entry=sv.n):
                            v.wait_ge(ss.h, xd)
                            v.wait_ge(sv.h, entry)
                            v.tensor_tensor(out=d(), in0=emax[:], in1=emin[:], op=ALU.add).then_inc(sv.h, 1)
                        g_AB[bk] = ("sv", sv.inc())
                    else:
                        @p.on("vector")
                        def _(v, mt=mt, m8=m8, bk=bk, mmd=mm_done, entry=sv.n, tg=g_tm):
                            v.wait_ge(st.h, mmd)
                            v.wait_ge(sv.h, entry)
                            if tg:
                                v.wait_ge(st.h, tg)
                            v.tensor_scalar(out=vtc[:, m8, :], in0=p.banks[bk][:, 0:CW],
                                            scalar1=pcol(OFF_QKVB, mt), scalar2=None,
                                            op0=ALU.add).then_inc(sv.h, 1)
                        g_AB[bk] = ("sv", sv.inc())
            fm_all = sv.n
            g_h1b = st.n

            @p.on("vector")
            def _(v, ch=ch, entry=sv.n):
                v.wait_ge(sv.h, entry)
                if ch == 0:
                    v.reduce_sum(out=z_sb[:], in_=kfc[:, :, :], axis=AXX).then_inc(sv.h, 1)
                else:
                    v.reduce_sum(out=z_pt[:], in_=kfc[:, :, :], axis=AXX).then_inc(sv.h, 1)
                    v.wait_ge(sv.h, entry + 1)
                    v.tensor_tensor(out=z_sb[:], in0=z_sb[:], in1=z_pt[:], op=ALU.add).then_inc(sv.h, 1)
            zacc = sv.inc(1 if ch == 0 else 2)

            for which in range(2):
                src = kfc if which == 0 else vtc
                dtm = ktm if which == 0 else vtm
                for tt in range(4):
                    tb = (which * 4 + tt) % 2

                    @p.on("tensor")
                    def _(t, src=src, tt=tt, tb=tb, fa=fm_all, gT_=g_T[tb], cr=cst["ready"]):
                        t.wait_ge(sv.h, fa)
                        wgd(t, gT_)
                        ins = None
                        for ct in range(CT):
                            ins = t.transpose(p.tbanks[tb][:, ct * 128:(ct + 1) * 128],
                                              src[:, ct, tt * 128:(tt + 1) * 128], cst["ident_bf"][:])
                        ins.then_inc(st.h, 1)
                    trd = st.inc()

                    @p.on("vector")
                    def _(v, dtm=dtm, tt=tt, tb=tb, trd=trd, entry=sv.n, tg=g_tm):
                        v.wait_ge(st.h, trd)
                        v.wait_ge(sv.h, entry)
                        if tg:
                            v.wait_ge(st.h, tg)
                        v.tensor_copy(out=dtm[:, tt, :], in_=p.tbanks[tb][:, 0:1024]).then_inc(sv.h, 1)
                    g_T[tb] = ("sv", sv.inc())
            tm_done = sv.n

            @p.on("tensor")
            def _(t, ch=ch, td=tm_done, gKV=g_B23[0], gZ=g_B23[1], cr=cst["ready"]):
                t.wait_ge(sv.h, td)
                if ch == 0:
                    wgd(t, gKV)
                    wgd(t, gZ)
                ins = None
                for j in range(CT):
                    bkv = 2 if j < 4 else 3
                    c0 = (j % 4) * 128
                    for tt in range(4):
                        ins = t.matmul(p.banks[bkv][:, c0:c0 + 128],
                                       ktm[:, tt, j * 128:(j + 1) * 128],
                                       vtm[:, tt, j * 128:(j + 1) * 128],
                                       start=(ch == 0 and tt == 0 and j % 4 == 0),
                                       stop=(ch == NCH - 1 and tt == 3 and j % 4 == 3))
                ins.then_inc(st.h, 1)
            g_tm = kv_mm = st.inc()

        @p.on("vector")
        def _(v, kd=kv_mm, entry=sv.n):
            v.wait_ge(st.h, kd)
            v.wait_ge(sv.h, entry)
            ins = None
            for h in range(NH):
                j = h // 2
                bkv = 2 if j < 4 else 3
                c0 = (j % 4) * 128 + (h % 2) * 64
                pb_ = (h % 2) * 64
                ins = v.tensor_copy(out=kvbf[pb_:pb_ + 64, j, :],
                                    in_=p.banks[bkv][pb_:pb_ + 64, c0:c0 + 64])
            ins.then_inc(sv.h, 1)
            v.wait_ge(sv.h, entry + 1)
            v.memset(zbf[:], 0.0).then_inc(sv.h, 1)
            v.wait_ge(sv.h, entry + 2)
            ins = None
            for h in range(NH):
                pb_ = (h % 2) * 64
                j = h // 2
                ins = v.tensor_copy(out=zbf[pb_:pb_ + 64, j, h:h + 1], in_=z_sb[pb_:pb_ + 64, j:j + 1])
            ins.then_inc(sv.h, 1)
        kvz_ev = sv.inc(3)
        g_B23 = [("sv", kvz_ev), ("sv", kvz_ev)]

        for ch in range(NCH):
            t0 = ch * CW

            @p.on("sync")
            def _(e, sqi=sqi, t0=t0, gx=g_xslot, pre=s_x.n, kv=kvz_ev):
                nm, val = gx
                if val:
                    e.wait_ge(semmap[nm].h, val)
                e.wait_ge(sv.h, kv)
                dmaw(e, s_x, pre, out=xf[:], in_=xr[:, :, sqi, t0:t0 + CW])
            xld = s_x.inc(16)

            @p.on("tensor")
            def _(t, t0=t0, xld=xld, kv=kvz_ev, gD=g_DEN):
                t.wait_ge(sv.h, kv)
                wgd(t, gD)
                ins = None
                for dt in range(CT):
                    ins = t.matmul(p.banks[4][0:16, 0:CW], zbf[:, dt, :],
                                   qfT[:, dt, t0:t0 + CW], start=(dt == 0), stop=(dt == CT - 1))
                ins.then_inc(st.h, 1)
            den_mm = st.inc()

            @p.on("vector")
            def _(v, dm=den_mm, entry=sv.n, rg=g_rden):
                v.wait_ge(st.h, dm)
                v.wait_ge(sv.h, entry)
                if rg:
                    v.wait_ge(st.h, rg)
                v.tensor_scalar(out=rdf[:], in0=p.banks[4][0:16, 0:CW], scalar1=1e-6,
                                scalar2=None, op0=ALU.max).then_inc(sv.h, 1)
                v.wait_ge(sv.h, entry + 1)
                v.reciprocal(out=rdf[:], in_=rdf[:]).then_inc(sv.h, 1)
                v.wait_ge(sv.h, entry + 2)
                v.tensor_copy(out=rden[:], in_=rdf[:]).then_inc(sv.h, 1)
            rden_done = sv.inc(3)
            g_DEN = ("sv", rden_done)

            for j in range(CT):
                bkY = 2 + (j % 2)

                @p.on("tensor")
                def _(t, j=j, t0=t0, bkY=bkY, rd=rden_done, gY=g_B23[j % 2], gBC=g_BC):
                    t.wait_ge(sv.h, rd)
                    wgd(t, gY)
                    wgd(t, gBC)
                    for hh in range(2):
                        h = 2 * j + hh
                        t.matmul(p.banks[bkY][hh * 64:hh * 64 + 64, 0:CW],
                                 kvbf[hh * 64:hh * 64 + 64, j, :],
                                 qfT[hh * 64:hh * 64 + 64, j, t0:t0 + CW],
                                 start=True, stop=True)
                    t.matmul(p.banks[5][:, 0:CW], e16[:, j, :, :].rearrange("p a b -> p (a b)"),
                             rden[:], start=True, stop=True).then_inc(st.h, 1)
                y_mm = st.inc()

                @p.on("vector")
                def _(v, j=j, bkY=bkY, ym=y_mm, entry=sv.n, fg=(g_ffb if j == 0 else 0)):
                    v.wait_ge(st.h, ym)
                    v.wait_ge(sv.h, entry)
                    if fg:
                        v.wait_ge(st.h, fg)
                    v.tensor_copy(out=rbc[:], in_=p.banks[5][:, 0:CW]).then_inc(sv.h, 1)
                    v.wait_ge(sv.h, entry + 1)
                    v.tensor_tensor(out=yT[:, j, :], in0=p.banks[bkY][:, 0:CW], in1=rbc[:],
                                    op=ALU.mult).then_inc(sv.h, 1)
                done = sv.inc(2)
                g_B23[j % 2] = ("sv", done)
                g_BC = ("sv", done)
            y_all = sv.n
            g_rden = st.n

            ao_done_v = 0
            for mt in range(CT):
                bk = mt % 2

                @p.on("tensor")
                def _(t, mt=mt, bk=bk, ya=y_all, wld=ow_ld, gb=g_AB[bk]):
                    t.wait_ge(sv.h, ya)
                    t.wait_ge(s_w2.h, wld)
                    wgd(t, gb)
                    ins = None
                    for kt in range(CT):
                        ins = t.matmul(p.banks[bk][:, 0:CW], outw_sb[:, kt, mt * 128:(mt + 1) * 128],
                                       yT[:, kt, :], start=(kt == 0), stop=(kt == CT - 1))
                    ins.then_inc(st.h, 1)
                ao_mm = st.inc()

                @p.on("vector")
                def _(v, mt=mt, bk=bk, mmd=ao_mm, entry=sv.n, xl=xld):
                    v.wait_ge(st.h, mmd)
                    v.wait_ge(sv.h, entry)
                    v.wait_ge(s_x.h, xl)
                    v.tensor_scalar(out=xA[:, mt, :], in0=p.banks[bk][:, 0:CW],
                                    scalar1=pcol(OFF_OUTB, mt), scalar2=None,
                                    op0=ALU.add).then_inc(sv.h, 1)
                    v.wait_ge(sv.h, entry + 1)
                    v.tensor_tensor(out=xA[:, mt, :], in0=xA[:, mt, :], in1=xf[:, mt, :],
                                    op=ALU.add).then_inc(sv.h, 1)
                ao_done_v = sv.inc(2)
                g_AB[bk] = ("sv", ao_done_v)
            g_xslot = ("sv", ao_done_v)

            @p.on("scalar")
            def _(s, ad=ao_done_v):
                s.wait_ge(sv.h, ad)
                s.memzero(p.dums[:]).then_inc(ss.h, 1)
            xA_bar_s = ss.inc()

            r2 = ln_chunk(
                p, cst, CW,
                xf=lambda ct: xA[:, ct, :],
                xb=None,
                sq=lambda ct: sq[:, ct, :],
                tmp=lambda ct: tmpf[:, ct, :],
                rm=rm[:], rr=rr[:], sems=sems,
                bank_a=p.banks[4], bank_b=p.banks[5],
                out_bf=lambda ct: h2b[:, ct, :],
                gcol=lambda ct: pcol(OFF_G2, ct), bcol=lambda ct: pcol(OFF_BE2, ct),
                bank_guard=max(g_DEN[1], g_BC[1]), sq_guard=g_sq,
                x_dep=(ss.h, xA_bar_s),
                xb_scr=lambda ct: xbfc[:, ct, :],
                out_guard=[(st.h, g_h2b), (so.h, g_store)],
            )
            g_DEN = ("sv", r2["var"]); g_BC = ("sv", r2["var"])
            g_sq = r2["stats"]
            h2_done_s = r2["s"]

            for mt in range(32):
                bk = mt % 2
                sl = mt % 2

                @p.on("gpsimd")
                def _(e, mt=mt, sl=sl, gw=(g_w if mt < 2 else ("st", g_w1s[sl])), pre=s_w.n):
                    nm, val = gw
                    if val:
                        e.wait_ge(semmap[nm].h, val)
                    dmaw(e, s_w, pre, out=w1v[sl][:],
                         in_=w1.rearrange("(kt p) o -> p kt o", p=128)[:, :, mt * 128:(mt + 1) * 128])
                w1ld = s_w.inc(16)

                @p.on("tensor")
                def _(t, mt=mt, bk=bk, sl=sl, hd=h2_done_s, wld=w1ld, gb=g_AB[bk]):
                    t.wait_ge(ss.h, hd)
                    t.wait_ge(s_w.h, wld)
                    wgd(t, gb)
                    ins = None
                    for kt in range(CT):
                        ins = t.matmul(p.banks[bk][:, 0:CW], w1v[sl][:, kt, :],
                                       h2b[:, kt, :], start=(kt == 0), stop=(kt == CT - 1))
                    ins.then_inc(st.h, 1)
                f1_mm = st.inc()
                g_w1s[sl] = f1_mm
                g_w = ("st", f1_mm)

                @p.on("scalar")
                def _(s, mt=mt, bk=bk, mmd=f1_mm, entry=ss.n, fg=(g_ffb if mt == 0 else 0)):
                    s.wait_ge(st.h, mmd)
                    s.wait_ge(ss.h, entry)
                    if fg:
                        s.wait_ge(st.h, fg)
                    s.activation(out=ffb[:, mt, :], in_=p.banks[bk][:, 0:CW], func=AF.Gelu,
                                 bias=pcol(OFF_B1, mt)).then_inc(ss.h, 1)
                g_AB[bk] = ("ss", ss.inc())
            ff1_done = ss.n
            g_h2b = st.n

            for mt in range(CT):
                bk = mt % 2

                @p.on("gpsimd")
                def _(e, mt=mt, gw=g_w2, pre=s_w2.n, need=st.n):
                    nm, val = gw
                    if val:
                        e.wait_ge(semmap[nm].h, val)
                    dmaw(e, s_w2, pre, out=wse[:, :, :],
                         in_=w2.rearrange("(kt p) o -> p kt o", p=128)[:, :, mt * 128:(mt + 1) * 128])
                w2ld = s_w2.inc(16)

                @p.on("tensor")
                def _(t, mt=mt, bk=bk, fd=ff1_done, wld=w2ld, gb=g_AB[bk]):
                    t.wait_ge(ss.h, fd)
                    t.wait_ge(s_w2.h, wld)
                    wgd(t, gb)
                    ins = None
                    for kt in range(32):
                        ins = t.matmul(p.banks[bk][:, 0:CW], wse[:, kt, :],
                                       ffb[:, kt, :], start=(kt == 0), stop=(kt == 31))
                    ins.then_inc(st.h, 1)
                f2_mm = st.inc()
                g_w2 = ("st", f2_mm)

                @p.on("vector")
                def _(v, mt=mt, bk=bk, mmd=f2_mm, entry=sv.n, sg=g_store):
                    v.wait_ge(st.h, mmd)
                    v.wait_ge(sv.h, entry)
                    if mt == 0 and sg:
                        v.wait_ge(so.h, sg)
                    v.tensor_scalar(out=sff[:, mt, :], in0=p.banks[bk][:, 0:CW],
                                    scalar1=pcol(OFF_B2, mt), scalar2=None,
                                    op0=ALU.add).then_inc(sv.h, 1)
                    v.wait_ge(sv.h, entry + 1)
                    v.tensor_tensor(out=sff[:, mt, :], in0=sff[:, mt, :], in1=xA[:, mt, :],
                                    op=ALU.add).then_inc(sv.h, 1)
                g_AB[bk] = ("sv", sv.inc(2))
            s_done = sv.n

            @p.on("sync")
            def _(e, sqi=sqi, t0=t0, sd_=s_done, pre=so.n):
                e.wait_ge(sv.h, sd_)
                dmaw(e, so, pre, out=sr[:, :, sqi, t0:t0 + CW], in_=sff[:])
            g_store = so.inc(16)
            g_ffb = st.n
            g_w = ("st", st.n)
        g_qf = st.n

    @p.on("sync")
    def _(e, need=so.n):
        e.wait_ge(so.h, need)


def build_s4(nseq=S, ncph=None):
    p = Prog(); p.mkdum(); p.init_psum(6, 2)
    xnT = p.dram("xnT", [C, S, T], F32, kind="ExternalInput")
    qkvw = p.dram("qkvw", [C, 3072], F32, kind="ExternalInput")
    outw = p.dram("outw", [C, C], F32, kind="ExternalInput")
    w1 = p.dram("w1", [C, 4096], F32, kind="ExternalInput")
    w2 = p.dram("w2", [4096, C], F32, kind="ExternalInput")
    qkvb = p.dram("qkvb", [128, 24], F32, kind="ExternalInput")
    outb = p.dram("outb", [128, CT], F32, kind="ExternalInput")
    b1c = p.dram("b1c", [128, 32], F32, kind="ExternalInput")
    b2c = p.dram("b2c", [128, CT], F32, kind="ExternalInput")
    g1c = p.dram("g1c", [128, CT], F32, kind="ExternalInput")
    be1c = p.dram("be1c", [128, CT], F32, kind="ExternalInput")
    g2c = p.dram("g2c", [128, CT], F32, kind="ExternalInput")
    be2c = p.dram("be2c", [128, CT], F32, kind="ExternalInput")
    sT = p.dram("sT", [C, S, T], F32, kind="ExternalOutput")
    cst = emit_consts(p)
    with p.stage():
        emit_s4(p, cst, xnT, qkvw, outw, w1, w2, qkvb, outb, b1c, b2c, g1c, be1c, g2c, be2c,
                sT, nseq, ncph)
    return p.finish()


def emit_s1(p, cst, xnT, wk, wv, wr, wo, fwk, fwr, fwv, pvec, sT, out1T, s1T, cmT, nseq=S):
    xr = xnT.rearrange("(ct p) s t -> p ct s t", p=128)
    o1r = out1T.rearrange("(ct p) s t -> p ct s t", p=128)
    s1r = s1T.rearrange("(ct p) s t -> p ct s t", p=128)
    cmr = cmT.rearrange("(ct p) s t -> p ct s t", p=128)
    srr = sT.rearrange("(ct p) s t -> p ct s t", p=128)

    CW = 256
    NCH = T // CW
    TC = 256
    NLC = T // TC

    P_LN1S, P_LN1B, P_LN2S, P_LN2B, P_TMK, P_TMV, P_TMR, P_FTMK, P_FTMR, P_WDEC, P_UF = range(11)

    wbig = p.sbuf("wbig", [128, CT, 4096], BF16)
    fwrs = p.sbuf("fwrs", [128, CT, C], BF16)
    fwvs = p.sbuf("fwvs", [128, 32, 128], BF16)
    parms = p.sbuf("parms", [128, 11 * CT], F32)
    acol = p.sbuf("acol", [128, 2 * CT], F32)
    xnf = p.sbuf("xnf", [128, CT, CW], F32)
    xnb = p.sbuf("xnb", [128, CT, CW], BF16)
    sq = p.sbuf("sq", [128, CT, CW], BF16)
    h1f = p.sbuf("h1f", [128, CT, CW + 1], F32)
    scr_bf = p.sbuf("scr_bf", [128, 32, CW], BF16)
    shared = p.sbuf("shared", [128, CT, 5, CW + 1], F32)
    x2b = p.sbuf("x2b", [128, 2 * CT, CW], BF16)
    rm = p.sbuf("rm", [128, CW], F32)
    rr = p.sbuf("rr", [128, CW], F32)
    mst = p.sbuf("mst", [128, 32], F32)
    carr = p.sbuf("carr", [128, CT, 3], F32)

    ekf = lambda ct: shared[:, ct, 0, 0:CW]
    vvf = lambda ct: shared[:, ct, 1, 0:CW]
    bbf = lambda ct: shared[:, ct, 2, 0:CW]
    lifbuf = shared[:, :, 0:S, 0:TC]
    spk = scr_bf.rearrange("p (ct s) t -> p ct s t", ct=CT)
    gatef = lambda ct: shared[:, ct, 0, 0:CW]
    xAf = lambda ct: shared[:, ct, 1, 0:CW]

    s_x = p.sem("s_x")
    s_xb = p.sem("s_xb")
    s_w = p.sem("s_w")
    s_w2 = p.sem("s_w2")
    s_pp = p.sem("s_pp")
    so = p.sem("so")
    sv = p.sem("v")
    ss = p.sem("s")
    st = p.sem("t")
    sems = dict(v=sv, s=ss, t=st)
    semmap = dict(sv=sv, ss=ss, st=st)

    def wgd(t, g):
        nm, val = g
        if val:
            t.wait_ge(semmap[nm].h, val)

    pcol = lambda which, ct: parms[:, which * CT + ct:which * CT + ct + 1]
    a_col = lambda ct: acol[:, ct:ct + 1]
    eu_col = lambda ct: acol[:, CT + ct:CT + ct + 1]

    @p.on("sync")
    def _(e):
        dmaw(e, s_pp, 0, out=parms[:], in_=pvec)
    pld = s_pp.inc(16)

    @p.on("scalar")
    def _(s, pl=pld):
        s.wait_ge(s_pp.h, pl)
        s.activation(out=acol[:, 0:CT], in_=parms[:, P_WDEC * CT:(P_WDEC + 1) * CT], func=AF.Exp).then_inc(ss.h, 1)
        s.wait_ge(ss.h, 1)
        s.activation(out=acol[:, 0:CT], in_=acol[:, 0:CT], func=AF.Exp, scale=-1.0).then_inc(ss.h, 1)
        s.wait_ge(ss.h, 2)
        s.activation(out=acol[:, CT:2 * CT], in_=parms[:, P_UF * CT:(P_UF + 1) * CT], func=AF.Exp).then_inc(ss.h, 1)
    acol_done = ss.inc(3)

    @p.on("gpsimd")
    def _(e):
        dmaw(e, s_w, 0, out=wbig[:, :, 0:1024], in_=wk.rearrange("(kt p) o -> p kt o", p=128))
        dmaw(e, s_w, 16, out=wbig[:, :, 1024:2048], in_=wv.rearrange("(kt p) o -> p kt o", p=128))
        dmaw(e, s_w, 32, out=wbig[:, :, 2048:3072], in_=wr.rearrange("(kt p) o -> p kt o", p=128))
        dmaw(e, s_w, 48, out=wbig[:, :, 3072:4096], in_=wo.rearrange("(kt p) o -> p kt o", p=128))
    w1_ld = s_w.inc(64)

    g_AB = [("ss", 0), ("ss", 0), ("ss", 0), ("ss", 0)]
    g_DEN = ("sv", 0)
    g_BC = ("sv", 0)
    g_sq = 0
    g_xslot = ("sv", 0)
    g_store_o1 = 0
    g_scr = 0
    g_shared_t = 0
    WOFF = dict(k=0, v=1024, r=2048, o=3072)

    for sqi in range(nseq):
        for ch in range(NCH):
            t0 = ch * CW
            first = (ch == 0)

            @p.on("sync")
            def _(e, sqi=sqi, t0=t0, gx=g_xslot, pre=s_x.n, sg_=g_store_o1):
                nm, val = gx
                if val:
                    e.wait_ge(semmap[nm].h, val)
                if sg_:
                    e.wait_ge(so.h, sg_)
                dmaw(e, s_x, pre, out=xnf[:], in_=xr[:, :, sqi, t0:t0 + CW])
            xld = s_x.inc(16)

            @p.on("gpsimd")
            def _(e, sqi=sqi, t0=t0, need=g_sq, pre=s_xb.n):
                if need:
                    e.wait_ge(st.h, need)
                dmaw(e, s_xb, pre, out=xnb[:], in_=xr[:, :, sqi, t0:t0 + CW])
            xbld = s_xb.inc(16)

            @p.on("vector")
            def _(v, nx=xld, nb=xbld, ad=acol_done, first=first, entry=sv.n):
                v.wait_ge(s_x.h, nx)
                v.wait_ge(s_xb.h, nb)
                v.wait_ge(ss.h, ad)
                v.wait_ge(sv.h, entry)
                if first:
                    v.memset(carr[:], 0.0).then_inc(sv.h, 1)
                else:
                    v.tensor_copy(out=h1f[:, :, 0:1], in_=carr[:, :, 0:1]).then_inc(sv.h, 1)
            bar_v = sv.inc()

            @p.on("scalar")
            def _(s, nx=xld, nb=xbld):
                s.wait_ge(s_x.h, nx)
                s.wait_ge(s_xb.h, nb)
                s.memzero(p.dums[:]).then_inc(ss.h, 1)
            bar_s = ss.inc()

            if first:
                @p.on("vector")
                def _(v, e_=bar_v):
                    v.wait_ge(sv.h, e_)
                    v.memset(h1f[:, :, 0:1], 0.0).then_inc(sv.h, 1)
                sv.inc()

            r = ln_chunk(
                p, cst, CW,
                xf=lambda ct: xnf[:, ct, :],
                xb=lambda ct: xnb[:, ct, :],
                sq=lambda ct: sq[:, ct, :],
                tmp=lambda ct: h1f[:, ct, 1:CW + 1],
                rm=rm[:], rr=rr[:], sems=sems,
                bank_a=p.banks[4], bank_b=p.banks[5],
                out_f32=lambda ct: h1f[:, ct, 1:CW + 1],
                gcol=lambda ct: pcol(P_LN1S, ct), bcol=lambda ct: pcol(P_LN1B, ct),
                bank_guard=max(g_DEN[1], g_BC[1]), sq_guard=g_sq,
                x_dep=(ss.h, bar_s),
                out_guard=[(st.h, g_shared_t)],
            )
            g_DEN = ("sv", r["var"]); g_BC = ("sv", r["var"])
            g_sq = r["stats"]
            h1_done = r["s"]

            @p.on("vector")
            def _(v, hd=h1_done, entry=sv.n, gs=g_scr):
                v.wait_ge(ss.h, hd)
                v.wait_ge(sv.h, entry)
                if gs:
                    v.wait_ge(st.h, gs)
                v.tensor_copy(out=carr[:, :, 0:1], in_=h1f[:, :, CW:CW + 1]).then_inc(sv.h, 1)
                v.wait_ge(sv.h, entry + 1)
                ins = None
                for ct in range(CT):
                    ins = v.tensor_tensor(out=xnf[:, ct, :], in0=h1f[:, ct, 1:CW + 1],
                                          in1=h1f[:, ct, 0:CW], op=ALU.subtract)
                ins.then_inc(sv.h, 1)
                v.wait_ge(sv.h, entry + 2)
                ins = None
                for ct in range(CT):
                    ins = v.scalar_tensor_tensor(out=scr_bf[:, ct, :], in0=xnf[:, ct, :],
                                                 scalar=pcol(P_TMK, ct), in1=h1f[:, ct, 0:CW],
                                                 op0=ALU.mult, op1=ALU.add)
                ins.then_inc(sv.h, 1)
                v.wait_ge(sv.h, entry + 3)
                ins = None
                for ct in range(CT):
                    ins = v.scalar_tensor_tensor(out=scr_bf[:, 8 + ct, :], in0=xnf[:, ct, :],
                                                 scalar=pcol(P_TMV, ct), in1=h1f[:, ct, 0:CW],
                                                 op0=ALU.mult, op1=ALU.add)
                ins.then_inc(sv.h, 1)
                v.wait_ge(sv.h, entry + 4)
                ins = None
                for ct in range(CT):
                    ins = v.scalar_tensor_tensor(out=scr_bf[:, 16 + ct, :], in0=xnf[:, ct, :],
                                                 scalar=pcol(P_TMR, ct), in1=h1f[:, ct, 0:CW],
                                                 op0=ALU.mult, op1=ALU.add)
                ins.then_inc(sv.h, 1)
            mix_done = sv.inc(5)

            for wname, which, col0 in (("k", 0, WOFF["k"]), ("v", 1, WOFF["v"])):
                for mt in range(CT):
                    bk = (which * 8 + mt) % 4

                    @p.on("tensor")
                    def _(t, which=which, col0=col0, mt=mt, bk=bk, md=mix_done, wl=w1_ld, gb=g_AB[bk]):
                        t.wait_ge(sv.h, md)
                        t.wait_ge(s_w.h, wl)
                        wgd(t, gb)
                        ins = None
                        for kt in range(CT):
                            ins = t.matmul(p.banks[bk][:, 0:CW],
                                           wbig[:, kt, col0 + mt * 128:col0 + (mt + 1) * 128],
                                           scr_bf[:, which * 8 + kt, :],
                                           start=(kt == 0), stop=(kt == CT - 1))
                        ins.then_inc(st.h, 1)
                    mmd = st.inc()

                    @p.on("scalar")
                    def _(s, which=which, mt=mt, bk=bk, mmd=mmd, entry=ss.n):
                        s.wait_ge(st.h, mmd)
                        s.wait_ge(ss.h, entry)
                        if which == 0:
                            s.activation(out=ekf(mt), in_=p.banks[bk][:, 0:CW], func=AF.Exp).then_inc(ss.h, 1)
                        else:
                            s.activation(out=vvf(mt), in_=p.banks[bk][:, 0:CW], func=AF.Identity).then_inc(ss.h, 1)
                    g_AB[bk] = ("ss", ss.inc())
            kv_ev = ss.n

            @p.on("vector")
            def _(v, kv=kv_ev, entry=sv.n, first=first):
                v.wait_ge(ss.h, kv)
                v.wait_ge(sv.h, entry)
                if first:
                    v.memset(shared[:, :, 3:5, 0:1], 0.0)
                else:
                    v.tensor_copy(out=shared[:, :, 3, 0:1], in_=carr[:, :, 1:2])
                    v.tensor_copy(out=shared[:, :, 4, 0:1], in_=carr[:, :, 2:3])
                ins = None
                for ct in range(CT):
                    ins = v.tensor_tensor(out=bbf(ct), in0=ekf(ct), in1=vvf(ct), op=ALU.mult)
                ins.then_inc(sv.h, 1)
                v.wait_ge(sv.h, entry + 1)
                for ct in range(CT):
                    v.tensor_tensor_scan(out=shared[:, ct, 3, 1:CW + 1],
                                         data0=a_col(ct).broadcast_to([128, CW]),
                                         data1=bbf(ct),
                                         initial=(0.0 if first else carr[:, ct, 1:2]),
                                         op0=ALU.mult, op1=ALU.add)
                ins = None
                for ct in range(CT):
                    ins = v.tensor_tensor_scan(out=shared[:, ct, 4, 1:CW + 1],
                                               data0=a_col(ct).broadcast_to([128, CW]),
                                               data1=ekf(ct),
                                               initial=(0.0 if first else carr[:, ct, 2:3]),
                                               op0=ALU.mult, op1=ALU.add)
                ins.then_inc(sv.h, 1)
                v.wait_ge(sv.h, entry + 2)
                v.tensor_copy(out=carr[:, :, 1:2], in_=shared[:, :, 3, CW:CW + 1])
                ins = v.tensor_copy(out=carr[:, :, 2:3], in_=shared[:, :, 4, CW:CW + 1])
                ins.then_inc(sv.h, 1)
                v.wait_ge(sv.h, entry + 3)
                ins = None
                for ct in range(CT):
                    ins = v.tensor_scalar(out=ekf(ct), in0=ekf(ct), scalar1=eu_col(ct),
                                          scalar2=None, op0=ALU.mult)
                ins.then_inc(sv.h, 1)
                v.wait_ge(sv.h, entry + 4)
                ins = None
                for ct in range(CT):
                    ins = v.tensor_tensor(out=vvf(ct), in0=ekf(ct), in1=vvf(ct), op=ALU.mult)
                ins.then_inc(sv.h, 1)
                v.wait_ge(sv.h, entry + 5)
                ins = None
                for ct in range(CT):
                    ins = v.tensor_tensor(out=vvf(ct), in0=vvf(ct), in1=shared[:, ct, 3, 0:CW], op=ALU.add)
                ins.then_inc(sv.h, 1)
                v.wait_ge(sv.h, entry + 6)
                ins = None
                for ct in range(CT):
                    ins = v.tensor_tensor(out=ekf(ct), in0=ekf(ct), in1=shared[:, ct, 4, 0:CW], op=ALU.add)
                ins.then_inc(sv.h, 1)
                v.wait_ge(sv.h, entry + 7)
                ins = None
                for ct in range(CT):
                    ins = v.reciprocal(out=ekf(ct), in_=ekf(ct))
                ins.then_inc(sv.h, 1)
                v.wait_ge(sv.h, entry + 8)
                ins = None
                for ct in range(CT):
                    ins = v.tensor_tensor(out=vvf(ct), in0=vvf(ct), in1=ekf(ct), op=ALU.mult)
                ins.then_inc(sv.h, 1)
            wkv_done = sv.inc(9)

            for mt in range(CT):
                bk = mt % 4

                @p.on("tensor")
                def _(t, mt=mt, bk=bk, wl=w1_ld, gb=g_AB[bk], wd=wkv_done):
                    t.wait_ge(sv.h, wd)
                    t.wait_ge(s_w.h, wl)
                    wgd(t, gb)
                    ins = None
                    for kt in range(CT):
                        ins = t.matmul(p.banks[bk][:, 0:CW],
                                       wbig[:, kt, WOFF["r"] + mt * 128:WOFF["r"] + (mt + 1) * 128],
                                       scr_bf[:, 16 + kt, :],
                                       start=(kt == 0), stop=(kt == CT - 1))
                    ins.then_inc(st.h, 1)
                mmd = st.inc()

                @p.on("scalar")
                def _(s, mt=mt, bk=bk, mmd=mmd, entry=ss.n):
                    s.wait_ge(st.h, mmd)
                    s.wait_ge(ss.h, entry)
                    s.activation(out=bbf(mt), in_=p.banks[bk][:, 0:CW], func=AF.Sigmoid).then_inc(ss.h, 1)
                g_AB[bk] = ("ss", ss.inc())
            r_ev = ss.n

            @p.on("vector")
            def _(v, re_=r_ev, entry=sv.n, gs=g_scr):
                v.wait_ge(ss.h, re_)
                v.wait_ge(sv.h, entry)
                ins = None
                for ct in range(CT):
                    ins = v.tensor_tensor(out=scr_bf[:, 24 + ct, :], in0=vvf(ct), in1=bbf(ct), op=ALU.mult)
                ins.then_inc(sv.h, 1)
            ry_done = sv.inc()

            o1_ev = 0
            for mt in range(CT):
                bk = mt % 4

                @p.on("tensor")
                def _(t, mt=mt, bk=bk, rd=ry_done, wl=w1_ld, gb=g_AB[bk]):
                    t.wait_ge(sv.h, rd)
                    t.wait_ge(s_w.h, wl)
                    wgd(t, gb)
                    ins = None
                    for kt in range(CT):
                        ins = t.matmul(p.banks[bk][:, 0:CW],
                                       wbig[:, kt, WOFF["o"] + mt * 128:WOFF["o"] + (mt + 1) * 128],
                                       scr_bf[:, 24 + kt, :],
                                       start=(kt == 0), stop=(kt == CT - 1))
                    ins.then_inc(st.h, 1)
                mmd = st.inc()

                @p.on("vector")
                def _(v, mt=mt, bk=bk, mmd=mmd, entry=sv.n, og=g_store_o1):
                    v.wait_ge(st.h, mmd)
                    v.wait_ge(sv.h, entry)
                    if mt == 0 and og:
                        v.wait_ge(so.h, og)
                    v.tensor_copy(out=xnf[:, mt, :], in_=p.banks[bk][:, 0:CW]).then_inc(sv.h, 1)
                o1_ev = sv.inc()
                g_AB[bk] = ("sv", o1_ev)

            @p.on("sync")
            def _(e, sqi=sqi, t0=t0, od=o1_ev, pre=so.n):
                e.wait_ge(sv.h, od)
                dmaw(e, so, pre, out=o1r[:, :, sqi, t0:t0 + CW], in_=xnf[:])
            g_store_o1 = so.inc(16)
            g_xslot = ("sv", o1_ev)
            g_scr = st.n
            g_shared_t = st.n

    def lif_phase(src_r, dst_r, tag, src_stored):
        lsem = p.sem(f"lb_{tag}")
        g_lb = 0
        g_spk = 0

        @p.on("vector")
        def _(v, entry=sv.n):
            v.wait_ge(sv.h, entry)
            v.memset(mst[:], 0.0).then_inc(sv.h, 1)
        mz_done = sv.inc()

        for lc in range(NLC):
            t0 = lc * TC

            @p.on("sync")
            def _(e, t0=t0, need=g_lb, pre=lsem.n, mz=mz_done, sst=src_stored):
                if need:
                    e.wait_ge(sv.h, need)
                e.wait_ge(sv.h, mz)
                e.wait_ge(so.h, sst)
                dmaw4(e, lsem, pre, lifbuf[:, :, 0:nseq, :], src_r[:, :, 0:nseq, t0:t0 + TC], nseq)
            ld = lsem.inc(16 * nseq)

            @p.on("vector")
            def _(v, ld=ld, entry=sv.n, gs=g_spk, nseq=nseq):
                v.wait_ge(lsem.h, ld)
                v.wait_ge(sv.h, entry)
                ins = v.tensor_scalar(out=lifbuf[:, :, 0:nseq, :], in0=lifbuf[:, :, 0:nseq, :],
                                      scalar1=0.5, scalar2=None, op0=ALU.mult)
                ins.then_inc(sv.h, 1)
                c = entry + 1
                for tt in range(TC):
                    v.wait_ge(sv.h, c)
                    v.scalar_tensor_tensor(out=lifbuf[:, :, 0:nseq, tt],
                                           in0=mst[:].rearrange("p (ct s) -> p ct s", ct=CT)[:, :, 0:nseq],
                                           scalar=0.5, in1=lifbuf[:, :, 0:nseq, tt],
                                           op0=ALU.mult, op1=ALU.add).then_inc(sv.h, 1)
                    c += 1
                    v.wait_ge(sv.h, c)
                    v.scalar_tensor_tensor(out=mst[:].rearrange("p (ct s) -> p ct s", ct=CT)[:, :, 0:nseq],
                                           in0=lifbuf[:, :, 0:nseq, tt], scalar=1.0,
                                           in1=lifbuf[:, :, 0:nseq, tt],
                                           op0=ALU.is_lt, op1=ALU.mult).then_inc(sv.h, 1)
                    c += 1
                v.wait_ge(sv.h, c)
                if gs:
                    v.wait_ge(so.h, gs)
                v.tensor_scalar(out=spk[:, :, 0:nseq, :], in0=lifbuf[:, :, 0:nseq, :],
                                scalar1=1.0, scalar2=None, op0=ALU.is_ge).then_inc(sv.h, 1)
            done = sv.inc(2 + 2 * TC)
            g_lb = done

            @p.on("sync")
            def _(e, t0=t0, dn=done, pre=so.n):
                e.wait_ge(sv.h, dn)
                dmaw4(e, so, pre, dst_r[:, :, 0:nseq, t0:t0 + TC], spk[:, :, 0:nseq, :], nseq)
            g_spk = so.inc(16 * nseq)
        return g_spk

    s1_stored = lif_phase(o1r, s1r, "l1", g_store_o1)

    @p.on("gpsimd")
    def _(e, gw=st.n, pre=s_w.n):
        e.wait_ge(st.h, gw)
        dmaw(e, s_w, pre, out=wbig[:], in_=fwk.rearrange("(kt p) o -> p kt o", p=128))
        dmaw(e, s_w, pre + 16, out=fwrs[:], in_=fwr.rearrange("(kt p) o -> p kt o", p=128))
    w3_ld = s_w.inc(32)

    g_x2b = 0
    g_kk = 0
    g_store_cm = 0
    for sqi in range(nseq):
        for ch in range(NCH):
            t0 = ch * CW
            first = (ch == 0)

            @p.on("sync")
            def _(e, sqi=sqi, t0=t0, gx=g_xslot, pre=s_x.n, sp=max(s1_stored, g_store_cm)):
                nm, val = gx
                if val:
                    e.wait_ge(semmap[nm].h, val)
                e.wait_ge(so.h, sp)
                dmaw(e, s_x, pre, out=xnf[:], in_=xr[:, :, sqi, t0:t0 + CW])
            xld = s_x.inc(16)

            @p.on("gpsimd")
            def _(e, sqi=sqi, t0=t0, need=g_sq, pre=s_xb.n, sp=s1_stored):
                if need:
                    e.wait_ge(st.h, need)
                e.wait_ge(so.h, sp)
                dmaw(e, s_xb, pre, out=sq[:], in_=s1r[:, :, sqi, t0:t0 + CW])
            sld = s_xb.inc(16)

            @p.on("vector")
            def _(v, nx=xld, ns=sld, entry=sv.n, gsh=g_shared_t, first=first):
                v.wait_ge(s_x.h, nx)
                v.wait_ge(s_xb.h, ns)
                v.wait_ge(sv.h, entry)
                if gsh:
                    v.wait_ge(st.h, gsh)
                ins = None
                for ct in range(CT):
                    ins = v.tensor_tensor(out=xAf(ct), in0=xnf[:, ct, :], in1=sq[:, ct, :], op=ALU.add)
                ins.then_inc(sv.h, 1)
                v.wait_ge(sv.h, entry + 1)
                ins = None
                for ct in range(CT):
                    ins = v.tensor_copy(out=xnb[:, ct, :], in_=xAf(ct))
                ins.then_inc(sv.h, 1)
                v.wait_ge(sv.h, entry + 2)
                if first:
                    v.memset(h1f[:, :, 0:1], 0.0).then_inc(sv.h, 1)
                else:
                    v.tensor_copy(out=h1f[:, :, 0:1], in_=carr[:, :, 0:1]).then_inc(sv.h, 1)
            xa_done = sv.inc(3)

            @p.on("scalar")
            def _(s, xa=xa_done):
                s.wait_ge(sv.h, xa)
                s.memzero(p.dums[:]).then_inc(ss.h, 1)
            bar_s = ss.inc()

            r3 = ln_chunk(
                p, cst, CW,
                xf=lambda ct: xAf(ct),
                xb=lambda ct: xnb[:, ct, :],
                sq=lambda ct: sq[:, ct, :],
                tmp=lambda ct: h1f[:, ct, 1:CW + 1],
                rm=rm[:], rr=rr[:], sems=sems,
                bank_a=p.banks[4], bank_b=p.banks[5],
                out_f32=lambda ct: h1f[:, ct, 1:CW + 1],
                gcol=lambda ct: pcol(P_LN2S, ct), bcol=lambda ct: pcol(P_LN2B, ct),
                bank_guard=max(g_DEN[1], g_BC[1]), sq_guard=g_sq,
                x_dep=(ss.h, bar_s),
            )
            g_DEN = ("sv", r3["var"]); g_BC = ("sv", r3["var"])
            g_sq = r3["stats"]
            h2_done = r3["s"]

            @p.on("vector")
            def _(v, hd=h2_done, entry=sv.n, gx2=g_x2b):
                v.wait_ge(ss.h, hd)
                v.wait_ge(sv.h, entry)
                if gx2:
                    v.wait_ge(st.h, gx2)
                v.tensor_copy(out=carr[:, :, 0:1], in_=h1f[:, :, CW:CW + 1]).then_inc(sv.h, 1)
                v.wait_ge(sv.h, entry + 1)
                ins = None
                for ct in range(CT):
                    ins = v.tensor_tensor(out=xnf[:, ct, :], in0=h1f[:, ct, 1:CW + 1],
                                          in1=h1f[:, ct, 0:CW], op=ALU.subtract)
                ins.then_inc(sv.h, 1)
                v.wait_ge(sv.h, entry + 2)
                ins = None
                for ct in range(CT):
                    ins = v.scalar_tensor_tensor(out=x2b[:, ct, :], in0=xnf[:, ct, :],
                                                 scalar=pcol(P_FTMK, ct), in1=h1f[:, ct, 0:CW],
                                                 op0=ALU.mult, op1=ALU.add)
                ins.then_inc(sv.h, 1)
                v.wait_ge(sv.h, entry + 3)
                ins = None
                for ct in range(CT):
                    ins = v.scalar_tensor_tensor(out=x2b[:, 8 + ct, :], in0=xnf[:, ct, :],
                                                 scalar=pcol(P_FTMR, ct), in1=h1f[:, ct, 0:CW],
                                                 op0=ALU.mult, op1=ALU.add)
                ins.then_inc(sv.h, 1)
            mix2_done = sv.inc(4)

            for mt in range(32):
                bk = mt % 4

                @p.on("tensor")
                def _(t, mt=mt, bk=bk, md=mix2_done, wl=w3_ld, gb=g_AB[bk]):
                    t.wait_ge(sv.h, md)
                    t.wait_ge(s_w.h, wl)
                    wgd(t, gb)
                    ins = None
                    for kt in range(CT):
                        ins = t.matmul(p.banks[bk][:, 0:CW],
                                       wbig[:, kt, mt * 128:(mt + 1) * 128],
                                       x2b[:, kt, :], start=(kt == 0), stop=(kt == CT - 1))
                    ins.then_inc(st.h, 1)
                mmd = st.inc()

                @p.on("scalar")
                def _(s, mt=mt, bk=bk, mmd=mmd, entry=ss.n, gk=(g_kk if mt == 0 else 0)):
                    s.wait_ge(st.h, mmd)
                    s.wait_ge(ss.h, entry)
                    if gk:
                        s.wait_ge(st.h, gk)
                    s.activation(out=sq[:, mt % CT, :], in_=p.banks[bk][:, 0:CW],
                                 func=AF.Relu).then_inc(ss.h, 1)
                    s.wait_ge(ss.h, entry + 1)
                    s.activation(out=scr_bf[:, mt, :], in_=sq[:, mt % CT, :], func=AF.Square).then_inc(ss.h, 1)
                g_AB[bk] = ("ss", ss.inc(2))
            kk_done = ss.n

            for mt in range(CT):
                bk = mt % 4

                @p.on("tensor")
                def _(t, mt=mt, bk=bk, md=mix2_done, wl=w3_ld, gb=g_AB[bk]):
                    t.wait_ge(sv.h, md)
                    t.wait_ge(s_w.h, wl)
                    wgd(t, gb)
                    ins = None
                    for kt in range(CT):
                        ins = t.matmul(p.banks[bk][:, 0:CW],
                                       fwrs[:, kt, mt * 128:(mt + 1) * 128],
                                       x2b[:, 8 + kt, :], start=(kt == 0), stop=(kt == CT - 1))
                    ins.then_inc(st.h, 1)
                mmd = st.inc()

                @p.on("scalar")
                def _(s, mt=mt, bk=bk, mmd=mmd, entry=ss.n):
                    s.wait_ge(st.h, mmd)
                    s.wait_ge(ss.h, entry)
                    s.activation(out=gatef(mt), in_=p.banks[bk][:, 0:CW], func=AF.Sigmoid).then_inc(ss.h, 1)
                g_AB[bk] = ("ss", ss.inc())
            gate_done = ss.n

            cm_ev = 0
            for mt in range(CT):
                bk = mt % 4

                @p.on("gpsimd")
                def _(e, mt=mt, pre=s_w2.n, need=st.n):
                    e.wait_ge(st.h, need)
                    dmaw(e, s_w2, pre, out=fwvs[:],
                         in_=fwv.rearrange("(kt p) o -> p kt o", p=128)[:, :, mt * 128:(mt + 1) * 128])
                wvld = s_w2.inc(16)

                @p.on("tensor")
                def _(t, mt=mt, bk=bk, kd=kk_done, wl=wvld, gb=g_AB[bk]):
                    t.wait_ge(ss.h, kd)
                    t.wait_ge(s_w2.h, wl)
                    wgd(t, gb)
                    ins = None
                    for kt in range(32):
                        ins = t.matmul(p.banks[bk][:, 0:CW], fwvs[:, kt, :],
                                       scr_bf[:, kt, :], start=(kt == 0), stop=(kt == 31))
                    ins.then_inc(st.h, 1)
                mmd = st.inc()

                @p.on("vector")
                def _(v, mt=mt, bk=bk, mmd=mmd, entry=sv.n, gd=gate_done, og=g_store_cm):
                    v.wait_ge(st.h, mmd)
                    v.wait_ge(sv.h, entry)
                    v.wait_ge(ss.h, gd)
                    if mt == 0 and og:
                        v.wait_ge(so.h, og)
                    v.tensor_tensor(out=xnf[:, mt, :], in0=p.banks[bk][:, 0:CW],
                                    in1=gatef(mt), op=ALU.mult).then_inc(sv.h, 1)
                cm_ev = sv.inc()
                g_AB[bk] = ("sv", cm_ev)

            @p.on("sync")
            def _(e, sqi=sqi, t0=t0, cd=cm_ev, pre=so.n):
                e.wait_ge(sv.h, cd)
                dmaw(e, so, pre, out=cmr[:, :, sqi, t0:t0 + CW], in_=xnf[:])
            g_store_cm = so.inc(16)
            g_xslot = ("sv", cm_ev)
            g_x2b = st.n
            g_kk = st.n
            g_shared_t = st.n

    lsem4 = p.sem("lb4")
    g_lb4 = 0
    g_out = 0

    @p.on("vector")
    def _(v, entry=sv.n):
        v.wait_ge(sv.h, entry)
        v.memset(mst[:], 0.0).then_inc(sv.h, 1)
    mz4 = sv.inc()

    for lc in range(NLC):
        t0 = lc * TC

        @p.on("sync")
        def _(e, t0=t0, need=g_lb4, pre=lsem4.n, mz=mz4, cs=max(g_store_cm, g_out)):
            if need:
                e.wait_ge(sv.h, need)
            e.wait_ge(sv.h, mz)
            e.wait_ge(so.h, cs)
            dmaw4(e, lsem4, pre, lifbuf[:, :, 0:nseq, :], cmr[:, :, 0:nseq, t0:t0 + TC], nseq)
        ld = lsem4.inc(16 * nseq)

        @p.on("vector")
        def _(v, ld=ld, entry=sv.n, nseq=nseq):
            v.wait_ge(lsem4.h, ld)
            v.wait_ge(sv.h, entry)
            ins = v.tensor_scalar(out=lifbuf[:, :, 0:nseq, :], in0=lifbuf[:, :, 0:nseq, :],
                                  scalar1=0.5, scalar2=None, op0=ALU.mult)
            ins.then_inc(sv.h, 1)
            c = entry + 1
            for tt in range(TC):
                v.wait_ge(sv.h, c)
                v.scalar_tensor_tensor(out=lifbuf[:, :, 0:nseq, tt],
                                       in0=mst[:].rearrange("p (ct s) -> p ct s", ct=CT)[:, :, 0:nseq],
                                       scalar=0.5, in1=lifbuf[:, :, 0:nseq, tt],
                                       op0=ALU.mult, op1=ALU.add).then_inc(sv.h, 1)
                c += 1
                v.wait_ge(sv.h, c)
                v.scalar_tensor_tensor(out=mst[:].rearrange("p (ct s) -> p ct s", ct=CT)[:, :, 0:nseq],
                                       in0=lifbuf[:, :, 0:nseq, tt], scalar=1.0,
                                       in1=lifbuf[:, :, 0:nseq, tt],
                                       op0=ALU.is_lt, op1=ALU.mult).then_inc(sv.h, 1)
                c += 1
            v.wait_ge(sv.h, c)
            v.tensor_scalar(out=spk[:, :, 0:nseq, :], in0=lifbuf[:, :, 0:nseq, :],
                            scalar1=1.0, scalar2=None, op0=ALU.is_ge).then_inc(sv.h, 1)
        lif_done = sv.inc(2 + 2 * TC)

        @p.on("sync")
        def _(e, t0=t0, dn=lif_done, pre=s_x.n):
            e.wait_ge(sv.h, dn)
            dmaw4(e, s_x, pre, lifbuf[:, :, 0:nseq, :], xr[:, :, 0:nseq, t0:t0 + TC], nseq)
        xld = s_x.inc(16 * nseq)

        @p.on("vector")
        def _(v, ld=xld, entry=sv.n):
            v.wait_ge(s_x.h, ld)
            v.wait_ge(sv.h, entry)
            v.tensor_tensor(out=lifbuf[:, :, 0:nseq, :], in0=lifbuf[:, :, 0:nseq, :],
                            in1=spk[:, :, 0:nseq, :], op=ALU.add).then_inc(sv.h, 1)
        c1_done = sv.inc()

        @p.on("gpsimd")
        def _(e, t0=t0, dn=c1_done, pre=s_xb.n):
            e.wait_ge(sv.h, dn)
            dmaw4(e, s_xb, pre, spk[:, :, 0:nseq, :], s1r[:, :, 0:nseq, t0:t0 + TC], nseq)
        s1ld = s_xb.inc(16 * nseq)

        @p.on("vector")
        def _(v, ld=s1ld, entry=sv.n, og=g_out):
            v.wait_ge(s_xb.h, ld)
            v.wait_ge(sv.h, entry)
            if og:
                v.wait_ge(so.h, og)
            v.tensor_tensor(out=lifbuf[:, :, 0:nseq, :], in0=lifbuf[:, :, 0:nseq, :],
                            in1=spk[:, :, 0:nseq, :], op=ALU.add).then_inc(sv.h, 1)
        s_done = sv.inc()
        g_lb4 = s_done

        @p.on("sync")
        def _(e, t0=t0, sd_=s_done, pre=so.n):
            e.wait_ge(sv.h, sd_)
            dmaw4(e, so, pre, srr[:, :, 0:nseq, t0:t0 + TC], lifbuf[:, :, 0:nseq, :], nseq)
        g_out = so.inc(16 * nseq)

    @p.on("sync")
    def _(e, need=so.n):
        e.wait_ge(so.h, need)


def build_s0(barrier=False, strong=()):
    p = Prog(); p.mkdum(); p.init_psum(6, 2)
    xT = p.dram("xT", [C, S, T], F32, kind="ExternalInput")
    xnT = p.dram("xnT", [C, S, T], F32, kind="ExternalOutput")
    cst = emit_consts(p)
    with p.stage():
        emit_s0(p, cst, xT, xnT, barrier=barrier, strong=strong)
    return p.finish()


def build_s1(nseq=S):
    p = Prog(); p.mkdum(); p.init_psum(6, 2)
    xnT = p.dram("xnT", [C, S, T], F32, kind="ExternalInput")
    wk = p.dram("wk", [C, C], F32, kind="ExternalInput")
    wv = p.dram("wv", [C, C], F32, kind="ExternalInput")
    wr = p.dram("wr", [C, C], F32, kind="ExternalInput")
    wo = p.dram("wo", [C, C], F32, kind="ExternalInput")
    fwk = p.dram("fwk", [C, 4096], F32, kind="ExternalInput")
    fwr = p.dram("fwr", [C, C], F32, kind="ExternalInput")
    fwv = p.dram("fwv", [4096, C], F32, kind="ExternalInput")
    pvec = p.dram("pvec", [128, 11 * CT], F32, kind="ExternalInput")
    sT = p.dram("sT", [C, S, T], F32, kind="ExternalOutput")
    out1T = p.dram("out1T", [C, S, T], F32, kind="Internal")
    s1T = p.dram("s1T", [C, S, T], BF16, kind="Internal")
    cmT = p.dram("cmT", [C, S, T], F32, kind="Internal")
    cst = emit_consts(p)
    with p.stage():
        emit_s1(p, cst, xnT, wk, wv, wr, wo, fwk, fwr, fwv, pvec, sT, out1T, s1T, cmT, nseq)
    return p.finish()



def to_cm(x):
    return np.ascontiguousarray(np.transpose(x, (2, 0, 1)))

def from_cm(xT):
    return np.transpose(xT, (1, 2, 0))


_PROGS = {}

def _prog(name, builder):
    if name not in _PROGS:
        _PROGS[name] = builder()
    return _PROGS[name]


def _col(v, n=CT):
    return np.ascontiguousarray(np.asarray(v).reshape(n, 128).T)


def build_mega():
    p = Prog(); p.mkdum(); p.init_psum(6, 2)
    F = F32
    xT = p.dram("xT", [C, S, T], F, kind="ExternalInput")
    wk = p.dram("wk", [C, C], F, kind="ExternalInput")
    wv = p.dram("wv", [C, C], F, kind="ExternalInput")
    wr = p.dram("wr", [C, C], F, kind="ExternalInput")
    wo = p.dram("wo", [C, C], F, kind="ExternalInput")
    fwk = p.dram("fwk", [C, 4096], F, kind="ExternalInput")
    fwr = p.dram("fwr", [C, C], F, kind="ExternalInput")
    fwv = p.dram("fwv", [4096, C], F, kind="ExternalInput")
    pvec = p.dram("pvec", [128, 11 * CT], F, kind="ExternalInput")
    qw = p.dram("qw", [6, C, C], I32, kind="ExternalInput")
    qsc = p.dram("qsc", [128, 6, CT, 64], F, kind="ExternalInput")
    qb = p.dram("qb", [128, 6, CT], F, kind="ExternalInput")
    la = p.dram("la", [6, 32, C], F, kind="ExternalInput")
    lb = p.dram("lb", [6, C, 32], F, kind="ExternalInput")
    lng = p.dram("lng", [128, 3, CT], F, kind="ExternalInput")
    lnb = p.dram("lnb", [128, 3, CT], F, kind="ExternalInput")
    qkvw = p.dram("qkvw", [C, 3072], F, kind="ExternalInput")
    outw = p.dram("outw", [C, C], F, kind="ExternalInput")
    w1 = p.dram("w1", [C, 4096], F, kind="ExternalInput")
    w2 = p.dram("w2", [4096, C], F, kind="ExternalInput")
    qkvb = p.dram("qkvb", [128, 24], F, kind="ExternalInput")
    outb = p.dram("outb", [128, CT], F, kind="ExternalInput")
    b1c = p.dram("b1c", [128, 32], F, kind="ExternalInput")
    b2c = p.dram("b2c", [128, CT], F, kind="ExternalInput")
    g1c = p.dram("g1c", [128, CT], F, kind="ExternalInput")
    be1c = p.dram("be1c", [128, CT], F, kind="ExternalInput")
    g2c = p.dram("g2c", [128, CT], F, kind="ExternalInput")
    be2c = p.dram("be2c", [128, CT], F, kind="ExternalInput")
    wgt = p.dram("wgt", [9, C, 2048], F, kind="ExternalInput")
    wut = p.dram("wut", [9, C, 2048], F, kind="ExternalInput")
    wdt = p.dram("wdt", [9, 2048, C], F, kind="ExternalInput")
    wrt = p.dram("wrt", [C, 8], F, kind="ExternalInput")
    rb = p.dram("rb", [8, 1], F, kind="ExternalInput")
    outT = p.dram("outT", [C, S, T], F, kind="ExternalOutput")
    xn0T = p.dram("xn0T", [C, S, T], F, kind="Internal")
    s0T = p.dram("s0T", [C, S, T], F, kind="Internal")
    g0T = p.dram("g0T", [C, S, T], F, kind="Internal")
    hA = p.dram("hA", [C, S, T], F, kind="Internal")
    x1T = p.dram("x1T", [C, S, T], F, kind="Internal")
    xn1T = p.dram("xn1T", [C, S, T], F, kind="Internal")
    s1aT = p.dram("s1aT", [C, S, T], F, kind="Internal")
    g1T = p.dram("g1T", [C, S, T], F, kind="Internal")
    o1i = p.dram("o1i", [C, S, T], F, kind="Internal")
    s1i = p.dram("s1i", [C, S, T], BF16, kind="Internal")
    cmi = p.dram("cmi", [C, S, T], F, kind="Internal")

    cst = emit_consts(p)
    with p.stage():
        emit_s0(p, cst, xT, xn0T)
    p.barrier()
    with p.stage():
        emit_s1(p, cst, xn0T, wk, wv, wr, wo, fwk, fwr, fwv, pvec, s0T, o1i, s1i, cmi)
    p.barrier()
    with p.stage():
        emit_s2(p, cst, xn0T, qw, qsc, qb, la, lb, lng, lnb, g0T, hA)
    p.barrier()
    with p.stage():
        emit_comb(p, cst, xT, s0T, g0T, x1T, xn1T)
    p.barrier()
    with p.stage():
        emit_s4(p, cst, xn1T, qkvw, outw, w1, w2, qkvb, outb, b1c, b2c, g1c, be1c, g2c, be2c, s1aT)
    p.barrier()
    with p.stage():
        emit_s5(p, cst, xn1T, wgt, wut, wdt, wrt, rb, g1T)
    p.barrier()
    with p.stage():
        emit_comb(p, cst, x1T, s1aT, g1T, outT, None)
    return p.finish()


def _host_inputs(z):
    m = {}
    for nm, key in [("wk", "L0_attn_Wk"), ("wv", "L0_attn_Wv"), ("wr", "L0_attn_Wr"),
                    ("wo", "L0_attn_Wo"), ("fwk", "L0_attn_fWk"), ("fwr", "L0_attn_fWr"),
                    ("fwv", "L0_attn_fWv"), ("qkvw", "L1_attn_qkv_w"), ("outw", "L1_attn_out_w"),
                    ("w1", "L1_attn_w1"), ("w2", "L1_attn_w2")]:
        m[nm] = np.ascontiguousarray(np.asarray(z[key]))
    m["pvec"] = np.ascontiguousarray(np.concatenate([_col(z[k]) for k in [
        "L0_attn_ln1_s", "L0_attn_ln1_b", "L0_attn_ln2_s", "L0_attn_ln2_b",
        "L0_attn_tmk", "L0_attn_tmv", "L0_attn_tmr", "L0_attn_f_tmk", "L0_attn_f_tmr",
        "L0_attn_w_decay", "L0_attn_u_first"]], axis=1))
    m["qw"] = np.ascontiguousarray(np.asarray(z["L0_ffn_qweights"]))
    m["qsc"] = np.ascontiguousarray(np.asarray(z["L0_ffn_scales"]).reshape(6, 8, 128, 64).transpose(2, 0, 1, 3))
    m["qb"] = np.ascontiguousarray(np.asarray(z["L0_ffn_biases"]).reshape(6, 8, 128).transpose(2, 0, 1))
    m["la"] = np.ascontiguousarray(np.asarray(z["L0_ffn_lora_a"]))
    m["lb"] = np.ascontiguousarray(np.asarray(z["L0_ffn_lora_b"]))
    m["lng"] = np.ascontiguousarray(np.asarray(z["L0_ffn_ln_gamma"]).reshape(3, 8, 128).transpose(2, 0, 1))
    m["lnb"] = np.ascontiguousarray(np.asarray(z["L0_ffn_ln_beta"]).reshape(3, 8, 128).transpose(2, 0, 1))
    m["qkvb"] = _col(z["L1_attn_qkv_b"], 24)
    m["outb"] = _col(z["L1_attn_out_b"])
    m["b1c"] = _col(z["L1_attn_b1"], 32)
    m["b2c"] = _col(z["L1_attn_b2"])
    m["g1c"] = _col(z["L1_attn_g1"])
    m["be1c"] = _col(z["L1_attn_be1"])
    m["g2c"] = _col(z["L1_attn_g2"])
    m["be2c"] = _col(z["L1_attn_be2"])
    wg = np.concatenate([np.asarray(z["L1_ffn_Wg_s"])[None], np.asarray(z["L1_ffn_Wg"])], 0)
    wu = np.concatenate([np.asarray(z["L1_ffn_Wu_s"])[None], np.asarray(z["L1_ffn_Wu"])], 0)
    wd = np.concatenate([np.asarray(z["L1_ffn_Wd_s"])[None], np.asarray(z["L1_ffn_Wd"])], 0)
    m["wgt"] = np.ascontiguousarray(wg.transpose(0, 2, 1))
    m["wut"] = np.ascontiguousarray(wu.transpose(0, 2, 1))
    m["wdt"] = np.ascontiguousarray(wd.transpose(0, 2, 1))
    m["wrt"] = np.ascontiguousarray(np.asarray(z["L1_ffn_Wr"]).T)
    m["rb"] = np.ascontiguousarray(np.asarray(z["L1_ffn_rb"]).reshape(8, 1))
    return m


def kernel(**inputs):
    x = np.asarray(inputs["L0_attn_x"])
    base = _host_inputs(inputs)
    nc = _prog("mega", build_mega)
    in_maps = [dict(base, xT=to_cm(x[c * S:(c + 1) * S])) for c in range(NCORE)]
    res = run_bass_kernel_spmd(nc, in_maps, core_ids=list(range(NCORE)))
    outs = [from_cm(np.asarray(rc["outT"])) for rc in res.results]
    return np.ascontiguousarray(np.concatenate(outs, axis=0))
